# version 16
# baseline (speedup 1.0000x reference)
import numpy as np
from contextlib import ExitStack

import concourse.bass as bass
import concourse.mybir as mybir
from concourse.bass_utils import run_bass_kernel_spmd

F32 = mybir.dt.float32
F32R = mybir.dt.float32r
BF16 = mybir.dt.bfloat16
AL = mybir.AluOpType
AF = mybir.ActivationFunctionType
AX = mybir.AxisListType

D = 1024
H = 4
DK = 256
CH = 128
NDC = D // 128


class P:
    ENGS = ("tensor", "vector", "scalar", "gpsimd", "sync")
    DSEMS = tuple(f"dma{i}" for i in range(6))
    QOF = {"sync": (0, 1, 2, 3), "gpsimd": (4, 5)}

    def __init__(self, nc, st):
        self.sem = {e: st.enter_context(nc.semaphore("sem_" + e))
                    for e in self.ENGS + self.DSEMS}
        self.cnt = {e: 0 for e in self.ENGS + self.DSEMS}
        self.waited = {}
        self.eng = {e: getattr(nc, e) for e in self.ENGS}
        self.fence = {d: 0 for d in self.DSEMS}
        self.selfw = {d: 0 for d in self.DSEMS}

    def inc(self, prod, instr, n=1):
        instr.then_inc(self.sem[prod], n)
        self.cnt[prod] += n
        return (prod, self.cnt[prod])

    def dma(self, q, out, in_, waits=(), slot=None):
        if slot is None:
            slot = 3 if q == "sync" else 4
        assert slot in self.QOF[q], (q, slot)
        d = f"dma{slot}"
        if self.fence[d] > self.selfw[d]:
            self.eng[q].wait_ge(self.sem[d], self.fence[d])
            self.selfw[d] = self.fence[d]
        for t in waits:
            if t is not None and t[0] == q:
                if self.waited.get((q, q), 0) < t[1]:
                    self.eng[q].wait_ge(self.sem[q], t[1])
                    self.waited[(q, q)] = t[1]
            else:
                self.wait(q, t)
        ins = self.eng[q].dma_start(out=out, in_=in_)
        ins.then_inc(self.sem[d], 16)
        self.cnt[d] += 16
        return (d, self.cnt[d])

    def tok(self, prod):
        return (prod, self.cnt[prod])

    def wait(self, cons, token):
        if token is None:
            return
        prod, val = token
        if val <= 0:
            return
        if prod.startswith("dma"):
            val = self.cnt[prod]
            self.fence[prod] = max(self.fence[prod], val)
            if (cons == "sync" and prod in ("dma0", "dma1", "dma2", "dma3")) or \
               (cons == "gpsimd" and prod in ("dma4", "dma5")):
                if val > self.selfw[prod]:
                    self.eng[cons].wait_ge(self.sem[prod], val)
                    self.selfw[prod] = max(self.selfw[prod], val)
                return
        elif prod == cons:
            return
        if self.waited.get((cons, prod), 0) >= val:
            return
        self.eng[cons].wait_ge(self.sem[prod], val)
        self.waited[(cons, prod)] = val

    def waits(self, cons, tokens):
        for t in tokens:
            self.wait(cons, t)

    def barrier(self):
        for q, slots in self.QOF.items():
            for s in slots:
                d = f"dma{s}"
                if self.cnt[d] > self.selfw[d]:
                    self.eng[q].wait_ge(self.sem[d], self.cnt[d])
                    self.selfw[d] = self.cnt[d]
                    self.fence[d] = self.cnt[d]
        toks = {}
        for e in ("tensor", "vector", "scalar", "gpsimd"):
            toks[e] = self.inc(e, self.eng[e].drain())
        t_sy = self.inc("sync", self.eng["sync"].nop())
        for c in self.ENGS:
            for e, t in toks.items():
                self.wait(c, t)
            if c != "sync":
                self.wait(c, t_sy)
        for d in self.DSEMS:
            for c in self.ENGS:
                if self.cnt[d] > 0:
                    self.wait(c, (d, self.cnt[d]))
        return toks["gpsimd"]


class Banks:

    def __init__(self, p, banks):
        self.p = p
        self.b = banks
        self.w = [None] * len(banks)
        self.r = [[] for _ in banks]

    def pre_write(self, i, extra=()):
        self.p.waits("tensor", tuple(self.r[i]) + tuple(extra))
        self.r[i] = []

    def set_write(self, i, tok):
        self.w[i] = tok

    def read(self, i, eng):
        self.p.wait(eng, self.w[i])

    def add_reader(self, i, tok):
        self.r[i].append(tok)


def build_program(n_seq, T, taps=None):
    taps = set(taps or ())
    nc = bass.Bass("TRN2", target_bir_lowering=False)
    NCH = T // CH

    def dram(n, s, k=None):
        return (nc.dram_tensor(n, s, F32, kind=k) if k else nc.dram_tensor(n, s, F32)).ap()

    X = dram("x", [n_seq, T, D], "ExternalInput")
    C_in = dram("c", [n_seq, D], "ExternalInput")
    WMOD = dram("wmod", [D, 6 * D], "ExternalInput")
    WQ = dram("wq", [D, D], "ExternalInput")
    WK = dram("wk", [D, D], "ExternalInput")
    WV = dram("wv", [D, D], "ExternalInput")
    WB = dram("wb", [D, H], "ExternalInput")
    WO = dram("wo", [D, D], "ExternalInput")
    GN = dram("g_norm", [DK], "ExternalInput")
    CW = [dram("cn_w1", [D, D], "ExternalInput"), dram("cn_w2", [D, D // 2], "ExternalInput"),
          dram("cn_w3", [D // 2, D // 4], "ExternalInput"), dram("cn_w4", [D // 4, 1], "ExternalInput")]
    CB = [dram("cn_b1", [D], "ExternalInput"), dram("cn_b2", [D // 2], "ExternalInput"),
          dram("cn_b3", [D // 4], "ExternalInput"), dram("cn_b4", [1], "ExternalInput")]
    GW = [dram("gc_w1", [D, D // 2], "ExternalInput"), dram("gc_w2", [D // 2, D // 4], "ExternalInput"),
          dram("gc_w3", [D // 4, 1], "ExternalInput")]
    GB = [dram("gc_b1", [D // 2], "ExternalInput"), dram("gc_b2", [D // 4], "ExternalInput"),
          dram("gc_b3", [1], "ExternalInput")]
    AR = dram("ar", [1], "ExternalInput")
    EYE = dram("eye", [128, 128], "ExternalInput")
    MASKU = dram("masku", [128, 128], "ExternalInput")
    MASKL = dram("maskl", [128, 128], "ExternalInput")
    MASKUI = dram("maskui", [128, 128], "ExternalInput")
    ONES1 = dram("ones1", [1, 128], "ExternalInput")
    SEL4 = dram("sel4", [4, 4 * 128], "ExternalInput")

    OUT = dram("out", [n_seq, T, D], "ExternalOutput")

    H1S = dram("h1_scr", [n_seq, NCH, D, CH])
    QS = dram("q_scr", [n_seq, T, D])
    KS = dram("k_scr", [n_seq, T, D])
    VS = dram("v_scr", [n_seq, T, D])
    BSc = dram("b_scr", [n_seq, T, H])
    X1S = dram("x1_scr", [n_seq, T, D])
    H2S = dram("h2_scr", [n_seq, NCH, D, CH])

    tap_handles = {}

    with ExitStack() as st:
        def sb(name, shape, dt=F32, stx=None):
            return (stx or st).enter_context(nc.sbuf_tensor("sb_" + name, shape, dt))

        def psum(name, shape, stx=None):
            return (stx or st).enter_context(nc.psum_tensor("ps_" + name, shape, F32))

        p = P(nc, st)
        TE, VE, SE, GE, SY = "tensor", "vector", "scalar", "gpsimd", "sync"

        def tap(name, ap, deps):
            if name in taps:
                t = nc.dram_tensor("dbg_" + name, list(ap.shape), F32,
                                   kind="ExternalOutput").ap()
                tap_handles[name] = t
                tk = p.dma(GE, t[...], ap, waits=tuple(deps))
                for e in P.ENGS:
                    if e != "gpsimd":
                        p.wait(e, tk)
                p.wait("gpsimd", tk)

        eye = sb("eye", [128, 128])
        masku = sb("masku", [128, 128])
        maskl = sb("maskl", [128, 128])
        maskui = sb("maskui", [128, 128])
        ones1 = sb("ones1", [1, 128])
        sel4 = sb("sel4", [4, 4 * 128])
        gn_bc = sb("gn_bc", [128, DK])
        modv = sb("modv", [128, 48 * n_seq])
        ca1 = sb("ca1", [128, NDC * n_seq])
        cb1 = sb("cb1", [128, NDC * n_seq])
        stats = sb("stats", [128, 24])
        ar_t = sb("ar_t", [1, 1])
        gfk = sb("gfk", [1, 2 * n_seq])
        junkD = sb("junkD", [128, D])
        silc = junkD[:, 32:32 + NDC * n_seq]
        bcg = sb("bcg", [128, 8])
        bcrow = sb("bcrow", [4, 2 * 128])
        gnrow = junkD[:1, 512:512 + DK]

        for apd, src in ((eye, EYE), (masku, MASKU), (maskl, MASKL), (maskui, MASKUI),
                         (ones1, ONES1), (sel4, SEL4)):
            p.dma(SY, apd[:], src[:, :])
        t_ar = p.dma(SY, ar_t[:], AR[:])
        t_gnr = p.dma(SY, gnrow, GN[:])
        t_consts = p.tok(SY)

        def rsqrt_newton(vcol, rcol, t0, t1):
            p.wait(SE, p.tok(VE))
            p.eng[SE].drain()
            i = p.eng[SE].activation(t0, vcol, AF.Ln)
            tr = p.inc(SE, i)
            p.eng[SE].drain()
            i = p.eng[SE].activation(rcol, t0, AF.Exp, scale=-0.5)
            ts_ = p.inc(SE, i)
            p.wait(VE, ts_)
            p.eng[VE].drain()
            p.eng[VE].tensor_tensor(t1, rcol, rcol, AL.mult)
            p.eng[VE].drain()
            p.eng[VE].tensor_tensor(t1, t1, vcol, AL.mult)
            p.eng[VE].drain()
            p.eng[VE].tensor_scalar(t1, t1, -0.5, 1.5, AL.mult, AL.add)
            p.eng[VE].drain()
            i = p.eng[VE].tensor_tensor(rcol, rcol, t1, AL.mult)
            return p.inc(VE, i)

        def ln_stats(xsl, deps):
            p.waits(SE, deps)
            p.eng[SE].drain()
            i = p.eng[SE].activation(junkD[:], xsl, AF.Square, accum_out=stats[:, 1:2])
            t_sq = p.inc(SE, i)
            p.waits(VE, deps)
            p.eng[VE].drain()
            i = p.eng[VE].tensor_reduce(stats[:, 0:1], xsl, AX.X, AL.add)
            p.inc(VE, i)
            p.wait(VE, t_sq)
            p.eng[VE].drain()
            p.eng[VE].tensor_scalar(stats[:, 2:3], stats[:, 0:1], -1.0 / D, None, AL.mult)
            p.eng[VE].drain()
            p.eng[VE].tensor_tensor(stats[:, 3:4], stats[:, 2:3], stats[:, 2:3], AL.mult)
            p.eng[VE].drain()
            p.eng[VE].tensor_scalar(stats[:, 4:5], stats[:, 1:2], 1.0 / D, None, AL.mult)
            p.eng[VE].drain()
            p.eng[VE].tensor_tensor(stats[:, 4:5], stats[:, 4:5], stats[:, 3:4], AL.subtract)
            p.eng[VE].drain()
            i = p.eng[VE].tensor_scalar(stats[:, 4:5], stats[:, 4:5], 1e-5, None, AL.add)
            p.inc(VE, i)
            return rsqrt_newton(stats[:, 4:5], stats[:, 7:8], stats[:, 5:6], stats[:, 6:7])

        def modcol(vidx, dc, b):
            c0 = (vidx * 8 + dc) * n_seq + b
            return modv[:, c0:c0 + 1]

        def split_pair(full_ap, hi_ap, lo_ap, deps):
            p.waits(SE, deps)
            p.eng[SE].drain()
            i = p.eng[SE].copy(hi_ap, full_ap)
            th = p.inc(SE, i)
            p.waits(GE, tuple(deps) + (th,))
            p.eng[GE].drain()
            i = p.eng[GE].tensor_tensor(lo_ap, full_ap, hi_ap.bitcast(F32), AL.subtract)
            return th, p.inc(GE, i)

        with ExitStack() as st0:
            mw = sb("mw", [128, 2 * D], stx=st0)
            pm = psum("pm", [128, 512], stx=st0)
            t_c = []
            for b in range(n_seq):
                for dc in range(NDC):
                    cix = dc * n_seq + b
                    t_c.append(p.dma(SY, silc[:, cix:cix + 1], C_in[b, dc * 128:(dc + 1) * 128]))
            p.waits(SE, tuple(t_c))
            p.eng[SE].drain()
            i = p.eng[SE].activation(silc[:], silc[:], AF.Silu)
            t_silc = p.inc(SE, i)

            t_buf = [None, None]
            t_evac = {}
            for mc in range(48):
                j = mc % 2
                ld = []
                for dc in range(NDC):
                    ld.append(p.dma(SY, mw[:, j * D + dc * 128: j * D + (dc + 1) * 128],
                                    WMOD[dc * 128:(dc + 1) * 128, mc * 128:(mc + 1) * 128],
                                    waits=(t_buf[j],) if t_buf[j] else ()))
                p.waits(TE, tuple(ld) + (t_silc,))
                slot = mc % 4
                if mc >= 4:
                    p.wait(TE, t_evac[mc - 4])
                psl = pm[:, slot * n_seq: slot * n_seq + n_seq]
                for dc in range(NDC):
                    ins = nc.tensor.matmul(psl, mw[:, j * D + dc * 128: j * D + (dc + 1) * 128],
                                           silc[:, dc * n_seq:(dc + 1) * n_seq],
                                           start=(dc == 0), stop=(dc == NDC - 1))
                t_buf[j] = p.inc(TE, ins)
                p.wait(VE, t_buf[j])
                p.eng[VE].drain()
                i = p.eng[VE].tensor_copy(modv[:, mc * n_seq:(mc + 1) * n_seq], psl)
                t_evac[mc] = p.inc(VE, i)
            p.barrier()

            def gate_fix(vidx):
                sl = slice(vidx * 8 * n_seq, (vidx + 1) * 8 * n_seq)
                g = modv[:, sl]
                sgn = junkD[:, 0:8 * n_seq]
                p.eng[VE].drain()
                p.eng[VE].tensor_scalar(sgn, g, 0.0, None, AL.is_ge)
                p.eng[VE].drain()
                p.eng[VE].tensor_scalar(sgn, sgn, 2.0, -1.0, AL.mult, AL.add)
                p.eng[VE].drain()
                p.eng[VE].tensor_tensor(g, g, sgn, AL.mult)
                p.eng[VE].drain()
                p.eng[VE].tensor_scalar(g, g, 0.25, None, AL.max)
                p.eng[VE].drain()
                i = p.eng[VE].tensor_tensor(g, g, sgn, AL.mult)
                return p.inc(VE, i)
            gate_fix(2)
            p.eng[VE].drain()
            gate_fix(5)
            p.eng[VE].drain()
            p.eng[VE].tensor_scalar(ca1[:], modv[:, 8 * n_seq:16 * n_seq], 1.0, None, AL.add)
            p.eng[VE].drain()
            i = p.eng[VE].tensor_scalar(cb1[:], modv[:, 32 * n_seq:40 * n_seq], 1.0, None, AL.add)
            t_mod = p.inc(VE, i)

            p.waits(TE, (t_gnr, t_consts, t_mod))
            ins = nc.tensor.matmul(pm[:, 0:DK], ones1[:], gnrow, start=True, stop=True)
            t_g = p.inc(TE, ins)
            p.wait(SE, t_g)
            p.eng[SE].drain()
            i = p.eng[SE].copy(gn_bc[:], pm[:, 0:DK])
            p.inc(SE, i)
            p.barrier()
        tap("modv", modv[:], [p.tok(VE)])
        tap("ca1", ca1[:], [p.tok(VE)])

        def build_gate_bc(vidx, dst, b, psbank, bk, bki):
            tt = []
            for dc in range(NDC):
                p.eng[VE].drain()
                i = p.eng[VE].tensor_copy(bcg[:, dc:dc + 1], modcol(vidx, dc, b))
                tt.append(p.inc(VE, i))
            t_row = None
            for g in range(2):
                bk.pre_write(bki, tuple(tt) + ((t_row,) if t_row else ()))
                ins = nc.tensor.transpose(psbank[:4, 0:128], bcg[:, g * 4:(g + 1) * 4],
                                          eye[:])
                t_tr = p.inc(TE, ins)
                bk.set_write(bki, t_tr)
                bk.read(bki, SE)
                p.eng[SE].drain()
                i = p.eng[SE].copy(bcrow[:, g * 128:(g + 1) * 128], psbank[:4, 0:128])
                t_row = p.inc(SE, i)
                bk.add_reader(bki, t_row)
            for dc in range(NDC):
                g, r = dc // 4, dc % 4
                bk.pre_write(bki, (t_row,))
                ins = nc.tensor.matmul(psbank[:, 128:256], sel4[:, r * 128:(r + 1) * 128],
                                       bcrow[:, g * 128:(g + 1) * 128],
                                       start=True, stop=True)
                t_b = p.inc(TE, ins)
                bk.set_write(bki, t_b)
                bk.read(bki, SE)
                p.eng[SE].drain()
                i = p.eng[SE].copy(dst[:, dc * 128:(dc + 1) * 128], psbank[:, 128:256])
                bk.add_reader(bki, p.inc(SE, i))
            return p.tok(SE)

        def pa1_qk():
            with ExitStack() as s1:
                wt = {}
                for nm in ("q", "k"):
                    wt[nm] = (sb("whi_" + nm, [128, NDC * D], F32R, stx=s1),
                              sb("whib_" + nm, [128, NDC * D], BF16, stx=s1),
                              sb("wlob_" + nm, [128, NDC * D], BF16, stx=s1))
                xt = sb("xt_qk", [128, 2 * D], stx=s1)
                h1f = sb("h1f_qk", [128, 2 * D], stx=s1)
                h1hi = sb("h1hi_qk", [128, 2 * D], F32R, stx=s1)
                h1hib = sb("h1hib_qk", [128, 2 * D], BF16, stx=s1)
                h1lob = sb("h1lob_qk", [128, 2 * D], BF16, stx=s1)
                o_t = sb("ot_qk", [128, 2 * D], stx=s1)
                wst = junkD[:]
                bank_ps = [psum(f"bkqk{i}", [128, 512], stx=s1) for i in range(4)]
                bk = Banks(p, bank_ps)

                t_wsplit = {"q": [], "k": []}
                prev = None
                for nm, W in (("q", WQ), ("k", WK)):
                    w_hi, w_hib, w_lob = wt[nm]
                    for dc in range(NDC):
                        dsl = slice(dc * D, (dc + 1) * D)
                        tl = p.dma(SY, wst, W[dc * 128:(dc + 1) * 128, :],
                                   waits=(prev,) if prev else ())
                        p.wait(SE, tl)
                        p.eng[SE].drain()
                        i = p.eng[SE].copy(w_hi[:, dsl], wst)
                        th = p.inc(SE, i)
                        p.waits(GE, (tl, th))
                        p.eng[GE].drain()
                        i = p.eng[GE].tensor_tensor(w_lob[:, dsl], wst,
                                                    w_hi[:, dsl].bitcast(F32), AL.subtract)
                        tlo = p.inc(GE, i)
                        p.wait(VE, th)
                        p.eng[VE].drain()
                        i = p.eng[VE].tensor_copy(w_hib[:, dsl], w_hi[:, dsl].bitcast(F32))
                        thb = p.inc(VE, i)
                        prev = tlo
                        t_wsplit[nm].append((tlo, thb))
                p.barrier()

                cons = [(), ()]
                l2prev = None
                for b in range(n_seq):
                    for ci in range(NCH):
                        t0 = ci * CH
                        jx = ci % 2
                        xsl = xt[:, jx * D:(jx + 1) * D]
                        hsl_f = h1f[:, jx * D:(jx + 1) * D]
                        hsl_hi = h1hi[:, jx * D:(jx + 1) * D]
                        hsl_hib = h1hib[:, jx * D:(jx + 1) * D]
                        hsl_lob = h1lob[:, jx * D:(jx + 1) * D]
                        osl = o_t[:, jx * D:(jx + 1) * D]
                        gate = cons[jx]
                        tx = p.dma(SY, xsl, X[b, t0:t0 + CH, :], waits=gate, slot=jx)
                        p.wait(SE, p.tok(VE))
                        ln_stats(xsl, (tx,))
                        p.eng[VE].drain()
                        i = p.eng[VE].tensor_scalar(xsl, xsl, stats[:, 2:3],
                                                    stats[:, 7:8], AL.add, AL.mult)
                        t_xn = p.inc(VE, i)
                        t_h1c = []
                        for dc in range(NDC):
                            bki = dc % 2
                            off = ((dc // 2) % 2) * 128
                            bk.pre_write(bki, (t_xn,))
                            ins = nc.tensor.transpose(
                                bank_ps[bki][:, off:off + 128],
                                xsl[:, dc * 128:(dc + 1) * 128], eye[:])
                            bk.set_write(bki, p.inc(TE, ins))
                            bk.read(bki, VE)
                            p.waits(VE, gate)
                            p.eng[VE].drain()
                            i = p.eng[VE].tensor_scalar(
                                hsl_f[:, dc * 128:(dc + 1) * 128],
                                bank_ps[bki][:, off:off + 128],
                                ca1[:, dc * n_seq + b: dc * n_seq + b + 1],
                                modcol(0, dc, b), AL.mult, AL.add)
                            tt = p.inc(VE, i)
                            bk.add_reader(bki, tt)
                            t_h1c.append(tt)
                        for dc in range(NDC):
                            p.dma(GE, H1S[b, ci, dc * 128:(dc + 1) * 128, :],
                                  hsl_f[:, dc * 128:(dc + 1) * 128],
                                  waits=(t_h1c[dc],), slot=4 + jx)
                        deps_h = tuple(t_h1c) + gate
                        p.waits(SE, deps_h)
                        p.eng[SE].drain()
                        i = p.eng[SE].copy(hsl_hi, hsl_f)
                        th = p.inc(SE, i)
                        p.waits(GE, deps_h + (th,))
                        p.eng[GE].drain()
                        i = p.eng[GE].tensor_tensor(hsl_lob, hsl_f,
                                                    hsl_hi.bitcast(F32), AL.subtract)
                        tlo = p.inc(GE, i)
                        p.wait(VE, th)
                        p.eng[VE].drain()
                        i = p.eng[VE].tensor_copy(hsl_hib, hsl_hi.bitcast(F32))
                        thb = p.inc(VE, i)
                        t_hs = (th, tlo, thb)

                        t_st = None
                        for nm, bko, DST in (("q", 2, QS), ("k", 0, KS)):
                            w_hi, w_hib, w_lob = wt[nm]
                            for half in range(2):
                                bki = bko + half
                                bk.pre_write(bki, t_hs)
                                for dc in range(NDC):
                                    hslc = slice(jx * D + dc * 128, jx * D + (dc + 1) * 128)
                                    wsl = slice(dc * D + half * 512, dc * D + (half + 1) * 512)
                                    p.waits(TE, t_wsplit[nm][dc])
                                    nc.tensor.matmul(bank_ps[bki][:], h1hi[:, hslc],
                                                     w_hi[:, wsl], start=(dc == 0), stop=False)
                                    nc.tensor.matmul(bank_ps[bki][:], h1hib[:, hslc],
                                                     w_lob[:, wsl], start=False, stop=False)
                                    ins = nc.tensor.matmul(bank_ps[bki][:], h1lob[:, hslc],
                                                           w_hib[:, wsl],
                                                           start=False, stop=(dc == NDC - 1))
                                t_mm = p.inc(TE, ins)
                                bk.set_write(bki, t_mm)
                            sq_toks = []
                            for half in range(2):
                                bki = bko + half
                                for hs_ in range(2):
                                    hh = half * 2 + hs_
                                    psl = bank_ps[bki][:, hs_ * 256:(hs_ + 1) * 256]
                                    bk.read(bki, SE)
                                    p.wait(SE, l2prev)
                                    p.eng[SE].drain()
                                    i = p.eng[SE].activation(junkD[:, 0:256], psl, AF.Square,
                                                             accum_out=stats[:, 8 + hh:9 + hh])
                                    t_ss = p.inc(SE, i)
                                    bk.add_reader(bki, t_ss)
                                    sq_toks.append(t_ss)
                            p.waits(VE, tuple(sq_toks))
                            rsqrt_newton(stats[:, 8:12], stats[:, 12:16],
                                         stats[:, 16:20], stats[:, 20:24])
                            for half in range(2):
                                bki = bko + half
                                for hs_ in range(2):
                                    hh = half * 2 + hs_
                                    psl = bank_ps[bki][:, hs_ * 256:(hs_ + 1) * 256]
                                    bk.read(bki, VE)
                                    p.waits(VE, gate + ((t_st,) if t_st else ()))
                                    p.eng[VE].drain()
                                    i = p.eng[VE].tensor_scalar(
                                        osl[:, hh * 256:(hh + 1) * 256], psl,
                                        stats[:, 12 + hh:13 + hh], None, AL.mult)
                                    l2prev = p.inc(VE, i)
                                    bk.add_reader(bki, l2prev)
                            t_st = p.dma(GE, DST[b, t0:t0 + CH, :], osl, waits=(l2prev,),
                                         slot=4 + jx)
                        cons[jx] = (p.tok(TE), p.tok(VE), p.tok(SE), t_st)
                        if b == 0 and ci == 0:
                            tap("h1_c0", hsl_f, [p.tok(VE), p.tok(GE), p.tok(SE)])
                p.barrier()

        def pa1_v():
            with ExitStack() as s1:
                wv_t = sb("wvt", [128, NDC * D], stx=s1)
                wb_t = sb("wbt", [128, NDC * H], stx=s1)
                beta_t = sb("betat", [128, 2 * H], stx=s1)
                h1f = sb("h1f_v", [128, 2 * D], stx=s1)
                o_t = sb("ot_v", [128, 2 * D], stx=s1)
                bank_ps = [psum(f"bkv{i}", [128, 512], stx=s1) for i in range(4)]
                bk = Banks(p, bank_ps)
                t_wv = [p.dma(SY, wv_t[:, dc * D:(dc + 1) * D],
                              WV[dc * 128:(dc + 1) * 128, :]) for dc in range(NDC)]
                t_wb = [p.dma(SY, wb_t[:, dc * H:(dc + 1) * H],
                              WB[dc * 128:(dc + 1) * 128, :]) for dc in range(NDC)]
                p.barrier()
                cons = [(), ()]
                for b in range(n_seq):
                    for ci in range(NCH):
                        t0 = ci * CH
                        jx = ci % 2
                        hsl_f = h1f[:, jx * D:(jx + 1) * D]
                        osl = o_t[:, jx * D:(jx + 1) * D]
                        gate = cons[jx]
                        t_h1c = []
                        for dc in range(NDC):
                            t_h1c.append(p.dma(
                                SY, hsl_f[:, dc * 128:(dc + 1) * 128],
                                H1S[b, ci, dc * 128:(dc + 1) * 128, :],
                                waits=gate if dc == 0 else (), slot=jx))
                        for half in range(2):
                            bki = 2 + half
                            bk.pre_write(bki, tuple(t_h1c))
                            for dc in range(NDC):
                                p.wait(TE, t_wv[dc])
                                ins = nc.tensor.matmul(
                                    bank_ps[bki][:], hsl_f[:, dc * 128:(dc + 1) * 128],
                                    wv_t[:, dc * D + half * 512: dc * D + (half + 1) * 512],
                                    start=(dc == 0), stop=(dc == NDC - 1))
                            t_vm = p.inc(TE, ins)
                            bk.set_write(bki, t_vm)
                            bk.read(bki, SE)
                            p.waits(SE, gate)
                            p.eng[SE].drain()
                            i = p.eng[SE].copy(osl[:, half * 512:(half + 1) * 512],
                                               bank_ps[bki][:])
                            bk.add_reader(bki, p.inc(SE, i))
                        bk.pre_write(0, tuple(t_h1c))
                        for dc in range(NDC):
                            p.wait(TE, t_wb[dc])
                            ins = nc.tensor.matmul(
                                bank_ps[0][:, 0:H], hsl_f[:, dc * 128:(dc + 1) * 128],
                                wb_t[:, dc * H:(dc + 1) * H],
                                start=(dc == 0), stop=(dc == NDC - 1))
                        t_bm = p.inc(TE, ins)
                        bk.set_write(0, t_bm)
                        bk.read(0, SE)
                        p.waits(SE, gate)
                        p.eng[SE].drain()
                        i = p.eng[SE].activation(beta_t[:, jx * H:(jx + 1) * H],
                                                 bank_ps[0][:, 0:H], AF.Sigmoid)
                        t_bb = p.inc(SE, i)
                        bk.add_reader(0, t_bb)
                        p.dma(GE, VS[b, t0:t0 + CH, :], osl, waits=(p.tok(SE),),
                              slot=4 + jx)
                        t_st = p.dma(GE, BSc[b, t0:t0 + CH, :],
                                     beta_t[:, jx * H:(jx + 1) * H],
                                     waits=(t_bb,), slot=4 + jx)
                        cons[jx] = (p.tok(TE), p.tok(SE), t_st)
                p.barrier()

        pa1_qk()
        pa1_v()

        with ExitStack() as s2:
            wst = sb("wst_o", [128, D], stx=s2)
            s_state = sb("s_state", [128, H * 2 * DK], stx=s2)
            ga_bc = sb("ga_bc", [128, D], stx=s2)
            wo_hi = sb("wo_hi", [128, NDC * D], F32R, stx=s2)
            wo_hib = sb("wo_hib", [128, NDC * D], BF16, stx=s2)
            wo_lob = sb("wo_lob", [128, NDC * D], BF16, stx=s2)
            qt = sb("qt", [128, 2 * D], stx=s2)
            kt = sb("kt", [128, 2 * D], stx=s2)
            vt = sb("vt", [128, 2 * D], stx=s2)
            xt2 = sb("xt2", [128, 2 * D], stx=s2)
            beta_t = sb("beta_t2", [128, 2 * H], stx=s2)
            negb = sb("negb", [128, 2 * H], stx=s2)
            qf = sb("qf", [128, 2 * D], stx=s2)
            kf = sb("kf", [128, 2 * D], stx=s2)
            brow = sb("brow", [4, 2 * 128], stx=s2)
            BPOW = ["c1b", "c1tb", "cA", "cAt", "cB", "cBt", "c64t"]
            cb = {n: sb("pw_" + n, [128, 4 * 128], BF16, stx=s2) for n in BPOW}
            pbA = sb("pw_pbA", [128, 4 * 128], BF16, stx=s2)
            pbB = sb("pw_pbB", [128, 4 * 128], BF16, stx=s2)
            c1f = sb("c1f", [128, 4 * 128], stx=s2)
            c1tf = sb("c1tf", [128, 4 * 128], stx=s2)
            m_t = sb("m_t", [128, 4 * 128], stx=s2)
            pfA = sb("pfA", [128, 4 * 128], stx=s2)
            pfB = sb("pfB", [128, 4 * 128], stx=s2)
            ptfA = sb("ptfA", [128, 4 * 128], stx=s2)
            ptfB = sb("ptfB", [128, 4 * 128], stx=s2)
            w_t = sb("w_t", [128, 4 * 128], stx=s2)
            gm_t = sb("gm_t", [128, 2 * 4 * 128], stx=s2)
            pf_fin = sb("pf_fin", [128, 2 * 4 * 128], stx=s2)
            bmu = sb("bmu", [128, 4 * 128], stx=s2)
            r_t = sb("r_t", [128, 4 * 256], stx=s2)
            u_t = sb("u_t", [128, 4 * 256], stx=s2)
            eyeb = sb("eyeb", [128, 128], BF16, stx=s2)
            eye2 = sb("eye2", [128, 128], stx=s2)
            on_t = sb("on_t", [128, 4 * 256], stx=s2)
            onf_hi = sb("onf_hi", [128, D], F32R, stx=s2)
            onf_hib = sb("onf_hib", [128, D], BF16, stx=s2)
            onf_lob = sb("onf_lob", [128, D], BF16, stx=s2)
            bank_ps = [psum(f"d{i}", [128, 512], stx=s2) for i in range(8)]
            bk = Banks(p, bank_ps)

            t_wsplit = []
            prev = None
            for dc in range(NDC):
                dsl = slice(dc * D, (dc + 1) * D)
                tl = p.dma(SY, wst[:], WO[dc * 128:(dc + 1) * 128, :],
                           waits=(prev,) if prev else ())
                p.wait(SE, tl)
                p.eng[SE].drain()
                i = p.eng[SE].copy(wo_hi[:, dsl], wst[:])
                th = p.inc(SE, i)
                p.waits(GE, (tl, th))
                p.eng[GE].drain()
                i = p.eng[GE].tensor_tensor(wo_lob[:, dsl], wst[:],
                                            wo_hi[:, dsl].bitcast(F32), AL.subtract)
                tlo = p.inc(GE, i)
                p.wait(VE, th)
                p.eng[VE].drain()
                i = p.eng[VE].tensor_copy(wo_hib[:, dsl], wo_hi[:, dsl].bitcast(F32))
                thb = p.inc(VE, i)
                prev = tlo
                t_wsplit.append((tlo, thb))
            p.waits(GE, (t_consts,))
            p.eng[GE].drain()
            p.eng[GE].tensor_copy(eyeb[:], eye[:])
            p.eng[GE].drain()
            i = p.eng[GE].tensor_scalar(eye2[:], eye[:], 2.0, None, AL.mult)
            p.inc(GE, i)
            p.barrier()

            cons = [(), ()]
            rms_prev = None
            pend = {}

            BNK = [0, 1, 6, 7]
            A = lambda h: 2 + h
            s1 = lambda h, a, bnd: bank_ps[A(h)][:, a:bnd]
            sB = lambda h, a, bnd: bank_ps[BNK[h]][:, a:bnd]
            h128 = lambda t, h: t[:, h * 128:(h + 1) * 128]
            h256 = lambda t, h: t[:, h * 256:(h + 1) * 256]
            s0_ = lambda h: s_state[:, h * 512:(h + 1) * 512]

            def chain_part(b, ci):
                t0 = ci * CH
                jx = ci % 2
                oD = jx * D
                gate = cons[jx]
                tq = p.dma(SY, qt[:, oD:oD + D], QS[b, t0:t0 + CH, :], waits=gate, slot=jx)
                tk = p.dma(SY, kt[:, oD:oD + D], KS[b, t0:t0 + CH, :], slot=jx)
                tv = p.dma(SY, vt[:, oD:oD + D], VS[b, t0:t0 + CH, :], slot=jx)
                tb = p.dma(SY, beta_t[:, jx * H:(jx + 1) * H], BSc[b, t0:t0 + CH, :], slot=jx)
                tx = p.dma(SY, xt2[:, oD:oD + D], X[b, t0:t0 + CH, :], slot=jx)
                betac = beta_t[:, jx * H:(jx + 1) * H]
                negc = negb[:, jx * H:(jx + 1) * H]
                browc = brow[:, jx * 128:(jx + 1) * 128]
                gmc = gm_t[:, jx * 512:(jx + 1) * 512]
                pfc = pf_fin[:, jx * 512:(jx + 1) * 512]
                kfh = lambda h, half: kf[:, oD + h * 256 + half * 128:
                                         oD + h * 256 + (half + 1) * 128]
                qfh = lambda h, half: qf[:, oD + h * 256 + half * 128:
                                         oD + h * 256 + (half + 1) * 128]
                yield
                for srct, dst, tok, base in ((kt, kf, tk, 0), (qt, qf, tq, 256)):
                    for dc in range(NDC):
                        h, half = dc // 2, dc % 2
                        bki = BNK[h]
                        off = base + half * 128
                        bk.pre_write(bki, (tok,))
                        ins = nc.tensor.transpose(
                            bank_ps[bki][:, off:off + 128],
                            srct[:, oD + dc * 128:oD + (dc + 1) * 128], eye[:])
                        bk.set_write(bki, p.inc(TE, ins))
                        bk.read(bki, SE)
                        p.eng[SE].drain()
                        i = p.eng[SE].copy(dst[:, oD + dc * 128:oD + (dc + 1) * 128],
                                           bank_ps[bki][:, off:off + 128])
                        bk.add_reader(bki, p.inc(SE, i))
                    yield
                bk.pre_write(0, (tb,))
                ins = nc.tensor.transpose(bank_ps[0][:4, 0:128], betac, eye[:])
                bk.set_write(0, p.inc(TE, ins))
                bk.read(0, SE)
                p.eng[SE].drain()
                i = p.eng[SE].copy(browc, bank_ps[0][:4, 0:128])
                t_brow = p.inc(SE, i)
                bk.add_reader(0, t_brow)
                t_tr = p.tok(SE)
                p.waits(VE, (tb, tv))
                p.eng[VE].drain()
                i = p.eng[VE].tensor_scalar(negc, betac, -1.0, None, AL.mult)
                t_nb = p.inc(VE, i)
                for h in range(H):
                    p.eng[VE].drain()
                    i = p.eng[VE].tensor_scalar(vt[:, oD + h * 256:oD + (h + 1) * 256],
                                                vt[:, oD + h * 256:oD + (h + 1) * 256],
                                                betac[:, h:h + 1], None, AL.mult)
                t_vb = p.inc(VE, i)
                yield
                for h in range(H):
                    bk.pre_write(BNK[h], (t_brow,))
                    ins = nc.tensor.matmul(sB(h, 0, 128), sel4[:, h * 128:(h + 1) * 128],
                                           browc, start=True, stop=True)
                    bk.set_write(BNK[h], p.inc(TE, ins))
                for h in range(H):
                    bk.read(BNK[h], VE)
                    p.eng[VE].drain()
                    i = p.eng[VE].scalar_tensor_tensor(h128(bmu, h), sB(h, 0, 128), -1.0,
                                                       masku[:], AL.mult, AL.mult)
                    bk.add_reader(BNK[h], p.inc(VE, i))
                yield
                for h in range(H):
                    bk.pre_write(BNK[h], (t_tr,))
                    nc.tensor.matmul(sB(h, 128, 256), kfh(h, 0), kfh(h, 0),
                                     start=True, stop=False)
                    ins = nc.tensor.matmul(sB(h, 128, 256), kfh(h, 1), kfh(h, 1),
                                           start=False, stop=True)
                    bk.set_write(BNK[h], p.inc(TE, ins))
                t_c1 = {}
                t_c1t = {}
                for h in range(H):
                    bk.read(BNK[h], VE)
                    p.eng[VE].drain()
                    i = p.eng[VE].tensor_tensor(h128(c1f, h), sB(h, 128, 256),
                                                h128(bmu, h), AL.mult)
                    t_c1[h] = p.inc(VE, i)
                    bk.add_reader(BNK[h], t_c1[h])
                    p.wait(VE, t_nb)
                    p.eng[VE].drain()
                    i = p.eng[VE].scalar_tensor_tensor(h128(c1tf, h), sB(h, 128, 256),
                                                       negc[:, h:h + 1], maskl[:],
                                                       AL.mult, AL.mult)
                    t_c1t[h] = p.inc(VE, i)
                    bk.add_reader(BNK[h], t_c1t[h])
                yield
                tok_bf = {}
                t_m = {}
                for h in range(H):
                    p.waits(GE, (t_c1[h], t_c1t[h]))
                    p.eng[GE].drain()
                    i = p.eng[GE].tensor_copy(h128(cb["c1b"], h), h128(c1f, h))
                    tok_bf[("c1b", h)] = p.inc(GE, i)
                    p.eng[GE].drain()
                    i = p.eng[GE].tensor_copy(h128(cb["c1tb"], h), h128(c1tf, h))
                    tok_bf[("c1tb", h)] = p.inc(GE, i)
                    p.eng[GE].drain()
                    i = p.eng[GE].tensor_tensor(h128(m_t, h), eye[:], h128(c1tf, h),
                                                AL.subtract)
                    t_m[h] = p.inc(GE, i)
                t_gm = {}
                for h in range(H):
                    bk.pre_write(BNK[h], ())
                    nc.tensor.matmul(sB(h, 256, 384), kfh(h, 0), qfh(h, 0),
                                     start=True, stop=False)
                    ins = nc.tensor.matmul(sB(h, 256, 384), kfh(h, 1), qfh(h, 1),
                                           start=False, stop=True)
                    bk.set_write(BNK[h], p.inc(TE, ins))
                for h in range(H):
                    bk.read(BNK[h], VE)
                    p.eng[VE].drain()
                    i = p.eng[VE].tensor_tensor(h128(gmc, h), sB(h, 256, 384), maskui[:],
                                                AL.mult)
                    t_gm[h] = p.inc(VE, i)
                    bk.add_reader(BNK[h], t_gm[h])
                t_pb = {}
                for h in range(H):
                    p.waits(GE, (tok_bf[("c1b", h)], t_consts))
                    p.eng[GE].drain()
                    i = p.eng[GE].tensor_tensor(h128(pbA, h), h128(cb["c1b"], h), eyeb[:],
                                                AL.add)
                    t_pb[h] = p.inc(GE, i)
                yield
                curb, nxtb = pbA, pbB
                cur_pair, nxt_pair = ("c1b", "c1tb"), ("cA", "cAt")
                for lvl in range(5):
                    pl, pt = cur_pair
                    nl, nt = nxt_pair
                    for h in range(H):
                        bk.pre_write(BNK[h], (tok_bf[(pl, h)], tok_bf[(pt, h)]))
                        ins = nc.tensor.matmul(sB(h, 384, 512), h128(cb[pt], h),
                                               h128(cb[pl], h), start=True, stop=True)
                        bk.set_write(BNK[h], p.inc(TE, ins))
                    for h in range(H):
                        eng = SE if h % 2 == 0 else VE
                        bk.read(BNK[h], eng)
                        p.eng[eng].drain()
                        if eng == SE:
                            i = p.eng[eng].copy(h128(cb[nl], h), sB(h, 384, 512))
                        else:
                            i = p.eng[eng].tensor_copy(h128(cb[nl], h), sB(h, 384, 512))
                        tok_bf[(nl, h)] = p.inc(eng, i)
                        bk.add_reader(BNK[h], tok_bf[(nl, h)])
                    yield
                    for h in range(H):
                        bk.pre_write(BNK[h], (tok_bf[(pl, h)], tok_bf[(pt, h)]))
                        ins = nc.tensor.matmul(sB(h, 384, 512), h128(cb[pl], h),
                                               h128(cb[pt], h), start=True, stop=True)
                        bk.set_write(BNK[h], p.inc(TE, ins))
                    for h in range(H):
                        eng = SE if h % 2 == 1 else VE
                        bk.read(BNK[h], eng)
                        p.eng[eng].drain()
                        if eng == SE:
                            i = p.eng[eng].copy(h128(cb[nt], h), sB(h, 384, 512))
                        else:
                            i = p.eng[eng].tensor_copy(h128(cb[nt], h), sB(h, 384, 512))
                        tok_bf[(nt, h)] = p.inc(eng, i)
                        bk.add_reader(BNK[h], tok_bf[(nt, h)])
                    yield
                    for h in range(H):
                        bk.pre_write(BNK[h], (t_pb[h], tok_bf[(nt, h)]))
                        ins = nc.tensor.matmul(sB(h, 384, 512), h128(cb[nt], h),
                                               h128(curb, h), start=True, stop=True)
                        bk.set_write(BNK[h], p.inc(TE, ins))
                    for h in range(H):
                        bk.read(BNK[h], VE)
                        p.eng[VE].drain()
                        i = p.eng[VE].tensor_tensor(h128(nxtb, h), sB(h, 384, 512),
                                                    h128(curb, h), AL.add)
                        t_pb[h] = p.inc(VE, i)
                        bk.add_reader(BNK[h], t_pb[h])
                    yield
                    curb, nxtb = nxtb, curb
                    cur_pair, nxt_pair = nxt_pair, (("cB", "cBt") if nxt_pair[0] == "cA"
                                                    else ("cA", "cAt"))
                for h in range(H):
                    pl, pt = cur_pair
                    bk.pre_write(BNK[h], (tok_bf[(pl, h)], tok_bf[(pt, h)]))
                    ins = nc.tensor.matmul(sB(h, 384, 512), h128(cb[pl], h),
                                           h128(cb[pt], h), start=True, stop=True)
                    bk.set_write(BNK[h], p.inc(TE, ins))
                for h in range(H):
                    bk.read(BNK[h], SE)
                    p.eng[SE].drain()
                    i = p.eng[SE].copy(h128(cb["c64t"], h), sB(h, 384, 512))
                    tok_bf[("c64t", h)] = p.inc(SE, i)
                    bk.add_reader(BNK[h], tok_bf[("c64t", h)])
                yield
                t_pf = {}
                t_ptf = {}
                for h in range(H):
                    bk.pre_write(BNK[h], (t_pb[h], tok_bf[("c64t", h)]))
                    ins = nc.tensor.matmul(sB(h, 384, 512), h128(cb["c64t"], h),
                                           h128(curb, h), start=True, stop=True)
                    bk.set_write(BNK[h], p.inc(TE, ins))
                for h in range(H):
                    bk.read(BNK[h], VE)
                    p.eng[VE].drain()
                    i = p.eng[VE].tensor_tensor(h128(pfA, h), sB(h, 384, 512),
                                                h128(curb, h), AL.add)
                    t_pf[h] = p.inc(VE, i)
                    bk.add_reader(BNK[h], t_pf[h])
                yield
                for h in range(H):
                    bk.pre_write(BNK[h], ())
                    nc.tensor.matmul(sB(h, 384, 512), h128(curb, h), h128(cb["c64t"], h),
                                     start=True, stop=False)
                    ins = nc.tensor.matmul(sB(h, 384, 512), h128(curb, h), eyeb[:],
                                           start=False, stop=True)
                    bk.set_write(BNK[h], p.inc(TE, ins))
                for h in range(H):
                    bk.read(BNK[h], SE)
                    p.eng[SE].drain()
                    i = p.eng[SE].copy(h128(ptfA, h), sB(h, 384, 512))
                    t_ptf[h] = p.inc(SE, i)
                    bk.add_reader(BNK[h], t_ptf[h])
                yield
                pf, pf2 = pfA, pfB
                ptf, ptf2 = ptfA, ptfB
                for it in range(2):
                    t_w = {}
                    for h in range(H):
                        bk.pre_write(BNK[h], (t_pf[h], t_m[h]))
                        ins = nc.tensor.matmul(sB(h, 384, 512), h128(m_t, h), h128(pf, h),
                                               start=True, stop=True)
                        bk.set_write(BNK[h], p.inc(TE, ins))
                    for h in range(H):
                        bk.read(BNK[h], VE)
                        p.eng[VE].drain()
                        i = p.eng[VE].scalar_tensor_tensor(h128(w_t, h), sB(h, 384, 512),
                                                           -1.0, eye2[:], AL.mult, AL.add)
                        t_w[h] = p.inc(VE, i)
                        bk.add_reader(BNK[h], t_w[h])
                    yield
                    for h in range(H):
                        bk.pre_write(BNK[h], (t_w[h], t_ptf[h]))
                        ins = nc.tensor.matmul(sB(h, 384, 512), h128(ptf, h), h128(w_t, h),
                                               start=True, stop=True)
                        bk.set_write(BNK[h], p.inc(TE, ins))
                    for h in range(H):
                        bk.read(BNK[h], SE)
                        p.eng[SE].drain()
                        dstp = h128(pfc, h) if it == 1 else h128(pf2, h)
                        i = p.eng[SE].copy(dstp, sB(h, 384, 512))
                        t_pf[h] = p.inc(SE, i)
                        bk.add_reader(BNK[h], t_pf[h])
                    yield
                    if it == 0:
                        for h in range(H):
                            bk.pre_write(BNK[h], ())
                            ins = nc.tensor.matmul(sB(h, 384, 512), h128(w_t, h),
                                                   h128(ptf, h), start=True, stop=True)
                            bk.set_write(BNK[h], p.inc(TE, ins))
                        for h in range(H):
                            bk.read(BNK[h], SE)
                            p.eng[SE].drain()
                            i = p.eng[SE].copy(h128(ptf2, h), sB(h, 384, 512))
                            t_ptf[h] = p.inc(SE, i)
                            bk.add_reader(BNK[h], t_ptf[h])
                        yield
                        ptf, ptf2 = ptf2, ptf
                    pf, pf2 = pf2, pf
                if b == 0 and ci == 0:
                    tap("c1_h0", c1f[:, 0:128], [p.tok(VE)])
                    tap("P_h0", pfc[:, 0:128], [p.tok(SE)])
                pend[(b, ci)] = dict(t_pf=dict(t_pf), t_gm=dict(t_gm), t_vb=t_vb,
                                     tx=tx, gmc=gmc, pfc=pfc, negc=negc,
                                     kfh=kfh, qfh=qfh, oD=oD, jx=jx)

            def state_part(b, ci):
                nonlocal rms_prev
                st = pend.pop((b, ci))
                t0 = ci * CH
                jx, oD = st["jx"], st["oD"]
                kfh, qfh = st["kfh"], st["qfh"]
                gmc, pfc, negc = st["gmc"], st["pfc"], st["negc"]
                t_pf, t_gm, t_vb, tx = st["t_pf"], st["t_gm"], st["t_vb"], st["tx"]
                t_sq_heads = []
                t_r = {}
                t_u = {}
                for h in range(H):
                    bk.pre_write(A(h), (t_vb,))
                    nc.tensor.matmul(s1(h, 0, 256), kfh(h, 0), s0_(h)[:, 0:256],
                                     start=True, stop=False)
                    ins = nc.tensor.matmul(s1(h, 0, 256), kfh(h, 1), s0_(h)[:, 256:512],
                                           start=False, stop=True)
                    bk.set_write(A(h), p.inc(TE, ins))
                    bk.read(A(h), VE)
                    p.eng[VE].drain()
                    i = p.eng[VE].scalar_tensor_tensor(h256(r_t, h), s1(h, 0, 256),
                                                       negc[:, h:h + 1],
                                                       vt[:, oD + h * 256:oD + (h + 1) * 256],
                                                       AL.mult, AL.add)
                    t_r[h] = p.inc(VE, i)
                    bk.add_reader(A(h), t_r[h])
                    if h % 2 == 1:
                        yield
                for h in range(H):
                    bk.pre_write(A(h), (t_pf[h], t_r[h]))
                    ins = nc.tensor.matmul(s1(h, 256, 512), h128(pfc, h), h256(r_t, h),
                                           start=True, stop=True)
                    bk.set_write(A(h), p.inc(TE, ins))
                    bk.read(A(h), SE)
                    p.eng[SE].drain()
                    i = p.eng[SE].copy(h256(u_t, h), s1(h, 256, 512))
                    t_u[h] = p.inc(SE, i)
                    bk.add_reader(A(h), t_u[h])
                    if h % 2 == 1:
                        yield
                for h in range(H):
                    bk.pre_write(A(h), (t_u[h], t_gm[h]))
                    nc.tensor.matmul(s1(h, 0, 256), qfh(h, 0), s0_(h)[:, 0:256],
                                     start=True, stop=False)
                    nc.tensor.matmul(s1(h, 0, 256), qfh(h, 1), s0_(h)[:, 256:512],
                                     start=False, stop=False)
                    ins = nc.tensor.matmul(s1(h, 0, 256), h128(gmc, h), h256(u_t, h),
                                           start=False, stop=True)
                    bk.set_write(A(h), p.inc(TE, ins))
                    bk.read(A(h), SE)
                    p.wait(SE, rms_prev)
                    p.eng[SE].drain()
                    i = p.eng[SE].copy(h256(on_t, h), s1(h, 0, 256))
                    bk.add_reader(A(h), p.inc(SE, i))
                    p.eng[SE].drain()
                    i = p.eng[SE].activation(junkD[:, 0:256], s1(h, 0, 256), AF.Square,
                                             accum_out=stats[:, 8 + h:9 + h])
                    t_ss = p.inc(SE, i)
                    bk.add_reader(A(h), t_ss)
                    t_sq_heads.append(t_ss)
                    if h % 2 == 1:
                        yield
                for h in range(H):
                    bk.pre_write(A(h), (t_u[h],))
                    for half in range(2):
                        ins = nc.tensor.matmul(
                            bank_ps[A(h)][:, half * 256:(half + 1) * 256],
                            kt[:, oD + h * 256 + half * 128: oD + h * 256 + half * 128 + 128],
                            h256(u_t, h), start=True, stop=True)
                    bk.set_write(A(h), p.inc(TE, ins))
                    bk.read(A(h), VE)
                    p.eng[VE].drain()
                    i = p.eng[VE].tensor_tensor(s0_(h)[:], s0_(h)[:],
                                                bank_ps[A(h)][:, 0:512], AL.add)
                    bk.add_reader(A(h), p.inc(VE, i))
                    if h % 2 == 1:
                        yield
                if b == 0 and ci == 0:
                    tap("u_h0", u_t[:, 0:256], [p.tok(SE)])
                p.waits(VE, tuple(t_sq_heads))
                p.eng[VE].drain()
                i = p.eng[VE].tensor_scalar(stats[:, 8:12], stats[:, 8:12],
                                            1.0 / DK, 1e-5, AL.mult, AL.add)
                p.inc(VE, i)
                rsqrt_newton(stats[:, 8:12], stats[:, 12:16],
                             stats[:, 16:20], stats[:, 20:24])
                for h in range(H):
                    p.eng[VE].drain()
                    i = p.eng[VE].scalar_tensor_tensor(
                        on_t[:, h * 256:(h + 1) * 256], on_t[:, h * 256:(h + 1) * 256],
                        stats[:, 12 + h:13 + h], gn_bc[:], AL.mult, AL.mult)
                rms_prev = p.inc(VE, i)
                if b == 0 and ci == 0:
                    tap("on_h0", on_t[:, 0:256], [rms_prev])
                yield
                for half in range(2):
                    for h in range(H):
                        bki = A(h)
                        off = half * 128
                        bk.pre_write(bki, (rms_prev,))
                        ins = nc.tensor.transpose(
                            bank_ps[bki][:, off:off + 128],
                            on_t[:, h * 256 + half * 128: h * 256 + half * 128 + 128],
                            eye[:])
                        bk.set_write(bki, p.inc(TE, ins))
                        dsl = slice(h * 256 + half * 128, h * 256 + half * 128 + 128)
                        bk.read(bki, SE)
                        p.eng[SE].drain()
                        i = p.eng[SE].copy(onf_hi[:, dsl], bank_ps[bki][:, off:off + 128])
                        t_of = p.inc(SE, i)
                        bk.add_reader(bki, t_of)
                        bk.read(bki, VE)
                        p.wait(VE, t_of)
                        p.eng[VE].drain()
                        i = p.eng[VE].scalar_tensor_tensor(
                            onf_lob[:, dsl], bank_ps[bki][:, off:off + 128], 1.0,
                            onf_hi[:, dsl].bitcast(F32), AL.mult, AL.subtract)
                        bk.add_reader(bki, p.inc(VE, i))
                        p.waits(GE, (t_of,))
                        p.eng[GE].drain()
                        i = p.eng[GE].tensor_copy(onf_hib[:, dsl],
                                                  onf_hi[:, dsl].bitcast(F32))
                        p.inc(GE, i)
                    yield
                t_split_all = (p.tok(SE), p.tok(VE), p.tok(GE))
                for half in range(2):
                    bki = 4 + half
                    bk.pre_write(bki, t_split_all)
                    for vc in range(NDC):
                        csl = slice(vc * 128, (vc + 1) * 128)
                        wsl = slice(vc * D + half * 512, vc * D + (half + 1) * 512)
                        p.waits(TE, t_wsplit[vc])
                        nc.tensor.matmul(bank_ps[bki][:], onf_hi[:, csl], wo_hi[:, wsl],
                                         start=(vc == 0), stop=False)
                        nc.tensor.matmul(bank_ps[bki][:], onf_hib[:, csl], wo_lob[:, wsl],
                                         start=False, stop=False)
                        ins = nc.tensor.matmul(bank_ps[bki][:], onf_lob[:, csl],
                                               wo_hib[:, wsl],
                                               start=False, stop=(vc == NDC - 1))
                        if vc == 3:
                            yield
                    bk.set_write(bki, p.inc(TE, ins))
                    bk.read(bki, VE)
                    p.eng[VE].drain()
                    i = p.eng[VE].tensor_tensor(qt[:, oD + half * 512:oD + (half + 1) * 512],
                                                bank_ps[bki][:],
                                                ga_bc[:, half * 512:(half + 1) * 512], AL.mult)
                    t_x1a = p.inc(VE, i)
                    bk.add_reader(bki, t_x1a)
                    p.waits(GE, (t_x1a, tx))
                    p.eng[GE].drain()
                    i = p.eng[GE].tensor_tensor(qt[:, oD + half * 512:oD + (half + 1) * 512],
                                                qt[:, oD + half * 512:oD + (half + 1) * 512],
                                                xt2[:, oD + half * 512:oD + (half + 1) * 512],
                                                AL.add)
                    p.inc(GE, i)
                    yield
                t_x1st = p.dma(GE, X1S[b, t0:t0 + CH, :], qt[:, oD:oD + D],
                               waits=(p.tok(GE),), slot=4 + jx)
                if b == 0 and ci == 0:
                    tap("x1_c0", qt[:, 0:D], [p.tok(GE)])
                cons[jx] = (p.tok(TE), p.tok(VE), p.tok(SE), p.tok(GE), t_x1st)

            def drive(gc, gs):
                done_c = gc is None
                done_s = gs is None
                k = 0
                while not (done_c and done_s):
                    if not done_c:
                        n = 2 if (k % 3) != 2 else 1
                        for _ in range(n):
                            try:
                                next(gc)
                            except StopIteration:
                                done_c = True
                                break
                    if not done_s:
                        try:
                            next(gs)
                        except StopIteration:
                            done_s = True
                    k += 1

            for b in range(n_seq):
                i = p.eng[VE].memset(s_state[:], 0.0)
                p.inc(VE, i)
                build_gate_bc(2, ga_bc, b, bank_ps[0], bk, 0)
                p.barrier()
                cons = [(), ()]
                drive(chain_part(b, 0), None)
                for ci in range(NCH):
                    gc2 = chain_part(b, ci + 1) if ci + 1 < NCH else None
                    drive(gc2, state_part(b, ci))
            p.barrier()

        with ExitStack() as s3:
            D2, D4 = D // 2, D // 4
            wst = junkD[:]
            cn1_hi = sb("cn1_hi", [128, NDC * D], F32R, stx=s3)
            cn1_hib = sb("cn1_hib", [128, NDC * D], BF16, stx=s3)
            cn1_lob = sb("cn1_lob", [128, NDC * D], BF16, stx=s3)
            cn2_hi = sb("cn2_hi", [128, NDC * D2], F32R, stx=s3)
            cn2_hib = sb("cn2_hib", [128, NDC * D2], BF16, stx=s3)
            cn2_lob = sb("cn2_lob", [128, NDC * D2], BF16, stx=s3)
            cn3_t = sb("cn3_t", [128, 4 * D4], stx=s3)
            cn4_t = sb("cn4_t", [128, 2], stx=s3)
            cnb = sb("cnb", [128, 16], stx=s3)
            gb_bc = sb("gb_bc", [128, D], stx=s3)
            TH = max(T // 4, CH)
            NCHH = TH // CH
            h2res = sb("h2res", [128, TH * NDC], stx=s3)
            h2sum = sb("h2sum", [128, 16], stx=s3)

            TILE = min(512, TH)
            NPT = TH // TILE
            spl_hi = sb("spl_hi", [128, NDC * TILE], F32R, stx=s3)
            spl_hib = sb("spl_hib", [128, NDC * TILE], BF16, stx=s3)
            spl_lob = sb("spl_lob", [128, NDC * TILE], BF16, stx=s3)
            g1_hi = sb("g1_hi", [128, NDC * TILE], F32R, stx=s3)
            g1_hib = sb("g1_hib", [128, NDC * TILE], BF16, stx=s3)
            g1_lob = sb("g1_lob", [128, NDC * TILE], BF16, stx=s3)
            g2f = sb("g2f", [128, 4 * TILE], stx=s3)
            tf_row = junkD[:1, 0:TILE]
            assert 2 * TILE <= 1024
            gcw_t = sb("gcw", [128, 1024], stx=s3)
            gcw = gcw_t[:]
            g3f = gcw_t[:, 0:2 * TILE]
            if 4 * TILE >= 2 * D:
                x1b = g2f[:, 0:D]
                outb = g2f[:, D:2 * D]
            else:
                x1b = sb("x1b_dev", [128, D], stx=s3)[:]
                outb = sb("outb_dev", [128, D], stx=s3)[:]
            tw1 = g2f[:, 0:TILE]
            tw2 = g2f[:, TILE:2 * TILE]
            bank_ps = [psum(f"f{i}", [128, 512], stx=s3) for i in range(6)]
            bk = Banks(p, bank_ps)

            def load_split(Wd, hi, hib, lob, ncols, nchunks):
                toks = []
                prev = None
                for dc in range(nchunks):
                    dsl = slice(dc * ncols, (dc + 1) * ncols)
                    tl = p.dma(SY, wst[:, 0:ncols], Wd[dc * 128:(dc + 1) * 128, :],
                               waits=(prev,) if prev else ())
                    p.wait(SE, tl)
                    p.eng[SE].drain()
                    i = p.eng[SE].copy(hi[:, dsl], wst[:, 0:ncols])
                    th = p.inc(SE, i)
                    p.waits(GE, (tl, th))
                    p.eng[GE].drain()
                    i = p.eng[GE].tensor_tensor(lob[:, dsl], wst[:, 0:ncols],
                                                hi[:, dsl].bitcast(F32), AL.subtract)
                    tlo = p.inc(GE, i)
                    p.wait(VE, th)
                    p.eng[VE].drain()
                    i = p.eng[VE].tensor_copy(hib[:, dsl], hi[:, dsl].bitcast(F32))
                    thb = p.inc(VE, i)
                    prev = tlo
                    toks.append((tlo, thb))
                return toks

            t_cn1 = load_split(CW[0], cn1_hi, cn1_hib, cn1_lob, D, NDC)
            p.barrier()
            t_cn2 = load_split(CW[1], cn2_hi, cn2_hib, cn2_lob, D2, NDC)
            t_cn3 = [p.dma(SY, cn3_t[:, dc * D4:(dc + 1) * D4],
                           CW[2][dc * 128:(dc + 1) * 128, :]) for dc in range(4)]
            t_cn4 = [p.dma(SY, cn4_t[:, dc:dc + 1], CW[3][dc * 128:(dc + 1) * 128, :])
                     for dc in range(2)]
            for dc in range(8):
                p.dma(SY, cnb[:, dc:dc + 1], CB[0][dc * 128:(dc + 1) * 128])
            for dc in range(4):
                p.dma(SY, cnb[:, 8 + dc:9 + dc], CB[1][dc * 128:(dc + 1) * 128])
            for dc in range(2):
                p.dma(SY, cnb[:, 12 + dc:13 + dc], CB[2][dc * 128:(dc + 1) * 128])
            t_cb4 = p.dma(SY, cnb[:1, 14:15], CB[3][:])
            t_cnb = p.tok("dma3")
            p.barrier()

            def pb_seq(b):
                i = p.eng[VE].memset(h2sum[:], 0.0)
                p.inc(VE, i)
                build_gate_bc(5, gb_bc, b, bank_ps[0], bk, 0)
                t_bar = p.barrier()
                b1gate = (t_bar,)
                t_h2st = {}
                for ci in range(NCH):
                    t0 = ci * CH
                    tx1 = p.dma(SY, x1b, X1S[b, t0:t0 + CH, :], waits=b1gate)
                    ln_stats(x1b, (tx1,))
                    p.eng[VE].drain()
                    i = p.eng[VE].tensor_scalar(x1b, x1b, stats[:, 2:3], stats[:, 7:8],
                                                AL.add, AL.mult)
                    t_xn = p.inc(VE, i)
                    for dc in range(NDC):
                        bki = dc % 2
                        off = ((dc // 2) % 2) * 128
                        bk.pre_write(bki, (t_xn,))
                        ins = nc.tensor.transpose(bank_ps[bki][:, off:off + 128],
                                                  x1b[:, dc * 128:(dc + 1) * 128], eye[:])
                        bk.set_write(bki, p.inc(TE, ins))
                        bk.read(bki, VE)
                        p.wait(VE, t_h2st.get(ci - 1))
                        p.eng[VE].drain()
                        i = p.eng[VE].tensor_scalar(
                            outb[:, dc * 128:(dc + 1) * 128], bank_ps[bki][:, off:off + 128],
                            cb1[:, dc * n_seq + b: dc * n_seq + b + 1],
                            modcol(3, dc, b), AL.mult, AL.add)
                        t_h2 = p.inc(VE, i)
                        bk.add_reader(bki, t_h2)
                        p.eng[VE].drain()
                        i = p.eng[VE].tensor_scalar(junkD[:, 0:CH],
                                                    outb[:, dc * 128:(dc + 1) * 128],
                                                    1.0, h2sum[:, 8 + dc:9 + dc],
                                                    AL.mult, AL.add,
                                                    accum_out=h2sum[:, 8 + dc:9 + dc])
                        p.inc(VE, i)
                        t_h2st[ci] = p.dma(GE, H2S[b, ci, dc * 128:(dc + 1) * 128, :],
                                           outb[:, dc * 128:(dc + 1) * 128], waits=(t_h2,),
                                           slot=4 + ci % 2)
                    b1gate = (p.tok(TE), p.tok(VE), p.tok(SE))
                t_bar = p.barrier()
                tap("h2sum" if b == 0 else "_skip", h2sum[:, 8:16], [p.tok(VE)])

                p.eng[VE].drain()
                i = p.eng[VE].tensor_scalar(h2sum[:, 0:8], h2sum[:, 8:16], 1.0 / T, None,
                                            AL.mult)
                t_mean = p.inc(VE, i)
                g1c = h2sum[:, 8:12]
                g2c = h2sum[:, 12:14]
                t_mv = None
                t_act_prev = None
                for m in range(4):
                    ldt = []
                    for dc in range(NDC):
                        ldt.append(p.dma(SY, gcw[:, dc * 128:(dc + 1) * 128],
                                         GW[0][dc * 128:(dc + 1) * 128, m * 128:(m + 1) * 128],
                                         waits=(t_mv,) if (t_mv and dc == 0) else ()))
                    bk.pre_write(4, (t_mean,))
                    p.waits(TE, tuple(ldt))
                    for dc in range(NDC):
                        ins = nc.tensor.matmul(bank_ps[4][:, m:m + 1],
                                               gcw[:, dc * 128:(dc + 1) * 128],
                                               h2sum[:, dc:dc + 1],
                                               start=(dc == 0), stop=(dc == NDC - 1))
                    t_mv = p.inc(TE, ins)
                    bk.set_write(4, t_mv)
                    tb1 = p.dma(SY, stats[:, 20:21], GB[0][m * 128:(m + 1) * 128],
                                waits=(t_act_prev,) if t_act_prev else ())
                    bk.read(4, SE)
                    p.waits(SE, (tb1,))
                    p.eng[SE].drain()
                    i = p.eng[SE].activation(g1c[:, m:m + 1], bank_ps[4][:, m:m + 1], AF.Gelu,
                                             bias=stats[:, 20:21], scale=1.0)
                    t_act_prev = p.inc(SE, i)
                    bk.add_reader(4, t_act_prev)
                for m in range(2):
                    ldt = []
                    for dc in range(4):
                        ldt.append(p.dma(SY, gcw[:, dc * 128:(dc + 1) * 128],
                                         GW[1][dc * 128:(dc + 1) * 128, m * 128:(m + 1) * 128],
                                         waits=(t_mv,) if dc == 0 else ()))
                    bk.pre_write(4, ())
                    p.waits(TE, tuple(ldt))
                    for dc in range(4):
                        ins = nc.tensor.matmul(bank_ps[4][:, 8 + m:9 + m],
                                               gcw[:, dc * 128:(dc + 1) * 128],
                                               g1c[:, dc:dc + 1],
                                               start=(dc == 0), stop=(dc == 3))
                    t_mv = p.inc(TE, ins)
                    bk.set_write(4, t_mv)
                    tb2 = p.dma(SY, stats[:, 20:21], GB[1][m * 128:(m + 1) * 128],
                                waits=(t_act_prev,))
                    bk.read(4, SE)
                    p.waits(SE, (tb2,))
                    p.eng[SE].drain()
                    i = p.eng[SE].activation(g2c[:, m:m + 1], bank_ps[4][:, 8 + m:9 + m],
                                             AF.Gelu, bias=stats[:, 20:21], scale=1.0)
                    t_act_prev = p.inc(SE, i)
                    bk.add_reader(4, t_act_prev)
                ldt = [p.dma(SY, gcw[:, dc * 128:dc * 128 + 1],
                             GW[2][dc * 128:(dc + 1) * 128, :], waits=(t_mv,) if dc == 0 else ())
                       for dc in range(2)]
                bk.pre_write(4, ())
                p.waits(TE, tuple(ldt))
                for dc in range(2):
                    ins = nc.tensor.matmul(bank_ps[4][:1, 12:13], gcw[:, dc * 128:dc * 128 + 1],
                                           g2c[:, dc:dc + 1], start=(dc == 0), stop=(dc == 1))
                t_mv = p.inc(TE, ins)
                bk.set_write(4, t_mv)
                tb3 = p.dma(SY, stats[:1, 20:21], GB[2][:], waits=(t_act_prev,))
                bk.read(4, SE)
                p.waits(SE, (tb3,))
                p.eng[SE].drain()
                i = p.eng[SE].activation(stats[:1, 21:22], bank_ps[4][:1, 12:13], AF.Sigmoid,
                                         bias=stats[:1, 20:21], scale=1.0)
                t_gf = p.inc(SE, i)
                bk.add_reader(4, t_gf)
                p.waits(VE, (t_gf, t_ar))
                p.eng[VE].drain()
                p.eng[VE].tensor_tensor(gfk[:, 2 * b:2 * b + 1], stats[:1, 21:22], ar_t[:],
                                        AL.mult)
                p.eng[VE].drain()
                p.eng[VE].tensor_scalar(gfk[:, 2 * b:2 * b + 1], gfk[:, 2 * b:2 * b + 1],
                                        1.4, 0.1, AL.mult, AL.add)
                p.eng[VE].drain()
                p.eng[VE].tensor_tensor(gfk[:, 2 * b:2 * b + 1], gfk[:, 2 * b:2 * b + 1],
                                        ar_t[:], AL.subtract)
                p.eng[VE].drain()
                i = p.eng[VE].tensor_scalar(gfk[:, 2 * b + 1:2 * b + 2], ar_t[:], 0.6, None,
                                            AL.mult)
                p.inc(VE, i)
                if b == 0:
                    tap("gf", stats[:1, 21:22], [p.tok(SE)])
                t_bar = p.barrier()

                t_outst = None
                for half in range(T // TH):
                  ci_base = half * NCHH
                  for cl in range(NCHH):
                      for dc in range(NDC):
                          col = (dc * NCHH + cl) * CH
                          gate_h = t_bar if (t_bar and isinstance(t_bar[0], tuple)) else (t_bar,)
                          p.dma(SY, h2res[:, col:col + CH],
                                H2S[b, ci_base + cl, dc * 128:(dc + 1) * 128, :],
                                waits=gate_h if (cl == 0 and dc == 0) else ())
                  t_ld = p.tok("dma3")
                  tw_tok = {}
                  spl_prev = None
                  for cyc in range(3):
                    for ti in range(NPT):
                        tc0 = ti * TILE

                        def outsl(dc):
                            base = dc * NCHH * CH + tc0
                            return h2res[:, base:base + TILE]

                        prev_g1_split = None
                        t_sp = []
                        gate0 = (t_ld, tw_tok.get(ti), spl_prev)
                        for dc in range(NDC):
                            ssl = slice(dc * TILE, (dc + 1) * TILE)
                            deps0 = gate0 if dc == 0 else ()
                            p.waits(SE, deps0)
                            p.eng[SE].drain()
                            i = p.eng[SE].copy(spl_hi[:, ssl], outsl(dc))
                            th = p.inc(SE, i)
                            p.waits(GE, deps0 + (th,))
                            p.eng[GE].drain()
                            i = p.eng[GE].tensor_tensor(spl_lob[:, ssl], outsl(dc),
                                                        spl_hi[:, ssl].bitcast(F32),
                                                        AL.subtract)
                            tlo = p.inc(GE, i)
                            p.waits(VE, deps0 + (th,))
                            p.eng[VE].drain()
                            i = p.eng[VE].tensor_copy(spl_hib[:, ssl],
                                                      spl_hi[:, ssl].bitcast(F32))
                            thb = p.inc(VE, i)
                            t_sp.extend((tlo, thb))
                        spl_prev = t_sp[-1]
                        for m in range(NDC):
                            bki = 2 + m % 2
                            bk.pre_write(bki, tuple(t_sp) + tuple(t for pr in t_cn1 for t in pr))
                            for dc in range(NDC):
                                wsl = slice(dc * D + m * 128, dc * D + m * 128 + 128)
                                ssl = slice(dc * TILE, (dc + 1) * TILE)
                                nc.tensor.matmul(bank_ps[bki][:, 0:TILE], cn1_hi[:, wsl],
                                                 spl_hi[:, ssl], start=(dc == 0), stop=False)
                                nc.tensor.matmul(bank_ps[bki][:, 0:TILE], cn1_lob[:, wsl],
                                                 spl_hib[:, ssl], start=False, stop=False)
                                ins = nc.tensor.matmul(bank_ps[bki][:, 0:TILE], cn1_hib[:, wsl],
                                                       spl_lob[:, ssl], start=False,
                                                       stop=(dc == NDC - 1))
                            bk.set_write(bki, p.inc(TE, ins))
                            bk.pre_write(5, ())
                            bk.read(bki, SE)
                            p.waits(SE, (t_cnb, prev_g1_split))
                            p.eng[SE].drain()
                            i = p.eng[SE].activation(bank_ps[5][:, 0:TILE],
                                                     bank_ps[bki][:, 0:TILE], AF.Gelu,
                                                     bias=cnb[:, m:m + 1], scale=1.0)
                            t_g1e = p.inc(SE, i)
                            bk.add_reader(bki, t_g1e)
                            bk.set_write(5, t_g1e)
                            bk.read(5, SE)
                            p.eng[SE].drain()
                            i = p.eng[SE].copy(g1_hi[:, m * TILE:(m + 1) * TILE],
                                               bank_ps[5][:, 0:TILE])
                            t_hi = p.inc(SE, i)
                            bk.add_reader(5, t_hi)
                            bk.read(5, VE)
                            p.wait(VE, t_hi)
                            p.eng[VE].drain()
                            i = p.eng[VE].scalar_tensor_tensor(
                                g1_lob[:, m * TILE:(m + 1) * TILE],
                                bank_ps[5][:, 0:TILE], 1.0,
                                g1_hi[:, m * TILE:(m + 1) * TILE].bitcast(F32),
                                AL.mult, AL.subtract)
                            prev_g1_split = p.inc(VE, i)
                            bk.add_reader(5, prev_g1_split)
                            p.waits(GE, (t_hi,))
                            p.eng[GE].drain()
                            i = p.eng[GE].tensor_copy(
                                g1_hib[:, m * TILE:(m + 1) * TILE],
                                g1_hi[:, m * TILE:(m + 1) * TILE].bitcast(F32))
                            p.inc(GE, i)
                        t_g1s = (p.tok(SE), p.tok(VE), p.tok(GE))
                        for m in range(4):
                            bki = 2 + m % 2
                            bk.pre_write(bki, t_g1s + tuple(t for pr in t_cn2 for t in pr))
                            for dc in range(NDC):
                                wsl = slice(dc * D2 + m * 128, dc * D2 + m * 128 + 128)
                                ssl = slice(dc * TILE, (dc + 1) * TILE)
                                nc.tensor.matmul(bank_ps[bki][:, 0:TILE], cn2_hi[:, wsl],
                                                 g1_hi[:, ssl], start=(dc == 0), stop=False)
                                nc.tensor.matmul(bank_ps[bki][:, 0:TILE], cn2_lob[:, wsl],
                                                 g1_hib[:, ssl], start=False, stop=False)
                                ins = nc.tensor.matmul(bank_ps[bki][:, 0:TILE], cn2_hib[:, wsl],
                                                       g1_lob[:, ssl], start=False,
                                                       stop=(dc == NDC - 1))
                            bk.set_write(bki, p.inc(TE, ins))
                            bk.read(bki, SE)
                            p.wait(SE, p.tok(GE))
                            p.eng[SE].drain()
                            i = p.eng[SE].activation(g2f[:, m * TILE:(m + 1) * TILE],
                                                     bank_ps[bki][:, 0:TILE], AF.Gelu,
                                                     bias=cnb[:, 8 + m:9 + m], scale=1.0)
                            bk.add_reader(bki, p.inc(SE, i))
                        t_g2 = p.tok(SE)
                        for m in range(2):
                            bki = 2 + m
                            bk.pre_write(bki, (t_g2,) + tuple(t_cn3))
                            for dc in range(4):
                                wsl = slice(dc * D4 + m * 128, dc * D4 + m * 128 + 128)
                                ins = nc.tensor.matmul(bank_ps[bki][:, 0:TILE], cn3_t[:, wsl],
                                                       g2f[:, dc * TILE:(dc + 1) * TILE],
                                                       start=(dc == 0), stop=(dc == 3))
                            bk.set_write(bki, p.inc(TE, ins))
                            bk.read(bki, SE)
                            p.eng[SE].drain()
                            i = p.eng[SE].activation(g3f[:, m * TILE:(m + 1) * TILE],
                                                     bank_ps[bki][:, 0:TILE], AF.Gelu,
                                                     bias=cnb[:, 12 + m:13 + m], scale=1.0)
                            bk.add_reader(bki, p.inc(SE, i))
                        t_g3 = p.tok(SE)
                        bk.pre_write(1, (t_g3,) + tuple(t_cn4))
                        for dc in range(2):
                            ins = nc.tensor.matmul(bank_ps[1][:1, 0:TILE], cn4_t[:, dc:dc + 1],
                                                   g3f[:, dc * TILE:(dc + 1) * TILE],
                                                   start=(dc == 0), stop=(dc == 1))
                        bk.set_write(1, p.inc(TE, ins))
                        bk.read(1, SE)
                        p.waits(SE, (t_cb4,))
                        p.eng[SE].drain()
                        i = p.eng[SE].activation(tf_row, bank_ps[1][:1, 0:TILE], AF.Sigmoid,
                                                 bias=cnb[:1, 14:15], scale=1.0)
                        t_tf = p.inc(SE, i)
                        bk.add_reader(1, t_tf)
                        p.waits(VE, (t_tf,))
                        p.eng[VE].drain()
                        i = p.eng[VE].tensor_scalar(tf_row, tf_row,
                                                    gfk[:, 2 * b + 1:2 * b + 2],
                                                    gfk[:, 2 * b:2 * b + 1], AL.mult, AL.add)
                        t_cp = p.inc(VE, i)
                        bk.pre_write(1, (t_cp,))
                        ins = nc.tensor.matmul(bank_ps[1][:, 0:TILE], ones1[:], tf_row,
                                               start=True, stop=True)
                        bk.set_write(1, p.inc(TE, ins))
                        prev_tw = None
                        for dc in range(4):
                            q13 = outsl(dc)
                            q24 = outsl(dc + 4)
                            bk.read(1, VE)
                            p.wait(VE, prev_tw)
                            p.eng[VE].drain()
                            p.eng[VE].tensor_tensor(tw1, q24, bank_ps[1][:, 0:TILE], AL.mult)
                            p.eng[VE].drain()
                            i = p.eng[VE].tensor_tensor(tw2, q13, bank_ps[1][:, 0:TILE],
                                                        AL.mult)
                            tsc = p.inc(VE, i)
                            bk.add_reader(1, tsc)
                            p.waits(GE, (tsc,))
                            p.eng[GE].drain()
                            p.eng[GE].tensor_tensor(q13, q13, tw1,
                                                    AL.add if dc < 2 else AL.subtract)
                            p.eng[GE].drain()
                            i = p.eng[GE].tensor_tensor(q24, q24, tw2,
                                                        AL.subtract if dc < 2 else AL.add)
                            prev_tw = p.inc(GE, i)
                        tw_tok[ti] = prev_tw

                  b3gate = tuple(tw_tok.values())
                  for cl in range(NCHH):
                    ci = ci_base + cl
                    t0 = ci * CH
                    tx1 = p.dma(SY, x1b, X1S[b, t0:t0 + CH, :],
                                waits=b3gate if cl == 0 else (p.tok(GE),))
                    for dc in range(NDC):
                        bki = dc % 2
                        off = ((dc // 2) % 2) * 128
                        col = (dc * NCHH + cl) * CH
                        bk.pre_write(bki, b3gate if (cl == 0 and dc == 0) else ())
                        ins = nc.tensor.transpose(bank_ps[bki][:, off:off + 128],
                                                  h2res[:, col:col + CH], eye[:])
                        bk.set_write(bki, p.inc(TE, ins))
                        bk.read(bki, VE)
                        if dc == 0:
                            p.wait(VE, p.tok(GE))
                        p.eng[VE].drain()
                        i = p.eng[VE].tensor_tensor(junkD[:, dc * 128:(dc + 1) * 128],
                                                    bank_ps[bki][:, off:off + 128],
                                                    gb_bc[:, dc * 128:(dc + 1) * 128], AL.mult)
                        t_m = p.inc(VE, i)
                        bk.add_reader(bki, t_m)
                        p.waits(GE, (t_m, tx1) + ((t_outst,) if (t_outst and dc == 0) else ()))
                        p.eng[GE].drain()
                        i = p.eng[GE].tensor_tensor(outb[:, dc * 128:(dc + 1) * 128],
                                                    junkD[:, dc * 128:(dc + 1) * 128],
                                                    x1b[:, dc * 128:(dc + 1) * 128], AL.add)
                        p.inc(GE, i)
                    t_outst = p.dma(GE, OUT[b, t0:t0 + CH, :], outb, waits=(p.tok(GE),),
                                    slot=4 + cl % 2)
                  t_bar = (p.tok(TE), p.tok(GE), p.tok(VE), t_outst)

            for b in range(n_seq):
                pb_seq(b)
            p.barrier()

        for d in ("dma4", "dma5"):
            if p.cnt[d]:
                p.eng[GE].wait_ge(p.sem[d], p.cnt[d])
        if p.cnt[GE]:
            p.eng[GE].wait_ge(p.sem[GE], p.cnt[GE])

    return nc, tap_handles


def _consts():
    return {
        "eye": np.eye(128, dtype=np.float32),
        "masku": np.triu(np.ones((128, 128), np.float32), 1),
        "maskl": np.tril(np.ones((128, 128), np.float32), -1),
        "maskui": np.triu(np.ones((128, 128), np.float32), 0),
        "ones1": np.ones((1, 128), np.float32),
        "sel4": np.concatenate([np.tile((np.arange(4) == h).astype(np.float32)[:, None],
                                        (1, 128)) for h in range(4)], axis=1),
    }


def make_in_map(inputs, c, n_seq):
    m = {
        "x": np.ascontiguousarray(inputs["L0_attn_x"][c * n_seq:(c + 1) * n_seq]),
        "c": np.ascontiguousarray(inputs["L0_c"][c * n_seq:(c + 1) * n_seq]),
        "wmod": inputs["L0_wmod"],
        "wq": inputs["L0_attn_Wq"], "wk": inputs["L0_attn_Wk"],
        "wv": inputs["L0_attn_Wv"], "wb": inputs["L0_attn_Wbeta"],
        "wo": inputs["L0_attn_Wo"], "g_norm": inputs["L0_attn_g_norm"],
        "cn_w1": inputs["L0_ffn_cn_w1"], "cn_b1": inputs["L0_ffn_cn_b1"],
        "cn_w2": inputs["L0_ffn_cn_w2"], "cn_b2": inputs["L0_ffn_cn_b2"],
        "cn_w3": inputs["L0_ffn_cn_w3"], "cn_b3": inputs["L0_ffn_cn_b3"],
        "cn_w4": inputs["L0_ffn_cn_w4"], "cn_b4": inputs["L0_ffn_cn_b4"],
        "gc_w1": inputs["L0_ffn_gc_w1"], "gc_b1": inputs["L0_ffn_gc_b1"],
        "gc_w2": inputs["L0_ffn_gc_w2"], "gc_b2": inputs["L0_ffn_gc_b2"],
        "gc_w3": inputs["L0_ffn_gc_w3"], "gc_b3": inputs["L0_ffn_gc_b3"],
        "ar": inputs["L0_ffn_adaptive_range"],
    }
    m.update(_consts())
    return m


def kernel(**inputs):
    n_cores = 8
    n_seq = 32 // n_cores
    nc, _ = build_program(n_seq, 2048)
    in_maps = [make_in_map(inputs, c, n_seq) for c in range(n_cores)]
    res = run_bass_kernel_spmd(nc, in_maps, core_ids=list(range(n_cores)))
    return np.concatenate([r["out"] for r in res.results], axis=0)


# version 17
# speedup vs baseline: 1.0406x; 1.0406x over previous
import numpy as np
from contextlib import ExitStack

import concourse.bass as bass
import concourse.mybir as mybir
from concourse.bass_utils import run_bass_kernel_spmd

F32 = mybir.dt.float32
F32R = mybir.dt.float32r
BF16 = mybir.dt.bfloat16
AL = mybir.AluOpType
AF = mybir.ActivationFunctionType
AX = mybir.AxisListType

D = 1024
H = 4
DK = 256
CH = 128
NDC = D // 128


class P:
    ENGS = ("tensor", "vector", "scalar", "gpsimd", "sync")
    DSEMS = tuple(f"dma{i}" for i in range(6))
    QOF = {"sync": (0, 1, 2, 3), "gpsimd": (4, 5)}

    def __init__(self, nc, st):
        self.sem = {e: st.enter_context(nc.semaphore("sem_" + e))
                    for e in self.ENGS + self.DSEMS}
        self.cnt = {e: 0 for e in self.ENGS + self.DSEMS}
        self.waited = {}
        self.eng = {e: getattr(nc, e) for e in self.ENGS}
        self.fence = {d: 0 for d in self.DSEMS}
        self.selfw = {d: 0 for d in self.DSEMS}

    def inc(self, prod, instr, n=1):
        instr.then_inc(self.sem[prod], n)
        self.cnt[prod] += n
        return (prod, self.cnt[prod])

    def dma(self, q, out, in_, waits=(), slot=None):
        if slot is None:
            slot = 3 if q == "sync" else 4
        assert slot in self.QOF[q], (q, slot)
        d = f"dma{slot}"
        if self.fence[d] > self.selfw[d]:
            self.eng[q].wait_ge(self.sem[d], self.fence[d])
            self.selfw[d] = self.fence[d]
        for t in waits:
            if t is not None and t[0] == q:
                if self.waited.get((q, q), 0) < t[1]:
                    self.eng[q].wait_ge(self.sem[q], t[1])
                    self.waited[(q, q)] = t[1]
            else:
                self.wait(q, t)
        ins = self.eng[q].dma_start(out=out, in_=in_)
        ins.then_inc(self.sem[d], 16)
        self.cnt[d] += 16
        return (d, self.cnt[d])

    def tok(self, prod):
        return (prod, self.cnt[prod])

    def wait(self, cons, token):
        if token is None:
            return
        prod, val = token
        if val <= 0:
            return
        if prod.startswith("dma"):
            val = self.cnt[prod]
            self.fence[prod] = max(self.fence[prod], val)
            if (cons == "sync" and prod in ("dma0", "dma1", "dma2", "dma3")) or \
               (cons == "gpsimd" and prod in ("dma4", "dma5")):
                if val > self.selfw[prod]:
                    self.eng[cons].wait_ge(self.sem[prod], val)
                    self.selfw[prod] = max(self.selfw[prod], val)
                return
        elif prod == cons:
            return
        if self.waited.get((cons, prod), 0) >= val:
            return
        self.eng[cons].wait_ge(self.sem[prod], val)
        self.waited[(cons, prod)] = val

    def waits(self, cons, tokens):
        for t in tokens:
            self.wait(cons, t)

    def barrier(self):
        for q, slots in self.QOF.items():
            for s in slots:
                d = f"dma{s}"
                if self.cnt[d] > self.selfw[d]:
                    self.eng[q].wait_ge(self.sem[d], self.cnt[d])
                    self.selfw[d] = self.cnt[d]
                    self.fence[d] = self.cnt[d]
        toks = {}
        for e in ("tensor", "vector", "scalar", "gpsimd"):
            toks[e] = self.inc(e, self.eng[e].drain())
        t_sy = self.inc("sync", self.eng["sync"].nop())
        for c in self.ENGS:
            for e, t in toks.items():
                self.wait(c, t)
            if c != "sync":
                self.wait(c, t_sy)
        for d in self.DSEMS:
            for c in self.ENGS:
                if self.cnt[d] > 0:
                    self.wait(c, (d, self.cnt[d]))
        return toks["gpsimd"]


class Banks:

    def __init__(self, p, banks):
        self.p = p
        self.b = banks
        self.w = [None] * len(banks)
        self.r = [[] for _ in banks]

    def pre_write(self, i, extra=()):
        self.p.waits("tensor", tuple(self.r[i]) + tuple(extra))
        self.r[i] = []

    def set_write(self, i, tok):
        self.w[i] = tok

    def read(self, i, eng):
        self.p.wait(eng, self.w[i])

    def add_reader(self, i, tok):
        self.r[i].append(tok)


def build_program(n_seq, T, taps=None):
    taps = set(taps or ())
    nc = bass.Bass("TRN2", target_bir_lowering=False)
    NCH = T // CH

    def dram(n, s, k=None):
        return (nc.dram_tensor(n, s, F32, kind=k) if k else nc.dram_tensor(n, s, F32)).ap()

    X = dram("x", [n_seq, T, D], "ExternalInput")
    C_in = dram("c", [n_seq, D], "ExternalInput")
    WMOD = dram("wmod", [D, 6 * D], "ExternalInput")
    WQ = dram("wq", [D, D], "ExternalInput")
    WK = dram("wk", [D, D], "ExternalInput")
    WV = dram("wv", [D, D], "ExternalInput")
    WB = dram("wb", [D, H], "ExternalInput")
    WO = dram("wo", [D, D], "ExternalInput")
    GN = dram("g_norm", [DK], "ExternalInput")
    CW = [dram("cn_w1", [D, D], "ExternalInput"), dram("cn_w2", [D, D // 2], "ExternalInput"),
          dram("cn_w3", [D // 2, D // 4], "ExternalInput"), dram("cn_w4", [D // 4, 1], "ExternalInput")]
    CB = [dram("cn_b1", [D], "ExternalInput"), dram("cn_b2", [D // 2], "ExternalInput"),
          dram("cn_b3", [D // 4], "ExternalInput"), dram("cn_b4", [1], "ExternalInput")]
    GW = [dram("gc_w1", [D, D // 2], "ExternalInput"), dram("gc_w2", [D // 2, D // 4], "ExternalInput"),
          dram("gc_w3", [D // 4, 1], "ExternalInput")]
    GB = [dram("gc_b1", [D // 2], "ExternalInput"), dram("gc_b2", [D // 4], "ExternalInput"),
          dram("gc_b3", [1], "ExternalInput")]
    AR = dram("ar", [1], "ExternalInput")
    EYE = dram("eye", [128, 128], "ExternalInput")
    MASKU = dram("masku", [128, 128], "ExternalInput")
    MASKL = dram("maskl", [128, 128], "ExternalInput")
    MASKUI = dram("maskui", [128, 128], "ExternalInput")
    ONES1 = dram("ones1", [1, 128], "ExternalInput")
    SEL4 = dram("sel4", [4, 4 * 128], "ExternalInput")

    OUT = dram("out", [n_seq, T, D], "ExternalOutput")

    H1S = dram("h1_scr", [n_seq, NCH, D, CH])
    QS = dram("q_scr", [n_seq, T, D])
    KS = dram("k_scr", [n_seq, T, D])
    VS = dram("v_scr", [n_seq, T, D])
    BSc = dram("b_scr", [n_seq, T, H])
    X1S = dram("x1_scr", [n_seq, T, D])
    H2S = dram("h2_scr", [n_seq, NCH, D, CH])

    tap_handles = {}

    with ExitStack() as st:
        def sb(name, shape, dt=F32, stx=None):
            return (stx or st).enter_context(nc.sbuf_tensor("sb_" + name, shape, dt))

        def psum(name, shape, stx=None):
            return (stx or st).enter_context(nc.psum_tensor("ps_" + name, shape, F32))

        p = P(nc, st)
        TE, VE, SE, GE, SY = "tensor", "vector", "scalar", "gpsimd", "sync"

        def tap(name, ap, deps):
            if name in taps:
                t = nc.dram_tensor("dbg_" + name, list(ap.shape), F32,
                                   kind="ExternalOutput").ap()
                tap_handles[name] = t
                tk = p.dma(GE, t[...], ap, waits=tuple(deps))
                for e in P.ENGS:
                    if e != "gpsimd":
                        p.wait(e, tk)
                p.wait("gpsimd", tk)

        eye = sb("eye", [128, 128])
        masku = sb("masku", [128, 128])
        maskl = sb("maskl", [128, 128])
        maskui = sb("maskui", [128, 128])
        ones1 = sb("ones1", [1, 128])
        sel4 = sb("sel4", [4, 4 * 128])
        gn_bc = sb("gn_bc", [128, DK])
        modv = sb("modv", [128, 48 * n_seq])
        ca1 = sb("ca1", [128, NDC * n_seq])
        cb1 = sb("cb1", [128, NDC * n_seq])
        stats = sb("stats", [128, 24])
        ar_t = sb("ar_t", [1, 1])
        gfk = sb("gfk", [1, 2 * n_seq])
        junkD = sb("junkD", [128, D])
        silc = junkD[:, 32:32 + NDC * n_seq]
        bcg = sb("bcg", [128, 8])
        bcrow = sb("bcrow", [4, 2 * 128])
        gnrow = junkD[:1, 512:512 + DK]

        for apd, src in ((eye, EYE), (masku, MASKU), (maskl, MASKL), (maskui, MASKUI),
                         (ones1, ONES1), (sel4, SEL4)):
            p.dma(SY, apd[:], src[:, :])
        t_ar = p.dma(SY, ar_t[:], AR[:])
        t_gnr = p.dma(SY, gnrow, GN[:])
        t_consts = p.tok(SY)

        def rsqrt_newton(vcol, rcol, t0, t1):
            p.wait(SE, p.tok(VE))
            p.eng[SE].drain()
            i = p.eng[SE].activation(t0, vcol, AF.Ln)
            tr = p.inc(SE, i)
            p.eng[SE].drain()
            i = p.eng[SE].activation(rcol, t0, AF.Exp, scale=-0.5)
            ts_ = p.inc(SE, i)
            p.wait(VE, ts_)
            p.eng[VE].drain()
            p.eng[VE].tensor_tensor(t1, rcol, rcol, AL.mult)
            p.eng[VE].drain()
            p.eng[VE].tensor_tensor(t1, t1, vcol, AL.mult)
            p.eng[VE].drain()
            p.eng[VE].tensor_scalar(t1, t1, -0.5, 1.5, AL.mult, AL.add)
            p.eng[VE].drain()
            i = p.eng[VE].tensor_tensor(rcol, rcol, t1, AL.mult)
            return p.inc(VE, i)

        def ln_stats(xsl, deps, co=0, stt=None):
            st_ = stats if stt is None else stt
            sc = lambda a, bnd: st_[:, co + a:co + bnd]
            p.waits(SE, deps)
            p.eng[SE].drain()
            i = p.eng[SE].activation(junkD[:], xsl, AF.Square, accum_out=sc(1, 2))
            t_sq = p.inc(SE, i)
            p.waits(VE, deps)
            p.eng[VE].drain()
            i = p.eng[VE].tensor_reduce(sc(0, 1), xsl, AX.X, AL.add)
            p.inc(VE, i)
            p.wait(VE, t_sq)
            p.eng[VE].drain()
            p.eng[VE].tensor_scalar(sc(2, 3), sc(0, 1), -1.0 / D, None, AL.mult)
            p.eng[VE].drain()
            p.eng[VE].tensor_tensor(sc(3, 4), sc(2, 3), sc(2, 3), AL.mult)
            p.eng[VE].drain()
            p.eng[VE].tensor_scalar(sc(4, 5), sc(1, 2), 1.0 / D, None, AL.mult)
            p.eng[VE].drain()
            p.eng[VE].tensor_tensor(sc(4, 5), sc(4, 5), sc(3, 4), AL.subtract)
            p.eng[VE].drain()
            i = p.eng[VE].tensor_scalar(sc(4, 5), sc(4, 5), 1e-5, None, AL.add)
            p.inc(VE, i)
            return rsqrt_newton(sc(4, 5), sc(7, 8), sc(5, 6), sc(6, 7))

        def modcol(vidx, dc, b):
            c0 = (vidx * 8 + dc) * n_seq + b
            return modv[:, c0:c0 + 1]

        def split_pair(full_ap, hi_ap, lo_ap, deps):
            p.waits(SE, deps)
            p.eng[SE].drain()
            i = p.eng[SE].copy(hi_ap, full_ap)
            th = p.inc(SE, i)
            p.waits(GE, tuple(deps) + (th,))
            p.eng[GE].drain()
            i = p.eng[GE].tensor_tensor(lo_ap, full_ap, hi_ap.bitcast(F32), AL.subtract)
            return th, p.inc(GE, i)

        with ExitStack() as st0:
            mw = sb("mw", [128, 2 * D], stx=st0)
            pm = psum("pm", [128, 512], stx=st0)
            t_c = []
            for b in range(n_seq):
                for dc in range(NDC):
                    cix = dc * n_seq + b
                    t_c.append(p.dma(SY, silc[:, cix:cix + 1], C_in[b, dc * 128:(dc + 1) * 128]))
            p.waits(SE, tuple(t_c))
            p.eng[SE].drain()
            i = p.eng[SE].activation(silc[:], silc[:], AF.Silu)
            t_silc = p.inc(SE, i)

            t_buf = [None, None]
            t_evac = {}
            for mc in range(48):
                j = mc % 2
                ld = []
                for dc in range(NDC):
                    ld.append(p.dma(SY, mw[:, j * D + dc * 128: j * D + (dc + 1) * 128],
                                    WMOD[dc * 128:(dc + 1) * 128, mc * 128:(mc + 1) * 128],
                                    waits=(t_buf[j],) if t_buf[j] else ()))
                p.waits(TE, tuple(ld) + (t_silc,))
                slot = mc % 4
                if mc >= 4:
                    p.wait(TE, t_evac[mc - 4])
                psl = pm[:, slot * n_seq: slot * n_seq + n_seq]
                for dc in range(NDC):
                    ins = nc.tensor.matmul(psl, mw[:, j * D + dc * 128: j * D + (dc + 1) * 128],
                                           silc[:, dc * n_seq:(dc + 1) * n_seq],
                                           start=(dc == 0), stop=(dc == NDC - 1))
                t_buf[j] = p.inc(TE, ins)
                p.wait(VE, t_buf[j])
                p.eng[VE].drain()
                i = p.eng[VE].tensor_copy(modv[:, mc * n_seq:(mc + 1) * n_seq], psl)
                t_evac[mc] = p.inc(VE, i)
            p.barrier()

            def gate_fix(vidx):
                sl = slice(vidx * 8 * n_seq, (vidx + 1) * 8 * n_seq)
                g = modv[:, sl]
                sgn = junkD[:, 0:8 * n_seq]
                p.eng[VE].drain()
                p.eng[VE].tensor_scalar(sgn, g, 0.0, None, AL.is_ge)
                p.eng[VE].drain()
                p.eng[VE].tensor_scalar(sgn, sgn, 2.0, -1.0, AL.mult, AL.add)
                p.eng[VE].drain()
                p.eng[VE].tensor_tensor(g, g, sgn, AL.mult)
                p.eng[VE].drain()
                p.eng[VE].tensor_scalar(g, g, 0.25, None, AL.max)
                p.eng[VE].drain()
                i = p.eng[VE].tensor_tensor(g, g, sgn, AL.mult)
                return p.inc(VE, i)
            gate_fix(2)
            p.eng[VE].drain()
            gate_fix(5)
            p.eng[VE].drain()
            p.eng[VE].tensor_scalar(ca1[:], modv[:, 8 * n_seq:16 * n_seq], 1.0, None, AL.add)
            p.eng[VE].drain()
            i = p.eng[VE].tensor_scalar(cb1[:], modv[:, 32 * n_seq:40 * n_seq], 1.0, None, AL.add)
            t_mod = p.inc(VE, i)

            p.waits(TE, (t_gnr, t_consts, t_mod))
            ins = nc.tensor.matmul(pm[:, 0:DK], ones1[:], gnrow, start=True, stop=True)
            t_g = p.inc(TE, ins)
            p.wait(SE, t_g)
            p.eng[SE].drain()
            i = p.eng[SE].copy(gn_bc[:], pm[:, 0:DK])
            p.inc(SE, i)
            p.barrier()
        tap("modv", modv[:], [p.tok(VE)])
        tap("ca1", ca1[:], [p.tok(VE)])

        def build_gate_bc(vidx, dst, b, psbank, bk, bki):
            tt = []
            for dc in range(NDC):
                p.eng[VE].drain()
                i = p.eng[VE].tensor_copy(bcg[:, dc:dc + 1], modcol(vidx, dc, b))
                tt.append(p.inc(VE, i))
            t_row = None
            for g in range(2):
                bk.pre_write(bki, tuple(tt) + ((t_row,) if t_row else ()))
                ins = nc.tensor.transpose(psbank[:4, 0:128], bcg[:, g * 4:(g + 1) * 4],
                                          eye[:])
                t_tr = p.inc(TE, ins)
                bk.set_write(bki, t_tr)
                bk.read(bki, SE)
                p.eng[SE].drain()
                i = p.eng[SE].copy(bcrow[:, g * 128:(g + 1) * 128], psbank[:4, 0:128])
                t_row = p.inc(SE, i)
                bk.add_reader(bki, t_row)
            for dc in range(NDC):
                g, r = dc // 4, dc % 4
                bk.pre_write(bki, (t_row,))
                ins = nc.tensor.matmul(psbank[:, 128:256], sel4[:, r * 128:(r + 1) * 128],
                                       bcrow[:, g * 128:(g + 1) * 128],
                                       start=True, stop=True)
                t_b = p.inc(TE, ins)
                bk.set_write(bki, t_b)
                bk.read(bki, SE)
                p.eng[SE].drain()
                i = p.eng[SE].copy(dst[:, dc * 128:(dc + 1) * 128], psbank[:, 128:256])
                bk.add_reader(bki, p.inc(SE, i))
            return p.tok(SE)

        def pa1_qk():
            with ExitStack() as s1:
                wt = {}
                for nm in ("q", "k"):
                    wt[nm] = (sb("whi_" + nm, [128, NDC * D], F32R, stx=s1),
                              sb("whib_" + nm, [128, NDC * D], BF16, stx=s1),
                              sb("wlob_" + nm, [128, NDC * D], BF16, stx=s1))
                xt = sb("xt_qk", [128, 3 * D], stx=s1)
                h1f = sb("h1f_qk", [128, 2 * D], stx=s1)
                h1hi = sb("h1hi_qk", [128, 2 * D], F32R, stx=s1)
                h1hib = sb("h1hib_qk", [128, 2 * D], BF16, stx=s1)
                h1lob = sb("h1lob_qk", [128, 2 * D], BF16, stx=s1)
                o_t = sb("ot_qk", [128, 2 * D], stx=s1)
                stats2 = sb("stats2_qk", [128, 8], stx=s1)
                wst = junkD[:]
                bank_ps = [psum(f"bkqk{i}", [128, 512], stx=s1) for i in range(4)]
                bk = Banks(p, bank_ps)

                t_wsplit = {"q": [], "k": []}
                prev = None
                for nm, W in (("q", WQ), ("k", WK)):
                    w_hi, w_hib, w_lob = wt[nm]
                    for dc in range(NDC):
                        dsl = slice(dc * D, (dc + 1) * D)
                        tl = p.dma(SY, wst, W[dc * 128:(dc + 1) * 128, :],
                                   waits=(prev,) if prev else ())
                        p.wait(SE, tl)
                        p.eng[SE].drain()
                        i = p.eng[SE].copy(w_hi[:, dsl], wst)
                        th = p.inc(SE, i)
                        p.waits(GE, (tl, th))
                        p.eng[GE].drain()
                        i = p.eng[GE].tensor_tensor(w_lob[:, dsl], wst,
                                                    w_hi[:, dsl].bitcast(F32), AL.subtract)
                        tlo = p.inc(GE, i)
                        p.wait(VE, th)
                        p.eng[VE].drain()
                        i = p.eng[VE].tensor_copy(w_hib[:, dsl], w_hi[:, dsl].bitcast(F32))
                        thb = p.inc(VE, i)
                        prev = tlo
                        t_wsplit[nm].append((tlo, thb))
                p.barrier()

                cons = [(), ()]
                cons3 = [(), (), ()]
                l2prev = None
                chunks = [(b, ci) for b in range(n_seq) for ci in range(NCH)]
                lnst = lambda ii: stats if ii % 2 == 0 else stats2
                t_lnp = {}
                tx_tok = {}
                xpar = lambda ii: xt[:, (ii % 3) * D:(ii % 3) * D + D]
                for pre in range(min(2, len(chunks))):
                    bp, cp = chunks[pre]
                    tx_tok[pre] = p.dma(SY, xpar(pre), X[bp, cp * CH:(cp + 1) * CH, :],
                                        slot=pre % 2)
                p.wait(SE, p.tok(VE))
                t_lnp[0] = ln_stats(xpar(0), (tx_tok[0],), stt=lnst(0))
                for idx, (b, ci) in enumerate(chunks):
                        t0 = ci * CH
                        jx = ci % 2
                        xsl = xpar(idx)
                        hsl_f = h1f[:, jx * D:(jx + 1) * D]
                        hsl_hi = h1hi[:, jx * D:(jx + 1) * D]
                        hsl_hib = h1hib[:, jx * D:(jx + 1) * D]
                        hsl_lob = h1lob[:, jx * D:(jx + 1) * D]
                        osl = o_t[:, jx * D:(jx + 1) * D]
                        gate = cons[jx]
                        stj = lnst(idx)
                        p.wait(VE, t_lnp[idx])
                        p.eng[VE].drain()
                        i = p.eng[VE].tensor_scalar(xsl, xsl, stj[:, 2:3], stj[:, 7:8],
                                                    AL.add, AL.mult)
                        t_xn = p.inc(VE, i)
                        t_h1c = []
                        for dc in range(NDC):
                            bki = dc % 2
                            off = ((dc // 2) % 2) * 128
                            bk.pre_write(bki, (t_xn,))
                            ins = nc.tensor.transpose(
                                bank_ps[bki][:, off:off + 128],
                                xsl[:, dc * 128:(dc + 1) * 128], eye[:])
                            bk.set_write(bki, p.inc(TE, ins))
                            bk.read(bki, VE)
                            p.waits(VE, gate)
                            p.eng[VE].drain()
                            i = p.eng[VE].tensor_scalar(
                                hsl_f[:, dc * 128:(dc + 1) * 128],
                                bank_ps[bki][:, off:off + 128],
                                ca1[:, dc * n_seq + b: dc * n_seq + b + 1],
                                modcol(0, dc, b), AL.mult, AL.add)
                            tt = p.inc(VE, i)
                            bk.add_reader(bki, tt)
                            t_h1c.append(tt)
                        if idx + 2 < len(chunks):
                            b2, c2 = chunks[idx + 2]
                            tx_tok[idx + 2] = p.dma(
                                SY, xpar(idx + 2), X[b2, c2 * CH:(c2 + 1) * CH, :],
                                waits=cons3[(idx + 2) % 3], slot=(idx + 2) % 2)
                        if idx + 1 < len(chunks):
                            p.wait(SE, p.tok(VE))
                            t_lnp[idx + 1] = ln_stats(xpar(idx + 1), (tx_tok[idx + 1],),
                                                      stt=lnst(idx + 1))
                        for dc in range(NDC):
                            p.dma(GE, H1S[b, ci, dc * 128:(dc + 1) * 128, :],
                                  hsl_f[:, dc * 128:(dc + 1) * 128],
                                  waits=(t_h1c[dc],), slot=4 + jx)
                        deps_h = tuple(t_h1c) + gate
                        p.waits(SE, deps_h)
                        p.eng[SE].drain()
                        i = p.eng[SE].copy(hsl_hi, hsl_f)
                        th = p.inc(SE, i)
                        p.waits(GE, deps_h + (th,))
                        p.eng[GE].drain()
                        i = p.eng[GE].tensor_tensor(hsl_lob, hsl_f,
                                                    hsl_hi.bitcast(F32), AL.subtract)
                        tlo = p.inc(GE, i)
                        p.wait(VE, th)
                        p.eng[VE].drain()
                        i = p.eng[VE].tensor_copy(hsl_hib, hsl_hi.bitcast(F32))
                        thb = p.inc(VE, i)
                        t_hs = (th, tlo, thb)

                        t_st = None
                        for nm, bko, DST in (("q", 2, QS), ("k", 0, KS)):
                            w_hi, w_hib, w_lob = wt[nm]
                            for half in range(2):
                                bki = bko + half
                                bk.pre_write(bki, t_hs)
                                for dc in range(NDC):
                                    hslc = slice(jx * D + dc * 128, jx * D + (dc + 1) * 128)
                                    wsl = slice(dc * D + half * 512, dc * D + (half + 1) * 512)
                                    p.waits(TE, t_wsplit[nm][dc])
                                    nc.tensor.matmul(bank_ps[bki][:], h1hi[:, hslc],
                                                     w_hi[:, wsl], start=(dc == 0), stop=False)
                                    nc.tensor.matmul(bank_ps[bki][:], h1hib[:, hslc],
                                                     w_lob[:, wsl], start=False, stop=False)
                                    ins = nc.tensor.matmul(bank_ps[bki][:], h1lob[:, hslc],
                                                           w_hib[:, wsl],
                                                           start=False, stop=(dc == NDC - 1))
                                t_mm = p.inc(TE, ins)
                                bk.set_write(bki, t_mm)
                            sq_toks = []
                            for half in range(2):
                                bki = bko + half
                                for hs_ in range(2):
                                    hh = half * 2 + hs_
                                    psl = bank_ps[bki][:, hs_ * 256:(hs_ + 1) * 256]
                                    bk.read(bki, SE)
                                    p.wait(SE, l2prev)
                                    p.eng[SE].drain()
                                    i = p.eng[SE].activation(junkD[:, 0:256], psl, AF.Square,
                                                             accum_out=stats[:, 8 + hh:9 + hh])
                                    t_ss = p.inc(SE, i)
                                    bk.add_reader(bki, t_ss)
                                    sq_toks.append(t_ss)
                            p.waits(VE, tuple(sq_toks))
                            rsqrt_newton(stats[:, 8:12], stats[:, 12:16],
                                         stats[:, 16:20], stats[:, 20:24])
                            for half in range(2):
                                bki = bko + half
                                for hs_ in range(2):
                                    hh = half * 2 + hs_
                                    psl = bank_ps[bki][:, hs_ * 256:(hs_ + 1) * 256]
                                    bk.read(bki, VE)
                                    p.waits(VE, gate + ((t_st,) if t_st else ()))
                                    p.eng[VE].drain()
                                    i = p.eng[VE].tensor_scalar(
                                        osl[:, hh * 256:(hh + 1) * 256], psl,
                                        stats[:, 12 + hh:13 + hh], None, AL.mult)
                                    l2prev = p.inc(VE, i)
                                    bk.add_reader(bki, l2prev)
                            t_st = p.dma(GE, DST[b, t0:t0 + CH, :], osl, waits=(l2prev,),
                                         slot=4 + jx)
                        cons[jx] = (p.tok(TE), p.tok(VE), p.tok(SE), t_st)
                        cons3[idx % 3] = (p.tok(TE), p.tok(VE), p.tok(SE))
                        if b == 0 and ci == 0:
                            tap("h1_c0", hsl_f, [p.tok(VE), p.tok(GE), p.tok(SE)])
                p.barrier()

        def pa1_v():
            with ExitStack() as s1:
                wv_t = sb("wvt", [128, NDC * D], stx=s1)
                wb_t = sb("wbt", [128, NDC * H], stx=s1)
                beta_t = sb("betat", [128, 2 * H], stx=s1)
                h1f = sb("h1f_v", [128, 2 * D], stx=s1)
                o_t = sb("ot_v", [128, 2 * D], stx=s1)
                bank_ps = [psum(f"bkv{i}", [128, 512], stx=s1) for i in range(4)]
                bk = Banks(p, bank_ps)
                t_wv = [p.dma(SY, wv_t[:, dc * D:(dc + 1) * D],
                              WV[dc * 128:(dc + 1) * 128, :]) for dc in range(NDC)]
                t_wb = [p.dma(SY, wb_t[:, dc * H:(dc + 1) * H],
                              WB[dc * 128:(dc + 1) * 128, :]) for dc in range(NDC)]
                p.barrier()
                cons = [(), ()]
                for b in range(n_seq):
                    for ci in range(NCH):
                        t0 = ci * CH
                        jx = ci % 2
                        hsl_f = h1f[:, jx * D:(jx + 1) * D]
                        osl = o_t[:, jx * D:(jx + 1) * D]
                        gate = cons[jx]
                        t_h1c = []
                        for dc in range(NDC):
                            t_h1c.append(p.dma(
                                SY, hsl_f[:, dc * 128:(dc + 1) * 128],
                                H1S[b, ci, dc * 128:(dc + 1) * 128, :],
                                waits=gate if dc == 0 else (), slot=jx))
                        for half in range(2):
                            bki = 2 + half
                            bk.pre_write(bki, tuple(t_h1c))
                            for dc in range(NDC):
                                p.wait(TE, t_wv[dc])
                                ins = nc.tensor.matmul(
                                    bank_ps[bki][:], hsl_f[:, dc * 128:(dc + 1) * 128],
                                    wv_t[:, dc * D + half * 512: dc * D + (half + 1) * 512],
                                    start=(dc == 0), stop=(dc == NDC - 1))
                            t_vm = p.inc(TE, ins)
                            bk.set_write(bki, t_vm)
                            bk.read(bki, SE)
                            p.waits(SE, gate)
                            p.eng[SE].drain()
                            i = p.eng[SE].copy(osl[:, half * 512:(half + 1) * 512],
                                               bank_ps[bki][:])
                            bk.add_reader(bki, p.inc(SE, i))
                        bk.pre_write(0, tuple(t_h1c))
                        for dc in range(NDC):
                            p.wait(TE, t_wb[dc])
                            ins = nc.tensor.matmul(
                                bank_ps[0][:, 0:H], hsl_f[:, dc * 128:(dc + 1) * 128],
                                wb_t[:, dc * H:(dc + 1) * H],
                                start=(dc == 0), stop=(dc == NDC - 1))
                        t_bm = p.inc(TE, ins)
                        bk.set_write(0, t_bm)
                        bk.read(0, SE)
                        p.waits(SE, gate)
                        p.eng[SE].drain()
                        i = p.eng[SE].activation(beta_t[:, jx * H:(jx + 1) * H],
                                                 bank_ps[0][:, 0:H], AF.Sigmoid)
                        t_bb = p.inc(SE, i)
                        bk.add_reader(0, t_bb)
                        p.dma(GE, VS[b, t0:t0 + CH, :], osl, waits=(p.tok(SE),),
                              slot=4 + jx)
                        t_st = p.dma(GE, BSc[b, t0:t0 + CH, :],
                                     beta_t[:, jx * H:(jx + 1) * H],
                                     waits=(t_bb,), slot=4 + jx)
                        cons[jx] = (p.tok(TE), p.tok(SE), t_st)
                p.barrier()

        pa1_qk()
        pa1_v()

        with ExitStack() as s2:
            wst = sb("wst_o", [128, D], stx=s2)
            s_state = sb("s_state", [128, H * 2 * DK], stx=s2)
            ga_bc = sb("ga_bc", [128, D], stx=s2)
            wo_hi = sb("wo_hi", [128, NDC * D], F32R, stx=s2)
            wo_hib = sb("wo_hib", [128, NDC * D], BF16, stx=s2)
            wo_lob = sb("wo_lob", [128, NDC * D], BF16, stx=s2)
            qt = sb("qt", [128, 2 * D], stx=s2)
            kt = sb("kt", [128, 2 * D], stx=s2)
            vt = sb("vt", [128, 2 * D], stx=s2)
            xt2 = sb("xt2", [128, 2 * D], stx=s2)
            beta_t = sb("beta_t2", [128, 2 * H], stx=s2)
            negb = sb("negb", [128, 2 * H], stx=s2)
            qf = sb("qf", [128, 2 * D], stx=s2)
            kf = sb("kf", [128, 2 * D], stx=s2)
            brow = sb("brow", [4, 2 * 128], stx=s2)
            BPOW = ["c1b", "c1tb", "cA", "cAt", "cB", "cBt", "c64t"]
            cb = {n: sb("pw_" + n, [128, 4 * 128], BF16, stx=s2) for n in BPOW}
            pbA = sb("pw_pbA", [128, 4 * 128], BF16, stx=s2)
            pbB = sb("pw_pbB", [128, 4 * 128], BF16, stx=s2)
            c1f = sb("c1f", [128, 4 * 128], stx=s2)
            c1tf = sb("c1tf", [128, 4 * 128], stx=s2)
            m_t = sb("m_t", [128, 4 * 128], stx=s2)
            pfA = sb("pfA", [128, 4 * 128], stx=s2)
            pfB = sb("pfB", [128, 4 * 128], stx=s2)
            ptfA = sb("ptfA", [128, 4 * 128], stx=s2)
            ptfB = sb("ptfB", [128, 4 * 128], stx=s2)
            w_t = sb("w_t", [128, 4 * 128], stx=s2)
            gm_t = sb("gm_t", [128, 2 * 4 * 128], stx=s2)
            pf_fin = sb("pf_fin", [128, 2 * 4 * 128], stx=s2)
            bmu = sb("bmu", [128, 4 * 128], stx=s2)
            r_t = sb("r_t", [128, 4 * 256], stx=s2)
            u_t = sb("u_t", [128, 4 * 256], stx=s2)
            eyeb = sb("eyeb", [128, 128], BF16, stx=s2)
            eye2 = sb("eye2", [128, 128], stx=s2)
            on_t = sb("on_t", [128, 4 * 256], stx=s2)
            onf_hi = sb("onf_hi", [128, D], F32R, stx=s2)
            onf_hib = sb("onf_hib", [128, D], BF16, stx=s2)
            onf_lob = sb("onf_lob", [128, D], BF16, stx=s2)
            bank_ps = [psum(f"d{i}", [128, 512], stx=s2) for i in range(8)]
            bk = Banks(p, bank_ps)

            t_wsplit = []
            prev = None
            for dc in range(NDC):
                dsl = slice(dc * D, (dc + 1) * D)
                tl = p.dma(SY, wst[:], WO[dc * 128:(dc + 1) * 128, :],
                           waits=(prev,) if prev else ())
                p.wait(SE, tl)
                p.eng[SE].drain()
                i = p.eng[SE].copy(wo_hi[:, dsl], wst[:])
                th = p.inc(SE, i)
                p.waits(GE, (tl, th))
                p.eng[GE].drain()
                i = p.eng[GE].tensor_tensor(wo_lob[:, dsl], wst[:],
                                            wo_hi[:, dsl].bitcast(F32), AL.subtract)
                tlo = p.inc(GE, i)
                p.wait(VE, th)
                p.eng[VE].drain()
                i = p.eng[VE].tensor_copy(wo_hib[:, dsl], wo_hi[:, dsl].bitcast(F32))
                thb = p.inc(VE, i)
                prev = tlo
                t_wsplit.append((tlo, thb))
            p.waits(GE, (t_consts,))
            p.eng[GE].drain()
            p.eng[GE].tensor_copy(eyeb[:], eye[:])
            p.eng[GE].drain()
            i = p.eng[GE].tensor_scalar(eye2[:], eye[:], 2.0, None, AL.mult)
            p.inc(GE, i)
            p.barrier()

            cons = [(), ()]
            rms_prev = None
            pend = {}

            BNK = [0, 1, 6, 7]
            A = lambda h: 2 + h
            s1 = lambda h, a, bnd: bank_ps[A(h)][:, a:bnd]
            sB = lambda h, a, bnd: bank_ps[BNK[h]][:, a:bnd]
            h128 = lambda t, h: t[:, h * 128:(h + 1) * 128]
            h256 = lambda t, h: t[:, h * 256:(h + 1) * 256]
            s0_ = lambda h: s_state[:, h * 512:(h + 1) * 512]

            def chain_part(b, ci):
                t0 = ci * CH
                jx = ci % 2
                oD = jx * D
                gate = cons[jx]
                tq = p.dma(SY, qt[:, oD:oD + D], QS[b, t0:t0 + CH, :], waits=gate, slot=jx)
                tk = p.dma(SY, kt[:, oD:oD + D], KS[b, t0:t0 + CH, :], slot=jx)
                tv = p.dma(SY, vt[:, oD:oD + D], VS[b, t0:t0 + CH, :], slot=jx)
                tb = p.dma(SY, beta_t[:, jx * H:(jx + 1) * H], BSc[b, t0:t0 + CH, :], slot=jx)
                tx = p.dma(SY, xt2[:, oD:oD + D], X[b, t0:t0 + CH, :], slot=jx)
                betac = beta_t[:, jx * H:(jx + 1) * H]
                negc = negb[:, jx * H:(jx + 1) * H]
                browc = brow[:, jx * 128:(jx + 1) * 128]
                gmc = gm_t[:, jx * 512:(jx + 1) * 512]
                pfc = pf_fin[:, jx * 512:(jx + 1) * 512]
                kfh = lambda h, half: kf[:, oD + h * 256 + half * 128:
                                         oD + h * 256 + (half + 1) * 128]
                qfh = lambda h, half: qf[:, oD + h * 256 + half * 128:
                                         oD + h * 256 + (half + 1) * 128]
                yield
                for srct, dst, tok, base in ((kt, kf, tk, 0), (qt, qf, tq, 256)):
                    for dc in range(NDC):
                        h, half = dc // 2, dc % 2
                        bki = BNK[h]
                        off = base + half * 128
                        bk.pre_write(bki, (tok,))
                        ins = nc.tensor.transpose(
                            bank_ps[bki][:, off:off + 128],
                            srct[:, oD + dc * 128:oD + (dc + 1) * 128], eye[:])
                        bk.set_write(bki, p.inc(TE, ins))
                        bk.read(bki, SE)
                        p.eng[SE].drain()
                        i = p.eng[SE].copy(dst[:, oD + dc * 128:oD + (dc + 1) * 128],
                                           bank_ps[bki][:, off:off + 128])
                        bk.add_reader(bki, p.inc(SE, i))
                    yield
                bk.pre_write(0, (tb,))
                ins = nc.tensor.transpose(bank_ps[0][:4, 0:128], betac, eye[:])
                bk.set_write(0, p.inc(TE, ins))
                bk.read(0, SE)
                p.eng[SE].drain()
                i = p.eng[SE].copy(browc, bank_ps[0][:4, 0:128])
                t_brow = p.inc(SE, i)
                bk.add_reader(0, t_brow)
                t_tr = p.tok(SE)
                p.waits(VE, (tb, tv))
                p.eng[VE].drain()
                i = p.eng[VE].tensor_scalar(negc, betac, -1.0, None, AL.mult)
                t_nb = p.inc(VE, i)
                for h in range(H):
                    p.eng[VE].drain()
                    i = p.eng[VE].tensor_scalar(vt[:, oD + h * 256:oD + (h + 1) * 256],
                                                vt[:, oD + h * 256:oD + (h + 1) * 256],
                                                betac[:, h:h + 1], None, AL.mult)
                t_vb = p.inc(VE, i)
                yield
                for h in range(H):
                    bk.pre_write(BNK[h], (t_brow,))
                    ins = nc.tensor.matmul(sB(h, 0, 128), sel4[:, h * 128:(h + 1) * 128],
                                           browc, start=True, stop=True)
                    bk.set_write(BNK[h], p.inc(TE, ins))
                for h in range(H):
                    bk.read(BNK[h], VE)
                    p.eng[VE].drain()
                    i = p.eng[VE].scalar_tensor_tensor(h128(bmu, h), sB(h, 0, 128), -1.0,
                                                       masku[:], AL.mult, AL.mult)
                    bk.add_reader(BNK[h], p.inc(VE, i))
                yield
                for h in range(H):
                    bk.pre_write(BNK[h], (t_tr,))
                    nc.tensor.matmul(sB(h, 128, 256), kfh(h, 0), kfh(h, 0),
                                     start=True, stop=False)
                    ins = nc.tensor.matmul(sB(h, 128, 256), kfh(h, 1), kfh(h, 1),
                                           start=False, stop=True)
                    bk.set_write(BNK[h], p.inc(TE, ins))
                t_c1 = {}
                t_c1t = {}
                for h in range(H):
                    bk.read(BNK[h], VE)
                    p.eng[VE].drain()
                    i = p.eng[VE].tensor_tensor(h128(c1f, h), sB(h, 128, 256),
                                                h128(bmu, h), AL.mult)
                    t_c1[h] = p.inc(VE, i)
                    bk.add_reader(BNK[h], t_c1[h])
                    p.wait(VE, t_nb)
                    p.eng[VE].drain()
                    i = p.eng[VE].scalar_tensor_tensor(h128(c1tf, h), sB(h, 128, 256),
                                                       negc[:, h:h + 1], maskl[:],
                                                       AL.mult, AL.mult)
                    t_c1t[h] = p.inc(VE, i)
                    bk.add_reader(BNK[h], t_c1t[h])
                yield
                tok_bf = {}
                t_m = {}
                for h in range(H):
                    p.waits(GE, (t_c1[h], t_c1t[h]))
                    p.eng[GE].drain()
                    i = p.eng[GE].tensor_copy(h128(cb["c1b"], h), h128(c1f, h))
                    tok_bf[("c1b", h)] = p.inc(GE, i)
                    p.eng[GE].drain()
                    i = p.eng[GE].tensor_copy(h128(cb["c1tb"], h), h128(c1tf, h))
                    tok_bf[("c1tb", h)] = p.inc(GE, i)
                    p.eng[GE].drain()
                    i = p.eng[GE].tensor_tensor(h128(m_t, h), eye[:], h128(c1tf, h),
                                                AL.subtract)
                    t_m[h] = p.inc(GE, i)
                t_gm = {}
                for h in range(H):
                    bk.pre_write(BNK[h], ())
                    nc.tensor.matmul(sB(h, 256, 384), kfh(h, 0), qfh(h, 0),
                                     start=True, stop=False)
                    ins = nc.tensor.matmul(sB(h, 256, 384), kfh(h, 1), qfh(h, 1),
                                           start=False, stop=True)
                    bk.set_write(BNK[h], p.inc(TE, ins))
                for h in range(H):
                    bk.read(BNK[h], VE)
                    p.eng[VE].drain()
                    i = p.eng[VE].tensor_tensor(h128(gmc, h), sB(h, 256, 384), maskui[:],
                                                AL.mult)
                    t_gm[h] = p.inc(VE, i)
                    bk.add_reader(BNK[h], t_gm[h])
                t_pb = {}
                for h in range(H):
                    p.waits(GE, (tok_bf[("c1b", h)], t_consts))
                    p.eng[GE].drain()
                    i = p.eng[GE].tensor_tensor(h128(pbA, h), h128(cb["c1b"], h), eyeb[:],
                                                AL.add)
                    t_pb[h] = p.inc(GE, i)
                yield
                curb, nxtb = pbA, pbB
                cur_pair, nxt_pair = ("c1b", "c1tb"), ("cA", "cAt")
                for lvl in range(5):
                    pl, pt = cur_pair
                    nl, nt = nxt_pair
                    for h in range(H):
                        bk.pre_write(BNK[h], (tok_bf[(pl, h)], tok_bf[(pt, h)]))
                        ins = nc.tensor.matmul(sB(h, 384, 512), h128(cb[pt], h),
                                               h128(cb[pl], h), start=True, stop=True)
                        bk.set_write(BNK[h], p.inc(TE, ins))
                    for h in range(H):
                        eng = SE if h % 2 == 0 else VE
                        bk.read(BNK[h], eng)
                        p.eng[eng].drain()
                        if eng == SE:
                            i = p.eng[eng].copy(h128(cb[nl], h), sB(h, 384, 512))
                        else:
                            i = p.eng[eng].tensor_copy(h128(cb[nl], h), sB(h, 384, 512))
                        tok_bf[(nl, h)] = p.inc(eng, i)
                        bk.add_reader(BNK[h], tok_bf[(nl, h)])
                    yield
                    for h in range(H):
                        bk.pre_write(BNK[h], (tok_bf[(pl, h)], tok_bf[(pt, h)]))
                        ins = nc.tensor.matmul(sB(h, 384, 512), h128(cb[pl], h),
                                               h128(cb[pt], h), start=True, stop=True)
                        bk.set_write(BNK[h], p.inc(TE, ins))
                    for h in range(H):
                        eng = SE if h % 2 == 1 else VE
                        bk.read(BNK[h], eng)
                        p.eng[eng].drain()
                        if eng == SE:
                            i = p.eng[eng].copy(h128(cb[nt], h), sB(h, 384, 512))
                        else:
                            i = p.eng[eng].tensor_copy(h128(cb[nt], h), sB(h, 384, 512))
                        tok_bf[(nt, h)] = p.inc(eng, i)
                        bk.add_reader(BNK[h], tok_bf[(nt, h)])
                    yield
                    for h in range(H):
                        bk.pre_write(BNK[h], (t_pb[h], tok_bf[(nt, h)]))
                        ins = nc.tensor.matmul(sB(h, 384, 512), h128(cb[nt], h),
                                               h128(curb, h), start=True, stop=True)
                        bk.set_write(BNK[h], p.inc(TE, ins))
                    for h in range(H):
                        bk.read(BNK[h], VE)
                        p.eng[VE].drain()
                        i = p.eng[VE].tensor_tensor(h128(nxtb, h), sB(h, 384, 512),
                                                    h128(curb, h), AL.add)
                        t_pb[h] = p.inc(VE, i)
                        bk.add_reader(BNK[h], t_pb[h])
                    yield
                    curb, nxtb = nxtb, curb
                    cur_pair, nxt_pair = nxt_pair, (("cB", "cBt") if nxt_pair[0] == "cA"
                                                    else ("cA", "cAt"))
                for h in range(H):
                    pl, pt = cur_pair
                    bk.pre_write(BNK[h], (tok_bf[(pl, h)], tok_bf[(pt, h)]))
                    ins = nc.tensor.matmul(sB(h, 384, 512), h128(cb[pl], h),
                                           h128(cb[pt], h), start=True, stop=True)
                    bk.set_write(BNK[h], p.inc(TE, ins))
                for h in range(H):
                    bk.read(BNK[h], SE)
                    p.eng[SE].drain()
                    i = p.eng[SE].copy(h128(cb["c64t"], h), sB(h, 384, 512))
                    tok_bf[("c64t", h)] = p.inc(SE, i)
                    bk.add_reader(BNK[h], tok_bf[("c64t", h)])
                yield
                t_pf = {}
                t_ptf = {}
                for h in range(H):
                    bk.pre_write(BNK[h], (t_pb[h], tok_bf[("c64t", h)]))
                    ins = nc.tensor.matmul(sB(h, 384, 512), h128(cb["c64t"], h),
                                           h128(curb, h), start=True, stop=True)
                    bk.set_write(BNK[h], p.inc(TE, ins))
                for h in range(H):
                    bk.read(BNK[h], VE)
                    p.eng[VE].drain()
                    i = p.eng[VE].tensor_tensor(h128(pfA, h), sB(h, 384, 512),
                                                h128(curb, h), AL.add)
                    t_pf[h] = p.inc(VE, i)
                    bk.add_reader(BNK[h], t_pf[h])
                yield
                for h in range(H):
                    bk.pre_write(BNK[h], ())
                    nc.tensor.matmul(sB(h, 384, 512), h128(curb, h), h128(cb["c64t"], h),
                                     start=True, stop=False)
                    ins = nc.tensor.matmul(sB(h, 384, 512), h128(curb, h), eyeb[:],
                                           start=False, stop=True)
                    bk.set_write(BNK[h], p.inc(TE, ins))
                for h in range(H):
                    bk.read(BNK[h], SE)
                    p.eng[SE].drain()
                    i = p.eng[SE].copy(h128(ptfA, h), sB(h, 384, 512))
                    t_ptf[h] = p.inc(SE, i)
                    bk.add_reader(BNK[h], t_ptf[h])
                yield
                pf, pf2 = pfA, pfB
                ptf, ptf2 = ptfA, ptfB
                for it in range(2):
                    t_w = {}
                    for h in range(H):
                        bk.pre_write(BNK[h], (t_pf[h], t_m[h]))
                        ins = nc.tensor.matmul(sB(h, 384, 512), h128(m_t, h), h128(pf, h),
                                               start=True, stop=True)
                        bk.set_write(BNK[h], p.inc(TE, ins))
                    for h in range(H):
                        bk.read(BNK[h], VE)
                        p.eng[VE].drain()
                        i = p.eng[VE].scalar_tensor_tensor(h128(w_t, h), sB(h, 384, 512),
                                                           -1.0, eye2[:], AL.mult, AL.add)
                        t_w[h] = p.inc(VE, i)
                        bk.add_reader(BNK[h], t_w[h])
                    yield
                    for h in range(H):
                        bk.pre_write(BNK[h], (t_w[h], t_ptf[h]))
                        ins = nc.tensor.matmul(sB(h, 384, 512), h128(ptf, h), h128(w_t, h),
                                               start=True, stop=True)
                        bk.set_write(BNK[h], p.inc(TE, ins))
                    for h in range(H):
                        bk.read(BNK[h], SE)
                        p.eng[SE].drain()
                        dstp = h128(pfc, h) if it == 1 else h128(pf2, h)
                        i = p.eng[SE].copy(dstp, sB(h, 384, 512))
                        t_pf[h] = p.inc(SE, i)
                        bk.add_reader(BNK[h], t_pf[h])
                    yield
                    if it == 0:
                        for h in range(H):
                            bk.pre_write(BNK[h], ())
                            ins = nc.tensor.matmul(sB(h, 384, 512), h128(w_t, h),
                                                   h128(ptf, h), start=True, stop=True)
                            bk.set_write(BNK[h], p.inc(TE, ins))
                        for h in range(H):
                            bk.read(BNK[h], SE)
                            p.eng[SE].drain()
                            i = p.eng[SE].copy(h128(ptf2, h), sB(h, 384, 512))
                            t_ptf[h] = p.inc(SE, i)
                            bk.add_reader(BNK[h], t_ptf[h])
                        yield
                        ptf, ptf2 = ptf2, ptf
                    pf, pf2 = pf2, pf
                if b == 0 and ci == 0:
                    tap("c1_h0", c1f[:, 0:128], [p.tok(VE)])
                    tap("P_h0", pfc[:, 0:128], [p.tok(SE)])
                pend[(b, ci)] = dict(t_pf=dict(t_pf), t_gm=dict(t_gm), t_vb=t_vb,
                                     tx=tx, gmc=gmc, pfc=pfc, negc=negc,
                                     kfh=kfh, qfh=qfh, oD=oD, jx=jx)

            def state_part(b, ci):
                nonlocal rms_prev
                st = pend.pop((b, ci))
                t0 = ci * CH
                jx, oD = st["jx"], st["oD"]
                kfh, qfh = st["kfh"], st["qfh"]
                gmc, pfc, negc = st["gmc"], st["pfc"], st["negc"]
                t_pf, t_gm, t_vb, tx = st["t_pf"], st["t_gm"], st["t_vb"], st["tx"]
                t_sq_heads = []
                t_r = {}
                t_u = {}
                for h in range(H):
                    bk.pre_write(A(h), (t_vb,))
                    nc.tensor.matmul(s1(h, 0, 256), kfh(h, 0), s0_(h)[:, 0:256],
                                     start=True, stop=False)
                    ins = nc.tensor.matmul(s1(h, 0, 256), kfh(h, 1), s0_(h)[:, 256:512],
                                           start=False, stop=True)
                    bk.set_write(A(h), p.inc(TE, ins))
                    bk.read(A(h), VE)
                    p.eng[VE].drain()
                    i = p.eng[VE].scalar_tensor_tensor(h256(r_t, h), s1(h, 0, 256),
                                                       negc[:, h:h + 1],
                                                       vt[:, oD + h * 256:oD + (h + 1) * 256],
                                                       AL.mult, AL.add)
                    t_r[h] = p.inc(VE, i)
                    bk.add_reader(A(h), t_r[h])
                    yield
                for h in range(H):
                    bk.pre_write(A(h), (t_pf[h], t_r[h]))
                    ins = nc.tensor.matmul(s1(h, 256, 512), h128(pfc, h), h256(r_t, h),
                                           start=True, stop=True)
                    bk.set_write(A(h), p.inc(TE, ins))
                    bk.read(A(h), SE)
                    p.eng[SE].drain()
                    i = p.eng[SE].copy(h256(u_t, h), s1(h, 256, 512))
                    t_u[h] = p.inc(SE, i)
                    bk.add_reader(A(h), t_u[h])
                    yield
                for h in range(H):
                    bk.pre_write(A(h), (t_u[h], t_gm[h]))
                    nc.tensor.matmul(s1(h, 0, 256), qfh(h, 0), s0_(h)[:, 0:256],
                                     start=True, stop=False)
                    nc.tensor.matmul(s1(h, 0, 256), qfh(h, 1), s0_(h)[:, 256:512],
                                     start=False, stop=False)
                    ins = nc.tensor.matmul(s1(h, 0, 256), h128(gmc, h), h256(u_t, h),
                                           start=False, stop=True)
                    bk.set_write(A(h), p.inc(TE, ins))
                    bk.read(A(h), SE)
                    p.wait(SE, rms_prev)
                    p.eng[SE].drain()
                    i = p.eng[SE].copy(h256(on_t, h), s1(h, 0, 256))
                    bk.add_reader(A(h), p.inc(SE, i))
                    p.eng[SE].drain()
                    i = p.eng[SE].activation(junkD[:, 0:256], s1(h, 0, 256), AF.Square,
                                             accum_out=stats[:, 8 + h:9 + h])
                    t_ss = p.inc(SE, i)
                    bk.add_reader(A(h), t_ss)
                    t_sq_heads.append(t_ss)
                    yield
                for h in range(H):
                    bk.pre_write(A(h), (t_u[h],))
                    for half in range(2):
                        ins = nc.tensor.matmul(
                            bank_ps[A(h)][:, half * 256:(half + 1) * 256],
                            kt[:, oD + h * 256 + half * 128: oD + h * 256 + half * 128 + 128],
                            h256(u_t, h), start=True, stop=True)
                    bk.set_write(A(h), p.inc(TE, ins))
                    bk.read(A(h), VE)
                    p.eng[VE].drain()
                    i = p.eng[VE].tensor_tensor(s0_(h)[:], s0_(h)[:],
                                                bank_ps[A(h)][:, 0:512], AL.add)
                    bk.add_reader(A(h), p.inc(VE, i))
                    yield
                if b == 0 and ci == 0:
                    tap("u_h0", u_t[:, 0:256], [p.tok(SE)])
                p.waits(VE, tuple(t_sq_heads))
                p.eng[VE].drain()
                i = p.eng[VE].tensor_scalar(stats[:, 8:12], stats[:, 8:12],
                                            1.0 / DK, 1e-5, AL.mult, AL.add)
                p.inc(VE, i)
                rsqrt_newton(stats[:, 8:12], stats[:, 12:16],
                             stats[:, 16:20], stats[:, 20:24])
                for h in range(H):
                    p.eng[VE].drain()
                    i = p.eng[VE].scalar_tensor_tensor(
                        on_t[:, h * 256:(h + 1) * 256], on_t[:, h * 256:(h + 1) * 256],
                        stats[:, 12 + h:13 + h], gn_bc[:], AL.mult, AL.mult)
                rms_prev = p.inc(VE, i)
                if b == 0 and ci == 0:
                    tap("on_h0", on_t[:, 0:256], [rms_prev])
                yield
                for half in range(2):
                    for h in range(H):
                        bki = A(h)
                        off = half * 128
                        bk.pre_write(bki, (rms_prev,))
                        ins = nc.tensor.transpose(
                            bank_ps[bki][:, off:off + 128],
                            on_t[:, h * 256 + half * 128: h * 256 + half * 128 + 128],
                            eye[:])
                        bk.set_write(bki, p.inc(TE, ins))
                        dsl = slice(h * 256 + half * 128, h * 256 + half * 128 + 128)
                        bk.read(bki, SE)
                        p.eng[SE].drain()
                        i = p.eng[SE].copy(onf_hi[:, dsl], bank_ps[bki][:, off:off + 128])
                        t_of = p.inc(SE, i)
                        bk.add_reader(bki, t_of)
                        bk.read(bki, VE)
                        p.wait(VE, t_of)
                        p.eng[VE].drain()
                        i = p.eng[VE].scalar_tensor_tensor(
                            onf_lob[:, dsl], bank_ps[bki][:, off:off + 128], 1.0,
                            onf_hi[:, dsl].bitcast(F32), AL.mult, AL.subtract)
                        bk.add_reader(bki, p.inc(VE, i))
                        p.waits(GE, (t_of,))
                        p.eng[GE].drain()
                        i = p.eng[GE].tensor_copy(onf_hib[:, dsl],
                                                  onf_hi[:, dsl].bitcast(F32))
                        p.inc(GE, i)
                    yield
                t_split_all = (p.tok(SE), p.tok(VE), p.tok(GE))
                for half in range(2):
                    bki = 4 + half
                    bk.pre_write(bki, t_split_all)
                    for vc in range(NDC):
                        csl = slice(vc * 128, (vc + 1) * 128)
                        wsl = slice(vc * D + half * 512, vc * D + (half + 1) * 512)
                        p.waits(TE, t_wsplit[vc])
                        nc.tensor.matmul(bank_ps[bki][:], onf_hi[:, csl], wo_hi[:, wsl],
                                         start=(vc == 0), stop=False)
                        nc.tensor.matmul(bank_ps[bki][:], onf_hib[:, csl], wo_lob[:, wsl],
                                         start=False, stop=False)
                        ins = nc.tensor.matmul(bank_ps[bki][:], onf_lob[:, csl],
                                               wo_hib[:, wsl],
                                               start=False, stop=(vc == NDC - 1))
                        if vc in (2, 5):
                            yield
                    bk.set_write(bki, p.inc(TE, ins))
                    bk.read(bki, VE)
                    p.eng[VE].drain()
                    i = p.eng[VE].tensor_tensor(qt[:, oD + half * 512:oD + (half + 1) * 512],
                                                bank_ps[bki][:],
                                                ga_bc[:, half * 512:(half + 1) * 512], AL.mult)
                    t_x1a = p.inc(VE, i)
                    bk.add_reader(bki, t_x1a)
                    p.waits(GE, (t_x1a, tx))
                    p.eng[GE].drain()
                    i = p.eng[GE].tensor_tensor(qt[:, oD + half * 512:oD + (half + 1) * 512],
                                                qt[:, oD + half * 512:oD + (half + 1) * 512],
                                                xt2[:, oD + half * 512:oD + (half + 1) * 512],
                                                AL.add)
                    p.inc(GE, i)
                    yield
                t_x1st = p.dma(GE, X1S[b, t0:t0 + CH, :], qt[:, oD:oD + D],
                               waits=(p.tok(GE),), slot=4 + jx)
                if b == 0 and ci == 0:
                    tap("x1_c0", qt[:, 0:D], [p.tok(GE)])
                cons[jx] = (p.tok(TE), p.tok(VE), p.tok(SE), p.tok(GE), t_x1st)

            def drive(gc, gs):
                if gc is not None:
                    try:
                        next(gc)
                    except StopIteration:
                        gc = None
                done_c = gc is None
                done_s = gs is None
                while not (done_c and done_s):
                    if not done_s:
                        try:
                            next(gs)
                        except StopIteration:
                            done_s = True
                    if not done_c:
                        try:
                            next(gc)
                        except StopIteration:
                            done_c = True

            for b in range(n_seq):
                i = p.eng[VE].memset(s_state[:], 0.0)
                p.inc(VE, i)
                build_gate_bc(2, ga_bc, b, bank_ps[0], bk, 0)
                p.barrier()
                cons = [(), ()]
                drive(chain_part(b, 0), None)
                for ci in range(NCH):
                    gc2 = chain_part(b, ci + 1) if ci + 1 < NCH else None
                    drive(gc2, state_part(b, ci))
            p.barrier()

        with ExitStack() as s3:
            D2, D4 = D // 2, D // 4
            wst = junkD[:]
            cn1_hi = sb("cn1_hi", [128, NDC * D], F32R, stx=s3)
            cn1_hib = sb("cn1_hib", [128, NDC * D], BF16, stx=s3)
            cn1_lob = sb("cn1_lob", [128, NDC * D], BF16, stx=s3)
            cn2_hi = sb("cn2_hi", [128, NDC * D2], F32R, stx=s3)
            cn2_hib = sb("cn2_hib", [128, NDC * D2], BF16, stx=s3)
            cn2_lob = sb("cn2_lob", [128, NDC * D2], BF16, stx=s3)
            cn3_t = sb("cn3_t", [128, 4 * D4], stx=s3)
            cn4_t = sb("cn4_t", [128, 2], stx=s3)
            cnb = sb("cnb", [128, 16], stx=s3)
            gb_bc = sb("gb_bc", [128, D], stx=s3)
            TH = max(T // 4, CH)
            NCHH = TH // CH
            h2res = sb("h2res", [128, TH * NDC], stx=s3)
            h2sum = sb("h2sum", [128, 16], stx=s3)

            TILE = min(512, TH)
            NPT = TH // TILE
            if TH * NDC < 2 * D:
                x1dev = sb("x1dev", [128, 2 * D], stx=s3)
            spl_hi = sb("spl_hi", [128, NDC * TILE], F32R, stx=s3)
            spl_hib = sb("spl_hib", [128, NDC * TILE], BF16, stx=s3)
            spl_lob = sb("spl_lob", [128, NDC * TILE], BF16, stx=s3)
            g1_hi = sb("g1_hi", [128, NDC * TILE], F32R, stx=s3)
            g1_hib = sb("g1_hib", [128, NDC * TILE], BF16, stx=s3)
            g1_lob = sb("g1_lob", [128, NDC * TILE], BF16, stx=s3)
            g2f = sb("g2f", [128, 4 * TILE], stx=s3)
            tf_row = junkD[:1, 0:TILE]
            assert 2 * TILE <= 1024
            gcw_t = sb("gcw", [128, 1024], stx=s3)
            gcw = gcw_t[:]
            g3f = gcw_t[:, 0:2 * TILE]
            if 4 * TILE >= 2 * D:
                x1b = g2f[:, 0:D]
                outb = g2f[:, D:2 * D]
            else:
                x1b = sb("x1b_dev", [128, D], stx=s3)[:]
                outb = sb("outb_dev", [128, D], stx=s3)[:]
            tw1 = g2f[:, 0:TILE]
            tw2 = g2f[:, TILE:2 * TILE]
            bank_ps = [psum(f"f{i}", [128, 512], stx=s3) for i in range(6)]
            bk = Banks(p, bank_ps)

            def load_split(Wd, hi, hib, lob, ncols, nchunks):
                toks = []
                prev = None
                for dc in range(nchunks):
                    dsl = slice(dc * ncols, (dc + 1) * ncols)
                    tl = p.dma(SY, wst[:, 0:ncols], Wd[dc * 128:(dc + 1) * 128, :],
                               waits=(prev,) if prev else ())
                    p.wait(SE, tl)
                    p.eng[SE].drain()
                    i = p.eng[SE].copy(hi[:, dsl], wst[:, 0:ncols])
                    th = p.inc(SE, i)
                    p.waits(GE, (tl, th))
                    p.eng[GE].drain()
                    i = p.eng[GE].tensor_tensor(lob[:, dsl], wst[:, 0:ncols],
                                                hi[:, dsl].bitcast(F32), AL.subtract)
                    tlo = p.inc(GE, i)
                    p.wait(VE, th)
                    p.eng[VE].drain()
                    i = p.eng[VE].tensor_copy(hib[:, dsl], hi[:, dsl].bitcast(F32))
                    thb = p.inc(VE, i)
                    prev = tlo
                    toks.append((tlo, thb))
                return toks

            t_cn1 = load_split(CW[0], cn1_hi, cn1_hib, cn1_lob, D, NDC)
            p.barrier()
            t_cn2 = load_split(CW[1], cn2_hi, cn2_hib, cn2_lob, D2, NDC)
            t_cn3 = [p.dma(SY, cn3_t[:, dc * D4:(dc + 1) * D4],
                           CW[2][dc * 128:(dc + 1) * 128, :]) for dc in range(4)]
            t_cn4 = [p.dma(SY, cn4_t[:, dc:dc + 1], CW[3][dc * 128:(dc + 1) * 128, :])
                     for dc in range(2)]
            for dc in range(8):
                p.dma(SY, cnb[:, dc:dc + 1], CB[0][dc * 128:(dc + 1) * 128])
            for dc in range(4):
                p.dma(SY, cnb[:, 8 + dc:9 + dc], CB[1][dc * 128:(dc + 1) * 128])
            for dc in range(2):
                p.dma(SY, cnb[:, 12 + dc:13 + dc], CB[2][dc * 128:(dc + 1) * 128])
            t_cb4 = p.dma(SY, cnb[:1, 14:15], CB[3][:])
            t_cnb = p.tok("dma3")
            p.barrier()

            spl_war = None

            def pb_seq(b):
                nonlocal spl_war
                i = p.eng[VE].memset(h2sum[:], 0.0)
                p.inc(VE, i)
                build_gate_bc(5, gb_bc, b, bank_ps[0], bk, 0)
                t_bar = p.barrier()
                b1gate = (t_bar,)
                t_h2st = {}
                if TH * NDC >= 2 * D:
                    x1v = lambda j: h2res[:, (j % 2) * D:(j % 2) * D + D]
                else:
                    x1v = lambda j: x1dev[:, (j % 2) * D:(j % 2) * D + D]
                t_lnp = {}
                tx1p = {0: p.dma(SY, x1v(0), X1S[b, 0:CH, :], waits=(t_bar,))}
                t_lnp[0] = ln_stats(x1v(0), (tx1p[0],), co=0)
                for ci in range(NCH):
                    t0 = ci * CH
                    j = ci % 2
                    xb = x1v(j)
                    co = j * 8
                    p.wait(VE, t_lnp[ci])
                    p.eng[VE].drain()
                    i = p.eng[VE].tensor_scalar(xb, xb, stats[:, co + 2:co + 3],
                                                stats[:, co + 7:co + 8], AL.add, AL.mult)
                    t_xn = p.inc(VE, i)
                    if ci + 1 < NCH:
                        tx1p[ci + 1] = p.dma(SY, x1v(ci + 1),
                                             X1S[b, t0 + CH:t0 + 2 * CH, :], waits=b1gate)
                        t_lnp[ci + 1] = ln_stats(x1v(ci + 1), (tx1p[ci + 1],),
                                                 co=(1 - j) * 8)
                    for dc in range(NDC):
                        bki = dc % 2
                        off = ((dc // 2) % 2) * 128
                        bk.pre_write(bki, (t_xn,))
                        ins = nc.tensor.transpose(bank_ps[bki][:, off:off + 128],
                                                  xb[:, dc * 128:(dc + 1) * 128], eye[:])
                        bk.set_write(bki, p.inc(TE, ins))
                        bk.read(bki, VE)
                        p.wait(VE, t_h2st.get(ci - 1))
                        p.eng[VE].drain()
                        i = p.eng[VE].tensor_scalar(
                            outb[:, dc * 128:(dc + 1) * 128], bank_ps[bki][:, off:off + 128],
                            cb1[:, dc * n_seq + b: dc * n_seq + b + 1],
                            modcol(3, dc, b), AL.mult, AL.add)
                        t_h2 = p.inc(VE, i)
                        bk.add_reader(bki, t_h2)
                        p.eng[VE].drain()
                        i = p.eng[VE].tensor_scalar(junkD[:, 0:CH],
                                                    outb[:, dc * 128:(dc + 1) * 128],
                                                    1.0, h2sum[:, 8 + dc:9 + dc],
                                                    AL.mult, AL.add,
                                                    accum_out=h2sum[:, 8 + dc:9 + dc])
                        p.inc(VE, i)
                        t_h2st[ci] = p.dma(GE, H2S[b, ci, dc * 128:(dc + 1) * 128, :],
                                           outb[:, dc * 128:(dc + 1) * 128], waits=(t_h2,),
                                           slot=4 + ci % 2)
                    b1gate = (p.tok(TE), p.tok(VE), p.tok(SE))
                t_bar = p.barrier()
                tap("h2sum" if b == 0 else "_skip", h2sum[:, 8:16], [p.tok(VE)])

                p.eng[VE].drain()
                i = p.eng[VE].tensor_scalar(h2sum[:, 0:8], h2sum[:, 8:16], 1.0 / T, None,
                                            AL.mult)
                t_mean = p.inc(VE, i)
                g1c = h2sum[:, 8:12]
                g2c = h2sum[:, 12:14]
                t_mv = None
                t_act_prev = None
                for m in range(4):
                    ldt = []
                    for dc in range(NDC):
                        ldt.append(p.dma(SY, gcw[:, dc * 128:(dc + 1) * 128],
                                         GW[0][dc * 128:(dc + 1) * 128, m * 128:(m + 1) * 128],
                                         waits=(t_mv,) if (t_mv and dc == 0) else ()))
                    bk.pre_write(4, (t_mean,))
                    p.waits(TE, tuple(ldt))
                    for dc in range(NDC):
                        ins = nc.tensor.matmul(bank_ps[4][:, m:m + 1],
                                               gcw[:, dc * 128:(dc + 1) * 128],
                                               h2sum[:, dc:dc + 1],
                                               start=(dc == 0), stop=(dc == NDC - 1))
                    t_mv = p.inc(TE, ins)
                    bk.set_write(4, t_mv)
                    tb1 = p.dma(SY, stats[:, 20:21], GB[0][m * 128:(m + 1) * 128],
                                waits=(t_act_prev,) if t_act_prev else ())
                    bk.read(4, SE)
                    p.waits(SE, (tb1,))
                    p.eng[SE].drain()
                    i = p.eng[SE].activation(g1c[:, m:m + 1], bank_ps[4][:, m:m + 1], AF.Gelu,
                                             bias=stats[:, 20:21], scale=1.0)
                    t_act_prev = p.inc(SE, i)
                    bk.add_reader(4, t_act_prev)
                for m in range(2):
                    ldt = []
                    for dc in range(4):
                        ldt.append(p.dma(SY, gcw[:, dc * 128:(dc + 1) * 128],
                                         GW[1][dc * 128:(dc + 1) * 128, m * 128:(m + 1) * 128],
                                         waits=(t_mv,) if dc == 0 else ()))
                    bk.pre_write(4, ())
                    p.waits(TE, tuple(ldt))
                    for dc in range(4):
                        ins = nc.tensor.matmul(bank_ps[4][:, 8 + m:9 + m],
                                               gcw[:, dc * 128:(dc + 1) * 128],
                                               g1c[:, dc:dc + 1],
                                               start=(dc == 0), stop=(dc == 3))
                    t_mv = p.inc(TE, ins)
                    bk.set_write(4, t_mv)
                    tb2 = p.dma(SY, stats[:, 20:21], GB[1][m * 128:(m + 1) * 128],
                                waits=(t_act_prev,))
                    bk.read(4, SE)
                    p.waits(SE, (tb2,))
                    p.eng[SE].drain()
                    i = p.eng[SE].activation(g2c[:, m:m + 1], bank_ps[4][:, 8 + m:9 + m],
                                             AF.Gelu, bias=stats[:, 20:21], scale=1.0)
                    t_act_prev = p.inc(SE, i)
                    bk.add_reader(4, t_act_prev)
                ldt = [p.dma(SY, gcw[:, dc * 128:dc * 128 + 1],
                             GW[2][dc * 128:(dc + 1) * 128, :], waits=(t_mv,) if dc == 0 else ())
                       for dc in range(2)]
                bk.pre_write(4, ())
                p.waits(TE, tuple(ldt))
                for dc in range(2):
                    ins = nc.tensor.matmul(bank_ps[4][:1, 12:13], gcw[:, dc * 128:dc * 128 + 1],
                                           g2c[:, dc:dc + 1], start=(dc == 0), stop=(dc == 1))
                t_mv = p.inc(TE, ins)
                bk.set_write(4, t_mv)
                tb3 = p.dma(SY, stats[:1, 20:21], GB[2][:], waits=(t_act_prev,))
                bk.read(4, SE)
                p.waits(SE, (tb3,))
                p.eng[SE].drain()
                i = p.eng[SE].activation(stats[:1, 21:22], bank_ps[4][:1, 12:13], AF.Sigmoid,
                                         bias=stats[:1, 20:21], scale=1.0)
                t_gf = p.inc(SE, i)
                bk.add_reader(4, t_gf)
                p.waits(VE, (t_gf, t_ar))
                p.eng[VE].drain()
                p.eng[VE].tensor_tensor(gfk[:, 2 * b:2 * b + 1], stats[:1, 21:22], ar_t[:],
                                        AL.mult)
                p.eng[VE].drain()
                p.eng[VE].tensor_scalar(gfk[:, 2 * b:2 * b + 1], gfk[:, 2 * b:2 * b + 1],
                                        1.4, 0.1, AL.mult, AL.add)
                p.eng[VE].drain()
                p.eng[VE].tensor_tensor(gfk[:, 2 * b:2 * b + 1], gfk[:, 2 * b:2 * b + 1],
                                        ar_t[:], AL.subtract)
                p.eng[VE].drain()
                i = p.eng[VE].tensor_scalar(gfk[:, 2 * b + 1:2 * b + 2], ar_t[:], 0.6, None,
                                            AL.mult)
                p.inc(VE, i)
                if b == 0:
                    tap("gf", stats[:1, 21:22], [p.tok(SE)])
                t_bar = p.barrier()

                t_outst = None
                for half in range(T // TH):
                  ci_base = half * NCHH
                  for cl in range(NCHH):
                      for dc in range(NDC):
                          col = (dc * NCHH + cl) * CH
                          gate_h = t_bar if (t_bar and isinstance(t_bar[0], tuple)) else (t_bar,)
                          p.dma(SY, h2res[:, col:col + CH],
                                H2S[b, ci_base + cl, dc * 128:(dc + 1) * 128, :],
                                waits=gate_h if (cl == 0 and dc == 0) else ())
                  t_ld = p.tok("dma3")
                  tw_tok = {}
                  assert NPT == 1
                  for cyc in range(3):
                    for ti in range(NPT):
                        tc0 = ti * TILE

                        def outsl(dc):
                            base = dc * NCHH * CH + tc0
                            return h2res[:, base:base + TILE]

                        prev_g1_split = None
                        if cyc == 0:
                            spl_tok = {}
                            gate0 = (t_ld, spl_war)
                            for dc in range(NDC):
                                ssl = slice(dc * TILE, (dc + 1) * TILE)
                                deps0 = gate0 if dc == 0 else ()
                                p.waits(SE, deps0)
                                p.eng[SE].drain()
                                i = p.eng[SE].copy(spl_hi[:, ssl], outsl(dc))
                                th = p.inc(SE, i)
                                p.eng[SE].drain()
                                i = p.eng[SE].copy(spl_hib[:, ssl],
                                                   spl_hi[:, ssl].bitcast(F32))
                                thb = p.inc(SE, i)
                                p.waits(VE, deps0 + (th,))
                                p.eng[VE].drain()
                                i = p.eng[VE].tensor_tensor(spl_lob[:, ssl], outsl(dc),
                                                            spl_hi[:, ssl].bitcast(F32),
                                                            AL.subtract)
                                tlo = p.inc(VE, i)
                                spl_tok[dc] = (thb, tlo)
                            dcord = list(range(NDC))
                        for m in range(NDC):
                            bki = 2 + m % 2
                            bk.pre_write(bki, tuple(t for pr in t_cn1 for t in pr))
                            for di, dc in enumerate(dcord):
                                wsl = slice(dc * D + m * 128, dc * D + m * 128 + 128)
                                ssl = slice(dc * TILE, (dc + 1) * TILE)
                                p.waits(TE, spl_tok[dc])
                                nc.tensor.matmul(bank_ps[bki][:, 0:TILE], cn1_hi[:, wsl],
                                                 spl_hi[:, ssl], start=(di == 0), stop=False)
                                nc.tensor.matmul(bank_ps[bki][:, 0:TILE], cn1_lob[:, wsl],
                                                 spl_hib[:, ssl], start=False, stop=False)
                                ins = nc.tensor.matmul(bank_ps[bki][:, 0:TILE], cn1_hib[:, wsl],
                                                       spl_lob[:, ssl], start=False,
                                                       stop=(di == NDC - 1))
                            bk.set_write(bki, p.inc(TE, ins))
                            bk.pre_write(5, ())
                            bk.read(bki, SE)
                            p.waits(SE, (t_cnb, prev_g1_split))
                            p.eng[SE].drain()
                            i = p.eng[SE].activation(bank_ps[5][:, 0:TILE],
                                                     bank_ps[bki][:, 0:TILE], AF.Gelu,
                                                     bias=cnb[:, m:m + 1], scale=1.0)
                            t_g1e = p.inc(SE, i)
                            bk.add_reader(bki, t_g1e)
                            bk.set_write(5, t_g1e)
                            bk.read(5, SE)
                            p.eng[SE].drain()
                            i = p.eng[SE].copy(g1_hi[:, m * TILE:(m + 1) * TILE],
                                               bank_ps[5][:, 0:TILE])
                            t_hi = p.inc(SE, i)
                            bk.add_reader(5, t_hi)
                            bk.read(5, VE)
                            p.wait(VE, t_hi)
                            p.eng[VE].drain()
                            i = p.eng[VE].scalar_tensor_tensor(
                                g1_lob[:, m * TILE:(m + 1) * TILE],
                                bank_ps[5][:, 0:TILE], 1.0,
                                g1_hi[:, m * TILE:(m + 1) * TILE].bitcast(F32),
                                AL.mult, AL.subtract)
                            prev_g1_split = p.inc(VE, i)
                            bk.add_reader(5, prev_g1_split)
                            p.waits(GE, (t_hi,))
                            p.eng[GE].drain()
                            i = p.eng[GE].tensor_copy(
                                g1_hib[:, m * TILE:(m + 1) * TILE],
                                g1_hi[:, m * TILE:(m + 1) * TILE].bitcast(F32))
                            p.inc(GE, i)
                        t_g1s = (p.tok(SE), p.tok(VE), p.tok(GE))
                        for m in range(4):
                            bki = 2 + m % 2
                            bk.pre_write(bki, t_g1s + tuple(t for pr in t_cn2 for t in pr))
                            for dc in range(NDC):
                                wsl = slice(dc * D2 + m * 128, dc * D2 + m * 128 + 128)
                                ssl = slice(dc * TILE, (dc + 1) * TILE)
                                nc.tensor.matmul(bank_ps[bki][:, 0:TILE], cn2_hi[:, wsl],
                                                 g1_hi[:, ssl], start=(dc == 0), stop=False)
                                nc.tensor.matmul(bank_ps[bki][:, 0:TILE], cn2_lob[:, wsl],
                                                 g1_hib[:, ssl], start=False, stop=False)
                                ins = nc.tensor.matmul(bank_ps[bki][:, 0:TILE], cn2_hib[:, wsl],
                                                       g1_lob[:, ssl], start=False,
                                                       stop=(dc == NDC - 1))
                            bk.set_write(bki, p.inc(TE, ins))
                            bk.read(bki, SE)
                            p.wait(SE, p.tok(GE))
                            p.eng[SE].drain()
                            i = p.eng[SE].activation(g2f[:, m * TILE:(m + 1) * TILE],
                                                     bank_ps[bki][:, 0:TILE], AF.Gelu,
                                                     bias=cnb[:, 8 + m:9 + m], scale=1.0)
                            bk.add_reader(bki, p.inc(SE, i))
                        t_g2 = p.tok(SE)
                        for m in range(2):
                            bki = 2 + m
                            bk.pre_write(bki, (t_g2,) + tuple(t_cn3))
                            for dc in range(4):
                                wsl = slice(dc * D4 + m * 128, dc * D4 + m * 128 + 128)
                                ins = nc.tensor.matmul(bank_ps[bki][:, 0:TILE], cn3_t[:, wsl],
                                                       g2f[:, dc * TILE:(dc + 1) * TILE],
                                                       start=(dc == 0), stop=(dc == 3))
                            bk.set_write(bki, p.inc(TE, ins))
                            bk.read(bki, SE)
                            p.eng[SE].drain()
                            i = p.eng[SE].activation(g3f[:, m * TILE:(m + 1) * TILE],
                                                     bank_ps[bki][:, 0:TILE], AF.Gelu,
                                                     bias=cnb[:, 12 + m:13 + m], scale=1.0)
                            bk.add_reader(bki, p.inc(SE, i))
                        t_g3 = p.tok(SE)
                        bk.pre_write(1, (t_g3,) + tuple(t_cn4))
                        for dc in range(2):
                            ins = nc.tensor.matmul(bank_ps[1][:1, 0:TILE], cn4_t[:, dc:dc + 1],
                                                   g3f[:, dc * TILE:(dc + 1) * TILE],
                                                   start=(dc == 0), stop=(dc == 1))
                        bk.set_write(1, p.inc(TE, ins))
                        bk.read(1, SE)
                        p.waits(SE, (t_cb4,))
                        p.eng[SE].drain()
                        i = p.eng[SE].activation(tf_row, bank_ps[1][:1, 0:TILE], AF.Sigmoid,
                                                 bias=cnb[:1, 14:15], scale=1.0)
                        t_tf = p.inc(SE, i)
                        bk.add_reader(1, t_tf)
                        p.waits(VE, (t_tf,))
                        p.eng[VE].drain()
                        i = p.eng[VE].tensor_scalar(tf_row, tf_row,
                                                    gfk[:, 2 * b + 1:2 * b + 2],
                                                    gfk[:, 2 * b:2 * b + 1], AL.mult, AL.add)
                        t_cp = p.inc(VE, i)
                        bk.pre_write(1, (t_cp,))
                        ins = nc.tensor.matmul(bank_ps[1][:, 0:TILE], ones1[:], tf_row,
                                               start=True, stop=True)
                        t_cbc = p.inc(TE, ins)
                        bk.set_write(1, t_cbc)
                        spl_war = t_cbc
                        prev_twA = {}
                        spl_new = {}
                        for pr in range(4):
                            q13 = outsl(pr)
                            q24 = outsl(pr + 4)
                            bA = g2f[:, (2 * (pr % 2)) * TILE:(2 * (pr % 2) + 1) * TILE]
                            bB = g2f[:, (2 * (pr % 2) + 1) * TILE:
                                     (2 * (pr % 2) + 2) * TILE]
                            bk.read(1, VE)
                            p.waits(VE, (prev_twA.get(pr - 2),))
                            p.eng[VE].drain()
                            p.eng[VE].tensor_tensor(bA, q24, bank_ps[1][:, 0:TILE], AL.mult)
                            p.eng[VE].drain()
                            i = p.eng[VE].tensor_tensor(bB, q13, bank_ps[1][:, 0:TILE],
                                                        AL.mult)
                            tsc = p.inc(VE, i)
                            bk.add_reader(1, tsc)
                            p.waits(GE, (tsc,))
                            p.eng[GE].drain()
                            i = p.eng[GE].tensor_tensor(q13, q13, bA,
                                                        AL.add if pr < 2 else AL.subtract)
                            tg13 = p.inc(GE, i)
                            p.eng[GE].drain()
                            i = p.eng[GE].tensor_tensor(q24, q24, bB,
                                                        AL.subtract if pr < 2 else AL.add)
                            tg24 = p.inc(GE, i)
                            prev_twA[pr] = tg24
                            if cyc < 2:
                                for qq, dcq, tg in ((q13, pr, tg13), (q24, pr + 4, tg24)):
                                    ssl = slice(dcq * TILE, (dcq + 1) * TILE)
                                    p.waits(SE, (tg, t_cbc))
                                    p.eng[SE].drain()
                                    i = p.eng[SE].copy(spl_hi[:, ssl], qq)
                                    th = p.inc(SE, i)
                                    p.eng[SE].drain()
                                    i = p.eng[SE].copy(spl_hib[:, ssl],
                                                       spl_hi[:, ssl].bitcast(F32))
                                    thb = p.inc(SE, i)
                                    p.waits(VE, (th,))
                                    p.eng[VE].drain()
                                    i = p.eng[VE].tensor_tensor(
                                        spl_lob[:, ssl], qq,
                                        spl_hi[:, ssl].bitcast(F32), AL.subtract)
                                    tlo = p.inc(VE, i)
                                    spl_new[dcq] = (thb, tlo)
                        tw_tok[ti] = prev_twA[3]
                        if cyc < 2:
                            spl_tok = spl_new
                            dcord = [0, 4, 1, 5, 2, 6, 3, 7]

                  b3gate = tuple(tw_tok.values())
                  t_b3ge = {}
                  for cl in range(NCHH):
                    ci = ci_base + cl
                    t0 = ci * CH
                    mbuf = junkD if cl % 2 == 0 else gcw_t
                    tx1 = p.dma(SY, x1b, X1S[b, t0:t0 + CH, :],
                                waits=b3gate if cl == 0 else (p.tok(GE),))
                    for dc in range(NDC):
                        bki = dc % 2
                        off = ((dc // 2) % 2) * 128
                        col = (dc * NCHH + cl) * CH
                        bk.pre_write(bki, b3gate if (cl == 0 and dc == 0) else ())
                        ins = nc.tensor.transpose(bank_ps[bki][:, off:off + 128],
                                                  h2res[:, col:col + CH], eye[:])
                        bk.set_write(bki, p.inc(TE, ins))
                        bk.read(bki, VE)
                        if dc == 0:
                            p.wait(VE, t_b3ge.get(cl - 2))
                        p.eng[VE].drain()
                        i = p.eng[VE].tensor_tensor(mbuf[:, dc * 128:(dc + 1) * 128],
                                                    bank_ps[bki][:, off:off + 128],
                                                    gb_bc[:, dc * 128:(dc + 1) * 128], AL.mult)
                        t_m = p.inc(VE, i)
                        bk.add_reader(bki, t_m)
                        p.waits(GE, (t_m, tx1) + ((t_outst,) if (t_outst and dc == 0) else ()))
                        p.eng[GE].drain()
                        i = p.eng[GE].tensor_tensor(outb[:, dc * 128:(dc + 1) * 128],
                                                    mbuf[:, dc * 128:(dc + 1) * 128],
                                                    x1b[:, dc * 128:(dc + 1) * 128], AL.add)
                        t_b3ge[cl] = p.inc(GE, i)
                    t_outst = p.dma(GE, OUT[b, t0:t0 + CH, :], outb, waits=(p.tok(GE),),
                                    slot=4 + cl % 2)
                  t_bar = (p.tok(TE), p.tok(GE), p.tok(VE), t_outst)

            for b in range(n_seq):
                pb_seq(b)
            p.barrier()

        for d in ("dma4", "dma5"):
            if p.cnt[d]:
                p.eng[GE].wait_ge(p.sem[d], p.cnt[d])
        if p.cnt[GE]:
            p.eng[GE].wait_ge(p.sem[GE], p.cnt[GE])

    return nc, tap_handles


def _consts():
    return {
        "eye": np.eye(128, dtype=np.float32),
        "masku": np.triu(np.ones((128, 128), np.float32), 1),
        "maskl": np.tril(np.ones((128, 128), np.float32), -1),
        "maskui": np.triu(np.ones((128, 128), np.float32), 0),
        "ones1": np.ones((1, 128), np.float32),
        "sel4": np.concatenate([np.tile((np.arange(4) == h).astype(np.float32)[:, None],
                                        (1, 128)) for h in range(4)], axis=1),
    }


def make_in_map(inputs, c, n_seq):
    m = {
        "x": np.ascontiguousarray(inputs["L0_attn_x"][c * n_seq:(c + 1) * n_seq]),
        "c": np.ascontiguousarray(inputs["L0_c"][c * n_seq:(c + 1) * n_seq]),
        "wmod": inputs["L0_wmod"],
        "wq": inputs["L0_attn_Wq"], "wk": inputs["L0_attn_Wk"],
        "wv": inputs["L0_attn_Wv"], "wb": inputs["L0_attn_Wbeta"],
        "wo": inputs["L0_attn_Wo"], "g_norm": inputs["L0_attn_g_norm"],
        "cn_w1": inputs["L0_ffn_cn_w1"], "cn_b1": inputs["L0_ffn_cn_b1"],
        "cn_w2": inputs["L0_ffn_cn_w2"], "cn_b2": inputs["L0_ffn_cn_b2"],
        "cn_w3": inputs["L0_ffn_cn_w3"], "cn_b3": inputs["L0_ffn_cn_b3"],
        "cn_w4": inputs["L0_ffn_cn_w4"], "cn_b4": inputs["L0_ffn_cn_b4"],
        "gc_w1": inputs["L0_ffn_gc_w1"], "gc_b1": inputs["L0_ffn_gc_b1"],
        "gc_w2": inputs["L0_ffn_gc_w2"], "gc_b2": inputs["L0_ffn_gc_b2"],
        "gc_w3": inputs["L0_ffn_gc_w3"], "gc_b3": inputs["L0_ffn_gc_b3"],
        "ar": inputs["L0_ffn_adaptive_range"],
    }
    m.update(_consts())
    return m


def kernel(**inputs):
    n_cores = 8
    n_seq = 32 // n_cores
    nc, _ = build_program(n_seq, 2048)
    in_maps = [make_in_map(inputs, c, n_seq) for c in range(n_cores)]
    res = run_bass_kernel_spmd(nc, in_maps, core_ids=list(range(n_cores)))
    return np.concatenate([r["out"] for r in res.results], axis=0)


# version 18
# speedup vs baseline: 1.0781x; 1.0360x over previous
import numpy as np
from contextlib import ExitStack

import concourse.bass as bass
import concourse.mybir as mybir
from concourse.bass_utils import run_bass_kernel_spmd

F32 = mybir.dt.float32
F32R = mybir.dt.float32r
BF16 = mybir.dt.bfloat16
AL = mybir.AluOpType
AF = mybir.ActivationFunctionType
AX = mybir.AxisListType

D = 1024
H = 4
DK = 256
CH = 128
NDC = D // 128


class P:
    ENGS = ("tensor", "vector", "scalar", "gpsimd", "sync")
    DSEMS = tuple(f"dma{i}" for i in range(6))
    QOF = {"sync": (0, 1, 2, 3), "gpsimd": (4, 5)}

    def __init__(self, nc, st):
        self.sem = {e: st.enter_context(nc.semaphore("sem_" + e))
                    for e in self.ENGS + self.DSEMS}
        self.cnt = {e: 0 for e in self.ENGS + self.DSEMS}
        self.waited = {}
        self.eng = {e: getattr(nc, e) for e in self.ENGS}
        self.fence = {d: 0 for d in self.DSEMS}
        self.selfw = {d: 0 for d in self.DSEMS}

    def inc(self, prod, instr, n=1):
        instr.then_inc(self.sem[prod], n)
        self.cnt[prod] += n
        return (prod, self.cnt[prod])

    def dma(self, q, out, in_, waits=(), slot=None):
        if slot is None:
            slot = 3 if q == "sync" else 4
        assert slot in self.QOF[q], (q, slot)
        d = f"dma{slot}"
        if self.fence[d] > self.selfw[d]:
            self.eng[q].wait_ge(self.sem[d], self.fence[d])
            self.selfw[d] = self.fence[d]
        for t in waits:
            if t is not None and t[0] == q:
                if self.waited.get((q, q), 0) < t[1]:
                    self.eng[q].wait_ge(self.sem[q], t[1])
                    self.waited[(q, q)] = t[1]
            else:
                self.wait(q, t)
        ins = self.eng[q].dma_start(out=out, in_=in_)
        ins.then_inc(self.sem[d], 16)
        self.cnt[d] += 16
        return (d, self.cnt[d])

    def tok(self, prod):
        return (prod, self.cnt[prod])

    def wait(self, cons, token):
        if token is None:
            return
        prod, val = token
        if val <= 0:
            return
        if prod.startswith("dma"):
            val = self.cnt[prod]
            self.fence[prod] = max(self.fence[prod], val)
            if (cons == "sync" and prod in ("dma0", "dma1", "dma2", "dma3")) or \
               (cons == "gpsimd" and prod in ("dma4", "dma5")):
                if val > self.selfw[prod]:
                    self.eng[cons].wait_ge(self.sem[prod], val)
                    self.selfw[prod] = max(self.selfw[prod], val)
                return
        elif prod == cons:
            return
        if self.waited.get((cons, prod), 0) >= val:
            return
        self.eng[cons].wait_ge(self.sem[prod], val)
        self.waited[(cons, prod)] = val

    def waits(self, cons, tokens):
        for t in tokens:
            self.wait(cons, t)

    def barrier(self):
        for q, slots in self.QOF.items():
            for s in slots:
                d = f"dma{s}"
                if self.cnt[d] > self.selfw[d]:
                    self.eng[q].wait_ge(self.sem[d], self.cnt[d])
                    self.selfw[d] = self.cnt[d]
                    self.fence[d] = self.cnt[d]
        toks = {}
        for e in ("tensor", "vector", "scalar", "gpsimd"):
            toks[e] = self.inc(e, self.eng[e].drain())
        t_sy = self.inc("sync", self.eng["sync"].nop())
        for c in self.ENGS:
            for e, t in toks.items():
                self.wait(c, t)
            if c != "sync":
                self.wait(c, t_sy)
        for d in self.DSEMS:
            for c in self.ENGS:
                if self.cnt[d] > 0:
                    self.wait(c, (d, self.cnt[d]))
        return toks["gpsimd"]


class Banks:

    def __init__(self, p, banks):
        self.p = p
        self.b = banks
        self.w = [None] * len(banks)
        self.r = [[] for _ in banks]

    def pre_write(self, i, extra=()):
        self.p.waits("tensor", tuple(self.r[i]) + tuple(extra))
        self.r[i] = []

    def set_write(self, i, tok):
        self.w[i] = tok

    def read(self, i, eng):
        self.p.wait(eng, self.w[i])

    def add_reader(self, i, tok):
        self.r[i].append(tok)


def build_program(n_seq, T, taps=None):
    taps = set(taps or ())
    nc = bass.Bass("TRN2", target_bir_lowering=False)
    NCH = T // CH

    def dram(n, s, k=None):
        return (nc.dram_tensor(n, s, F32, kind=k) if k else nc.dram_tensor(n, s, F32)).ap()

    X = dram("x", [n_seq, T, D], "ExternalInput")
    C_in = dram("c", [n_seq, D], "ExternalInput")
    WMOD = dram("wmod", [D, 6 * D], "ExternalInput")
    WQ = dram("wq", [D, D], "ExternalInput")
    WK = dram("wk", [D, D], "ExternalInput")
    WV = dram("wv", [D, D], "ExternalInput")
    WB = dram("wb", [D, H], "ExternalInput")
    WO = dram("wo", [D, D], "ExternalInput")
    GN = dram("g_norm", [DK], "ExternalInput")
    CW = [dram("cn_w1", [D, D], "ExternalInput"), dram("cn_w2", [D, D // 2], "ExternalInput"),
          dram("cn_w3", [D // 2, D // 4], "ExternalInput"), dram("cn_w4", [D // 4, 1], "ExternalInput")]
    CB = [dram("cn_b1", [D], "ExternalInput"), dram("cn_b2", [D // 2], "ExternalInput"),
          dram("cn_b3", [D // 4], "ExternalInput"), dram("cn_b4", [1], "ExternalInput")]
    GW = [dram("gc_w1", [D, D // 2], "ExternalInput"), dram("gc_w2", [D // 2, D // 4], "ExternalInput"),
          dram("gc_w3", [D // 4, 1], "ExternalInput")]
    GB = [dram("gc_b1", [D // 2], "ExternalInput"), dram("gc_b2", [D // 4], "ExternalInput"),
          dram("gc_b3", [1], "ExternalInput")]
    AR = dram("ar", [1], "ExternalInput")
    EYE = dram("eye", [128, 128], "ExternalInput")
    MASKU = dram("masku", [128, 128], "ExternalInput")
    MASKL = dram("maskl", [128, 128], "ExternalInput")
    MASKUI = dram("maskui", [128, 128], "ExternalInput")
    ONES1 = dram("ones1", [1, 128], "ExternalInput")
    SEL4 = dram("sel4", [4, 4 * 128], "ExternalInput")

    OUT = dram("out", [n_seq, T, D], "ExternalOutput")

    H1S = dram("h1_scr", [n_seq, NCH, D, CH])
    QS = dram("q_scr", [n_seq, T, D])
    KS = dram("k_scr", [n_seq, T, D])
    VS = dram("v_scr", [n_seq, T, D])
    BSc = dram("b_scr", [n_seq, T, H])
    X1S = dram("x1_scr", [n_seq, T, D])
    H2S = dram("h2_scr", [n_seq, NCH, D, CH])

    tap_handles = {}

    with ExitStack() as st:
        def sb(name, shape, dt=F32, stx=None):
            return (stx or st).enter_context(nc.sbuf_tensor("sb_" + name, shape, dt))

        def psum(name, shape, stx=None):
            return (stx or st).enter_context(nc.psum_tensor("ps_" + name, shape, F32))

        p = P(nc, st)
        TE, VE, SE, GE, SY = "tensor", "vector", "scalar", "gpsimd", "sync"

        def tap(name, ap, deps):
            if name in taps:
                t = nc.dram_tensor("dbg_" + name, list(ap.shape), F32,
                                   kind="ExternalOutput").ap()
                tap_handles[name] = t
                tk = p.dma(GE, t[...], ap, waits=tuple(deps))
                for e in P.ENGS:
                    if e != "gpsimd":
                        p.wait(e, tk)
                p.wait("gpsimd", tk)

        eye = sb("eye", [128, 128])
        masku = sb("masku", [128, 128])
        maskl = sb("maskl", [128, 128])
        maskui = sb("maskui", [128, 128])
        ones1 = sb("ones1", [1, 128])
        sel4 = sb("sel4", [4, 4 * 128])
        gn_bc = sb("gn_bc", [128, DK])
        modv = sb("modv", [128, 48 * n_seq])
        ca1 = sb("ca1", [128, NDC * n_seq])
        cb1 = sb("cb1", [128, NDC * n_seq])
        stats = sb("stats", [128, 24])
        ar_t = sb("ar_t", [1, 1])
        gfk = sb("gfk", [1, 2 * n_seq])
        junkD = sb("junkD", [128, D])
        silc = junkD[:, 32:32 + NDC * n_seq]
        bcg = sb("bcg", [128, 8])
        bcrow = sb("bcrow", [4, 2 * 128])
        gnrow = junkD[:1, 512:512 + DK]

        for apd, src in ((eye, EYE), (masku, MASKU), (maskl, MASKL), (maskui, MASKUI),
                         (ones1, ONES1), (sel4, SEL4)):
            p.dma(SY, apd[:], src[:, :])
        t_ar = p.dma(SY, ar_t[:], AR[:])
        t_gnr = p.dma(SY, gnrow, GN[:])
        t_consts = p.tok(SY)

        def rsqrt_newton(vcol, rcol, t0, t1):
            p.wait(SE, p.tok(VE))
            p.eng[SE].drain()
            i = p.eng[SE].activation(t0, vcol, AF.Ln)
            tr = p.inc(SE, i)
            p.eng[SE].drain()
            i = p.eng[SE].activation(rcol, t0, AF.Exp, scale=-0.5)
            ts_ = p.inc(SE, i)
            p.wait(VE, ts_)
            p.eng[VE].drain()
            p.eng[VE].tensor_tensor(t1, rcol, rcol, AL.mult)
            p.eng[VE].drain()
            p.eng[VE].tensor_tensor(t1, t1, vcol, AL.mult)
            p.eng[VE].drain()
            p.eng[VE].tensor_scalar(t1, t1, -0.5, 1.5, AL.mult, AL.add)
            p.eng[VE].drain()
            i = p.eng[VE].tensor_tensor(rcol, rcol, t1, AL.mult)
            return p.inc(VE, i)

        def ln_stats(xsl, deps, co=0, stt=None):
            st_ = stats if stt is None else stt
            sc = lambda a, bnd: st_[:, co + a:co + bnd]
            p.waits(SE, deps)
            p.eng[SE].drain()
            i = p.eng[SE].activation(junkD[:], xsl, AF.Square, accum_out=sc(1, 2))
            t_sq = p.inc(SE, i)
            p.waits(VE, deps)
            p.eng[VE].drain()
            i = p.eng[VE].tensor_reduce(sc(0, 1), xsl, AX.X, AL.add)
            p.inc(VE, i)
            p.wait(VE, t_sq)
            p.eng[VE].drain()
            p.eng[VE].tensor_scalar(sc(2, 3), sc(0, 1), -1.0 / D, None, AL.mult)
            p.eng[VE].drain()
            p.eng[VE].tensor_tensor(sc(3, 4), sc(2, 3), sc(2, 3), AL.mult)
            p.eng[VE].drain()
            p.eng[VE].tensor_scalar(sc(4, 5), sc(1, 2), 1.0 / D, None, AL.mult)
            p.eng[VE].drain()
            p.eng[VE].tensor_tensor(sc(4, 5), sc(4, 5), sc(3, 4), AL.subtract)
            p.eng[VE].drain()
            i = p.eng[VE].tensor_scalar(sc(4, 5), sc(4, 5), 1e-5, None, AL.add)
            p.inc(VE, i)
            return rsqrt_newton(sc(4, 5), sc(7, 8), sc(5, 6), sc(6, 7))

        def modcol(vidx, dc, b):
            c0 = (vidx * 8 + dc) * n_seq + b
            return modv[:, c0:c0 + 1]

        def split_pair(full_ap, hi_ap, lo_ap, deps):
            p.waits(SE, deps)
            p.eng[SE].drain()
            i = p.eng[SE].copy(hi_ap, full_ap)
            th = p.inc(SE, i)
            p.waits(GE, tuple(deps) + (th,))
            p.eng[GE].drain()
            i = p.eng[GE].tensor_tensor(lo_ap, full_ap, hi_ap.bitcast(F32), AL.subtract)
            return th, p.inc(GE, i)

        with ExitStack() as st0:
            mw = sb("mw", [128, 2 * D], stx=st0)
            pm = psum("pm", [128, 512], stx=st0)
            t_c = []
            for b in range(n_seq):
                for dc in range(NDC):
                    cix = dc * n_seq + b
                    t_c.append(p.dma(SY, silc[:, cix:cix + 1], C_in[b, dc * 128:(dc + 1) * 128]))
            p.waits(SE, tuple(t_c))
            p.eng[SE].drain()
            i = p.eng[SE].activation(silc[:], silc[:], AF.Silu)
            t_silc = p.inc(SE, i)

            t_buf = [None, None]
            t_evac = {}
            for mc in range(48):
                j = mc % 2
                ld = []
                for dc in range(NDC):
                    ld.append(p.dma(SY, mw[:, j * D + dc * 128: j * D + (dc + 1) * 128],
                                    WMOD[dc * 128:(dc + 1) * 128, mc * 128:(mc + 1) * 128],
                                    waits=(t_buf[j],) if t_buf[j] else ()))
                p.waits(TE, tuple(ld) + (t_silc,))
                slot = mc % 4
                if mc >= 4:
                    p.wait(TE, t_evac[mc - 4])
                psl = pm[:, slot * n_seq: slot * n_seq + n_seq]
                for dc in range(NDC):
                    ins = nc.tensor.matmul(psl, mw[:, j * D + dc * 128: j * D + (dc + 1) * 128],
                                           silc[:, dc * n_seq:(dc + 1) * n_seq],
                                           start=(dc == 0), stop=(dc == NDC - 1))
                t_buf[j] = p.inc(TE, ins)
                p.wait(VE, t_buf[j])
                p.eng[VE].drain()
                i = p.eng[VE].tensor_copy(modv[:, mc * n_seq:(mc + 1) * n_seq], psl)
                t_evac[mc] = p.inc(VE, i)
            p.barrier()

            def gate_fix(vidx):
                sl = slice(vidx * 8 * n_seq, (vidx + 1) * 8 * n_seq)
                g = modv[:, sl]
                sgn = junkD[:, 0:8 * n_seq]
                p.eng[VE].drain()
                p.eng[VE].tensor_scalar(sgn, g, 0.0, None, AL.is_ge)
                p.eng[VE].drain()
                p.eng[VE].tensor_scalar(sgn, sgn, 2.0, -1.0, AL.mult, AL.add)
                p.eng[VE].drain()
                p.eng[VE].tensor_tensor(g, g, sgn, AL.mult)
                p.eng[VE].drain()
                p.eng[VE].tensor_scalar(g, g, 0.25, None, AL.max)
                p.eng[VE].drain()
                i = p.eng[VE].tensor_tensor(g, g, sgn, AL.mult)
                return p.inc(VE, i)
            gate_fix(2)
            p.eng[VE].drain()
            gate_fix(5)
            p.eng[VE].drain()
            p.eng[VE].tensor_scalar(ca1[:], modv[:, 8 * n_seq:16 * n_seq], 1.0, None, AL.add)
            p.eng[VE].drain()
            i = p.eng[VE].tensor_scalar(cb1[:], modv[:, 32 * n_seq:40 * n_seq], 1.0, None, AL.add)
            t_mod = p.inc(VE, i)

            p.waits(TE, (t_gnr, t_consts, t_mod))
            ins = nc.tensor.matmul(pm[:, 0:DK], ones1[:], gnrow, start=True, stop=True)
            t_g = p.inc(TE, ins)
            p.wait(SE, t_g)
            p.eng[SE].drain()
            i = p.eng[SE].copy(gn_bc[:], pm[:, 0:DK])
            p.inc(SE, i)
            p.barrier()
        tap("modv", modv[:], [p.tok(VE)])
        tap("ca1", ca1[:], [p.tok(VE)])

        def build_gate_bc(vidx, dst, b, psbank, bk, bki, colf=None):
            tt = []
            p.wait(VE, p.tok(TE))
            for dc in range(NDC):
                p.eng[VE].drain()
                srccol = colf(dc) if colf is not None else modcol(vidx, dc, b)
                i = p.eng[VE].tensor_copy(bcg[:, dc:dc + 1], srccol)
                tt.append(p.inc(VE, i))
            t_row = None
            for g in range(2):
                bk.pre_write(bki, tuple(tt) + ((t_row,) if t_row else ()))
                ins = nc.tensor.transpose(psbank[:4, 0:128], bcg[:, g * 4:(g + 1) * 4],
                                          eye[:])
                t_tr = p.inc(TE, ins)
                bk.set_write(bki, t_tr)
                bk.read(bki, SE)
                p.eng[SE].drain()
                i = p.eng[SE].copy(bcrow[:, g * 128:(g + 1) * 128], psbank[:4, 0:128])
                t_row = p.inc(SE, i)
                bk.add_reader(bki, t_row)
            for dc in range(NDC):
                g, r = dc // 4, dc % 4
                bk.pre_write(bki, (t_row,))
                ins = nc.tensor.matmul(psbank[:, 128:256], sel4[:, r * 128:(r + 1) * 128],
                                       bcrow[:, g * 128:(g + 1) * 128],
                                       start=True, stop=True)
                t_b = p.inc(TE, ins)
                bk.set_write(bki, t_b)
                bk.read(bki, SE)
                p.eng[SE].drain()
                i = p.eng[SE].copy(dst[:, dc * 128:(dc + 1) * 128], psbank[:, 128:256])
                bk.add_reader(bki, p.inc(SE, i))
            return p.tok(SE)

        def pa1_qk():
            with ExitStack() as s1:
                wt = {}
                for nm in ("q", "k"):
                    wt[nm] = (sb("whi_" + nm, [128, NDC * D], F32R, stx=s1),
                              sb("whib_" + nm, [128, NDC * D], BF16, stx=s1),
                              sb("wlob_" + nm, [128, NDC * D], BF16, stx=s1))
                xt = sb("xt_qk", [128, 3 * D], stx=s1)
                h1f = sb("h1f_qk", [128, 2 * D], stx=s1)
                h1hi = sb("h1hi_qk", [128, 2 * D], F32R, stx=s1)
                h1hib = sb("h1hib_qk", [128, 2 * D], BF16, stx=s1)
                h1lob = sb("h1lob_qk", [128, 2 * D], BF16, stx=s1)
                o_t = sb("ot_qk", [128, 2 * D], stx=s1)
                stats2 = sb("stats2_qk", [128, 8], stx=s1)
                wst = junkD[:]
                bank_ps = [psum(f"bkqk{i}", [128, 512], stx=s1) for i in range(4)]
                bk = Banks(p, bank_ps)

                t_wsplit = {"q": [], "k": []}
                prev = None
                for nm, W in (("q", WQ), ("k", WK)):
                    w_hi, w_hib, w_lob = wt[nm]
                    for dc in range(NDC):
                        dsl = slice(dc * D, (dc + 1) * D)
                        tl = p.dma(SY, wst, W[dc * 128:(dc + 1) * 128, :],
                                   waits=(prev,) if prev else ())
                        p.wait(SE, tl)
                        p.eng[SE].drain()
                        i = p.eng[SE].copy(w_hi[:, dsl], wst)
                        th = p.inc(SE, i)
                        p.waits(GE, (tl, th))
                        p.eng[GE].drain()
                        i = p.eng[GE].tensor_tensor(w_lob[:, dsl], wst,
                                                    w_hi[:, dsl].bitcast(F32), AL.subtract)
                        tlo = p.inc(GE, i)
                        p.wait(VE, th)
                        p.eng[VE].drain()
                        i = p.eng[VE].tensor_copy(w_hib[:, dsl], w_hi[:, dsl].bitcast(F32))
                        thb = p.inc(VE, i)
                        prev = tlo
                        t_wsplit[nm].append((tlo, thb))
                p.barrier()

                cons = [(), ()]
                cons3 = [(), (), ()]
                l2prev = None
                chunks = [(b, ci) for b in range(n_seq) for ci in range(NCH)]
                lnst = lambda ii: stats if ii % 2 == 0 else stats2
                t_lnp = {}
                tx_tok = {}
                xpar = lambda ii: xt[:, (ii % 3) * D:(ii % 3) * D + D]
                for pre in range(min(2, len(chunks))):
                    bp, cp = chunks[pre]
                    tx_tok[pre] = p.dma(SY, xpar(pre), X[bp, cp * CH:(cp + 1) * CH, :],
                                        slot=pre % 2)
                p.wait(SE, p.tok(VE))
                t_lnp[0] = ln_stats(xpar(0), (tx_tok[0],), stt=lnst(0))
                for idx, (b, ci) in enumerate(chunks):
                        t0 = ci * CH
                        jx = ci % 2
                        xsl = xpar(idx)
                        hsl_f = h1f[:, jx * D:(jx + 1) * D]
                        hsl_hi = h1hi[:, jx * D:(jx + 1) * D]
                        hsl_hib = h1hib[:, jx * D:(jx + 1) * D]
                        hsl_lob = h1lob[:, jx * D:(jx + 1) * D]
                        osl = o_t[:, jx * D:(jx + 1) * D]
                        gate = cons[jx]
                        stj = lnst(idx)
                        p.wait(VE, t_lnp[idx])
                        p.eng[VE].drain()
                        i = p.eng[VE].tensor_scalar(xsl, xsl, stj[:, 2:3], stj[:, 7:8],
                                                    AL.add, AL.mult)
                        t_xn = p.inc(VE, i)
                        t_h1c = []
                        for dc in range(NDC):
                            bki = dc % 2
                            off = ((dc // 2) % 2) * 128
                            bk.pre_write(bki, (t_xn,))
                            ins = nc.tensor.transpose(
                                bank_ps[bki][:, off:off + 128],
                                xsl[:, dc * 128:(dc + 1) * 128], eye[:])
                            bk.set_write(bki, p.inc(TE, ins))
                            bk.read(bki, VE)
                            p.waits(VE, gate)
                            p.eng[VE].drain()
                            i = p.eng[VE].tensor_scalar(
                                hsl_f[:, dc * 128:(dc + 1) * 128],
                                bank_ps[bki][:, off:off + 128],
                                ca1[:, dc * n_seq + b: dc * n_seq + b + 1],
                                modcol(0, dc, b), AL.mult, AL.add)
                            tt = p.inc(VE, i)
                            bk.add_reader(bki, tt)
                            t_h1c.append(tt)
                        if idx + 2 < len(chunks):
                            b2, c2 = chunks[idx + 2]
                            tx_tok[idx + 2] = p.dma(
                                SY, xpar(idx + 2), X[b2, c2 * CH:(c2 + 1) * CH, :],
                                waits=cons3[(idx + 2) % 3], slot=(idx + 2) % 2)
                        if idx + 1 < len(chunks):
                            p.wait(SE, p.tok(VE))
                            t_lnp[idx + 1] = ln_stats(xpar(idx + 1), (tx_tok[idx + 1],),
                                                      stt=lnst(idx + 1))
                        for dc in range(NDC):
                            p.dma(GE, H1S[b, ci, dc * 128:(dc + 1) * 128, :],
                                  hsl_f[:, dc * 128:(dc + 1) * 128],
                                  waits=(t_h1c[dc],), slot=4 + jx)
                        deps_h = tuple(t_h1c) + gate
                        p.waits(SE, deps_h)
                        p.eng[SE].drain()
                        i = p.eng[SE].copy(hsl_hi, hsl_f)
                        th = p.inc(SE, i)
                        p.waits(GE, deps_h + (th,))
                        p.eng[GE].drain()
                        i = p.eng[GE].tensor_tensor(hsl_lob, hsl_f,
                                                    hsl_hi.bitcast(F32), AL.subtract)
                        tlo = p.inc(GE, i)
                        p.wait(VE, th)
                        p.eng[VE].drain()
                        i = p.eng[VE].tensor_copy(hsl_hib, hsl_hi.bitcast(F32))
                        thb = p.inc(VE, i)
                        t_hs = (th, tlo, thb)

                        t_st = None
                        for nm, bko, DST in (("q", 2, QS), ("k", 0, KS)):
                            w_hi, w_hib, w_lob = wt[nm]
                            for half in range(2):
                                bki = bko + half
                                bk.pre_write(bki, t_hs)
                                for dc in range(NDC):
                                    hslc = slice(jx * D + dc * 128, jx * D + (dc + 1) * 128)
                                    wsl = slice(dc * D + half * 512, dc * D + (half + 1) * 512)
                                    p.waits(TE, t_wsplit[nm][dc])
                                    nc.tensor.matmul(bank_ps[bki][:], h1hi[:, hslc],
                                                     w_hi[:, wsl], start=(dc == 0), stop=False)
                                    nc.tensor.matmul(bank_ps[bki][:], h1hib[:, hslc],
                                                     w_lob[:, wsl], start=False, stop=False)
                                    ins = nc.tensor.matmul(bank_ps[bki][:], h1lob[:, hslc],
                                                           w_hib[:, wsl],
                                                           start=False, stop=(dc == NDC - 1))
                                t_mm = p.inc(TE, ins)
                                bk.set_write(bki, t_mm)
                            sq_toks = []
                            for half in range(2):
                                bki = bko + half
                                for hs_ in range(2):
                                    hh = half * 2 + hs_
                                    psl = bank_ps[bki][:, hs_ * 256:(hs_ + 1) * 256]
                                    bk.read(bki, SE)
                                    p.wait(SE, l2prev)
                                    p.eng[SE].drain()
                                    i = p.eng[SE].activation(junkD[:, 0:256], psl, AF.Square,
                                                             accum_out=stats[:, 8 + hh:9 + hh])
                                    t_ss = p.inc(SE, i)
                                    bk.add_reader(bki, t_ss)
                                    sq_toks.append(t_ss)
                            p.waits(VE, tuple(sq_toks))
                            rsqrt_newton(stats[:, 8:12], stats[:, 12:16],
                                         stats[:, 16:20], stats[:, 20:24])
                            for half in range(2):
                                bki = bko + half
                                for hs_ in range(2):
                                    hh = half * 2 + hs_
                                    psl = bank_ps[bki][:, hs_ * 256:(hs_ + 1) * 256]
                                    bk.read(bki, VE)
                                    p.waits(VE, gate + ((t_st,) if t_st else ()))
                                    p.eng[VE].drain()
                                    i = p.eng[VE].tensor_scalar(
                                        osl[:, hh * 256:(hh + 1) * 256], psl,
                                        stats[:, 12 + hh:13 + hh], None, AL.mult)
                                    l2prev = p.inc(VE, i)
                                    bk.add_reader(bki, l2prev)
                            t_st = p.dma(GE, DST[b, t0:t0 + CH, :], osl, waits=(l2prev,),
                                         slot=4 + jx)
                        cons[jx] = (p.tok(TE), p.tok(VE), p.tok(SE), t_st)
                        cons3[idx % 3] = (p.tok(TE), p.tok(VE), p.tok(SE))
                        if b == 0 and ci == 0:
                            tap("h1_c0", hsl_f, [p.tok(VE), p.tok(GE), p.tok(SE)])
                p.barrier()

        def pa1_v():
            with ExitStack() as s1:
                wv_t = sb("wvt", [128, NDC * D], stx=s1)
                wb_t = sb("wbt", [128, NDC * H], stx=s1)
                beta_t = sb("betat", [128, 2 * H], stx=s1)
                h1f = sb("h1f_v", [128, 2 * D], stx=s1)
                o_t = sb("ot_v", [128, 2 * D], stx=s1)
                bank_ps = [psum(f"bkv{i}", [128, 512], stx=s1) for i in range(4)]
                bk = Banks(p, bank_ps)
                t_wv = [p.dma(SY, wv_t[:, dc * D:(dc + 1) * D],
                              WV[dc * 128:(dc + 1) * 128, :]) for dc in range(NDC)]
                t_wb = [p.dma(SY, wb_t[:, dc * H:(dc + 1) * H],
                              WB[dc * 128:(dc + 1) * 128, :]) for dc in range(NDC)]
                p.barrier()
                cons = [(), ()]
                for b in range(n_seq):
                    for ci in range(NCH):
                        t0 = ci * CH
                        jx = ci % 2
                        hsl_f = h1f[:, jx * D:(jx + 1) * D]
                        osl = o_t[:, jx * D:(jx + 1) * D]
                        gate = cons[jx]
                        t_h1c = []
                        for dc in range(NDC):
                            t_h1c.append(p.dma(
                                SY, hsl_f[:, dc * 128:(dc + 1) * 128],
                                H1S[b, ci, dc * 128:(dc + 1) * 128, :],
                                waits=gate if dc == 0 else (), slot=jx))
                        for half in range(2):
                            bki = 2 + half
                            bk.pre_write(bki, tuple(t_h1c))
                            for dc in range(NDC):
                                p.wait(TE, t_wv[dc])
                                ins = nc.tensor.matmul(
                                    bank_ps[bki][:], hsl_f[:, dc * 128:(dc + 1) * 128],
                                    wv_t[:, dc * D + half * 512: dc * D + (half + 1) * 512],
                                    start=(dc == 0), stop=(dc == NDC - 1))
                            t_vm = p.inc(TE, ins)
                            bk.set_write(bki, t_vm)
                            bk.read(bki, SE)
                            p.waits(SE, gate)
                            p.eng[SE].drain()
                            i = p.eng[SE].copy(osl[:, half * 512:(half + 1) * 512],
                                               bank_ps[bki][:])
                            bk.add_reader(bki, p.inc(SE, i))
                        bk.pre_write(0, tuple(t_h1c))
                        for dc in range(NDC):
                            p.wait(TE, t_wb[dc])
                            ins = nc.tensor.matmul(
                                bank_ps[0][:, 0:H], hsl_f[:, dc * 128:(dc + 1) * 128],
                                wb_t[:, dc * H:(dc + 1) * H],
                                start=(dc == 0), stop=(dc == NDC - 1))
                        t_bm = p.inc(TE, ins)
                        bk.set_write(0, t_bm)
                        bk.read(0, SE)
                        p.waits(SE, gate)
                        p.eng[SE].drain()
                        i = p.eng[SE].activation(beta_t[:, jx * H:(jx + 1) * H],
                                                 bank_ps[0][:, 0:H], AF.Sigmoid)
                        t_bb = p.inc(SE, i)
                        bk.add_reader(0, t_bb)
                        p.dma(GE, VS[b, t0:t0 + CH, :], osl, waits=(p.tok(SE),),
                              slot=4 + jx)
                        t_st = p.dma(GE, BSc[b, t0:t0 + CH, :],
                                     beta_t[:, jx * H:(jx + 1) * H],
                                     waits=(t_bb,), slot=4 + jx)
                        cons[jx] = (p.tok(TE), p.tok(SE), t_st)
                p.barrier()

        pa1_qk()
        pa1_v()

        with ExitStack() as s2:
            wst = sb("wst_o", [128, D], stx=s2)
            s_state = sb("s_state", [128, H * 2 * DK], stx=s2)
            ga_bc = sb("ga_bc", [128, D], stx=s2)
            wo_hi = sb("wo_hi", [128, NDC * D], F32R, stx=s2)
            wo_hib = sb("wo_hib", [128, NDC * D], BF16, stx=s2)
            wo_lob = sb("wo_lob", [128, NDC * D], BF16, stx=s2)
            qt = sb("qt", [128, 2 * D], stx=s2)
            kt = sb("kt", [128, 2 * D], stx=s2)
            vt = sb("vt", [128, 2 * D], stx=s2)
            xt2 = sb("xt2", [128, 2 * D], stx=s2)
            beta_t = sb("beta_t2", [128, 2 * H], stx=s2)
            negb = sb("negb", [128, 2 * H], stx=s2)
            qf = sb("qf", [128, 2 * D], stx=s2)
            kf = sb("kf", [128, 2 * D], stx=s2)
            brow = sb("brow", [4, 2 * 128], stx=s2)
            BPOW = ["c1b", "c1tb", "cA", "cAt", "cB", "cBt", "c64t"]
            cb = {n: sb("pw_" + n, [128, 4 * 128], BF16, stx=s2) for n in BPOW}
            pbA = sb("pw_pbA", [128, 4 * 128], BF16, stx=s2)
            pbB = sb("pw_pbB", [128, 4 * 128], BF16, stx=s2)
            c1f = sb("c1f", [128, 4 * 128], stx=s2)
            c1tf = sb("c1tf", [128, 4 * 128], stx=s2)
            m_t = sb("m_t", [128, 4 * 128], stx=s2)
            pfA = sb("pfA", [128, 4 * 128], stx=s2)
            pfB = sb("pfB", [128, 4 * 128], stx=s2)
            ptfA = sb("ptfA", [128, 4 * 128], stx=s2)
            ptfB = sb("ptfB", [128, 4 * 128], stx=s2)
            w_t = sb("w_t", [128, 4 * 128], stx=s2)
            gm_t = sb("gm_t", [128, 2 * 4 * 128], stx=s2)
            pf_fin = sb("pf_fin", [128, 2 * 4 * 128], stx=s2)
            bmu = sb("bmu", [128, 4 * 128], stx=s2)
            r_t = sb("r_t", [128, 4 * 256], stx=s2)
            u_t = sb("u_t", [128, 4 * 256], stx=s2)
            eyeb = sb("eyeb", [128, 128], BF16, stx=s2)
            eye2 = sb("eye2", [128, 128], stx=s2)
            on_t = sb("on_t", [128, 4 * 256], stx=s2)
            onf_hi = sb("onf_hi", [128, D], F32R, stx=s2)
            onf_hib = sb("onf_hib", [128, D], BF16, stx=s2)
            onf_lob = sb("onf_lob", [128, D], BF16, stx=s2)
            bank_ps = [psum(f"d{i}", [128, 512], stx=s2) for i in range(8)]
            bk = Banks(p, bank_ps)

            t_wsplit = []
            prev = None
            for dc in range(NDC):
                dsl = slice(dc * D, (dc + 1) * D)
                tl = p.dma(SY, wst[:], WO[dc * 128:(dc + 1) * 128, :],
                           waits=(prev,) if prev else ())
                p.wait(SE, tl)
                p.eng[SE].drain()
                i = p.eng[SE].copy(wo_hi[:, dsl], wst[:])
                th = p.inc(SE, i)
                p.waits(GE, (tl, th))
                p.eng[GE].drain()
                i = p.eng[GE].tensor_tensor(wo_lob[:, dsl], wst[:],
                                            wo_hi[:, dsl].bitcast(F32), AL.subtract)
                tlo = p.inc(GE, i)
                p.wait(VE, th)
                p.eng[VE].drain()
                i = p.eng[VE].tensor_copy(wo_hib[:, dsl], wo_hi[:, dsl].bitcast(F32))
                thb = p.inc(VE, i)
                prev = tlo
                t_wsplit.append((tlo, thb))
            p.waits(GE, (t_consts,))
            p.eng[GE].drain()
            p.eng[GE].tensor_copy(eyeb[:], eye[:])
            p.eng[GE].drain()
            i = p.eng[GE].tensor_scalar(eye2[:], eye[:], 2.0, None, AL.mult)
            p.inc(GE, i)
            p.barrier()

            cons = [(), ()]
            rms_prev = None
            pend = {}

            BNK = [0, 1, 6, 7]
            A = lambda h: 2 + h
            s1 = lambda h, a, bnd: bank_ps[A(h)][:, a:bnd]
            sB = lambda h, a, bnd: bank_ps[BNK[h]][:, a:bnd]
            h128 = lambda t, h: t[:, h * 128:(h + 1) * 128]
            h256 = lambda t, h: t[:, h * 256:(h + 1) * 256]
            s0_ = lambda h: s_state[:, h * 512:(h + 1) * 512]

            def chain_part(b, ci):
                t0 = ci * CH
                jx = ci % 2
                oD = jx * D
                gate = cons[jx]
                tq = p.dma(SY, qt[:, oD:oD + D], QS[b, t0:t0 + CH, :], waits=gate, slot=jx)
                tk = p.dma(SY, kt[:, oD:oD + D], KS[b, t0:t0 + CH, :], slot=jx)
                tv = p.dma(SY, vt[:, oD:oD + D], VS[b, t0:t0 + CH, :], slot=jx)
                tb = p.dma(SY, beta_t[:, jx * H:(jx + 1) * H], BSc[b, t0:t0 + CH, :], slot=jx)
                tx = p.dma(SY, xt2[:, oD:oD + D], X[b, t0:t0 + CH, :], slot=jx)
                betac = beta_t[:, jx * H:(jx + 1) * H]
                negc = negb[:, jx * H:(jx + 1) * H]
                browc = brow[:, jx * 128:(jx + 1) * 128]
                gmc = gm_t[:, jx * 512:(jx + 1) * 512]
                pfc = pf_fin[:, jx * 512:(jx + 1) * 512]
                kfh = lambda h, half: kf[:, oD + h * 256 + half * 128:
                                         oD + h * 256 + (half + 1) * 128]
                qfh = lambda h, half: qf[:, oD + h * 256 + half * 128:
                                         oD + h * 256 + (half + 1) * 128]
                yield
                for srct, dst, tok, base in ((kt, kf, tk, 0), (qt, qf, tq, 256)):
                    for dc in range(NDC):
                        h, half = dc // 2, dc % 2
                        bki = BNK[h]
                        off = base + half * 128
                        bk.pre_write(bki, (tok,))
                        ins = nc.tensor.transpose(
                            bank_ps[bki][:, off:off + 128],
                            srct[:, oD + dc * 128:oD + (dc + 1) * 128], eye[:])
                        bk.set_write(bki, p.inc(TE, ins))
                        bk.read(bki, SE)
                        p.eng[SE].drain()
                        i = p.eng[SE].copy(dst[:, oD + dc * 128:oD + (dc + 1) * 128],
                                           bank_ps[bki][:, off:off + 128])
                        bk.add_reader(bki, p.inc(SE, i))
                    yield
                bk.pre_write(0, (tb,))
                ins = nc.tensor.transpose(bank_ps[0][:4, 0:128], betac, eye[:])
                bk.set_write(0, p.inc(TE, ins))
                bk.read(0, SE)
                p.eng[SE].drain()
                i = p.eng[SE].copy(browc, bank_ps[0][:4, 0:128])
                t_brow = p.inc(SE, i)
                bk.add_reader(0, t_brow)
                t_tr = p.tok(SE)
                p.waits(VE, (tb, tv))
                p.eng[VE].drain()
                i = p.eng[VE].tensor_scalar(negc, betac, -1.0, None, AL.mult)
                t_nb = p.inc(VE, i)
                for h in range(H):
                    p.eng[VE].drain()
                    i = p.eng[VE].tensor_scalar(vt[:, oD + h * 256:oD + (h + 1) * 256],
                                                vt[:, oD + h * 256:oD + (h + 1) * 256],
                                                betac[:, h:h + 1], None, AL.mult)
                t_vb = p.inc(VE, i)
                yield
                for h in range(H):
                    bk.pre_write(BNK[h], (t_brow,))
                    ins = nc.tensor.matmul(sB(h, 0, 128), sel4[:, h * 128:(h + 1) * 128],
                                           browc, start=True, stop=True)
                    bk.set_write(BNK[h], p.inc(TE, ins))
                for h in range(H):
                    bk.read(BNK[h], VE)
                    p.eng[VE].drain()
                    i = p.eng[VE].scalar_tensor_tensor(h128(bmu, h), sB(h, 0, 128), -1.0,
                                                       masku[:], AL.mult, AL.mult)
                    bk.add_reader(BNK[h], p.inc(VE, i))
                yield
                for h in range(H):
                    bk.pre_write(BNK[h], (t_tr,))
                    nc.tensor.matmul(sB(h, 128, 256), kfh(h, 0), kfh(h, 0),
                                     start=True, stop=False)
                    ins = nc.tensor.matmul(sB(h, 128, 256), kfh(h, 1), kfh(h, 1),
                                           start=False, stop=True)
                    bk.set_write(BNK[h], p.inc(TE, ins))
                t_c1 = {}
                t_c1t = {}
                for h in range(H):
                    bk.read(BNK[h], VE)
                    p.eng[VE].drain()
                    i = p.eng[VE].tensor_tensor(h128(c1f, h), sB(h, 128, 256),
                                                h128(bmu, h), AL.mult)
                    t_c1[h] = p.inc(VE, i)
                    bk.add_reader(BNK[h], t_c1[h])
                    p.wait(VE, t_nb)
                    p.eng[VE].drain()
                    i = p.eng[VE].scalar_tensor_tensor(h128(c1tf, h), sB(h, 128, 256),
                                                       negc[:, h:h + 1], maskl[:],
                                                       AL.mult, AL.mult)
                    t_c1t[h] = p.inc(VE, i)
                    bk.add_reader(BNK[h], t_c1t[h])
                yield
                tok_bf = {}
                t_m = {}
                for h in range(H):
                    p.waits(GE, (t_c1[h], t_c1t[h]))
                    p.eng[GE].drain()
                    i = p.eng[GE].tensor_copy(h128(cb["c1b"], h), h128(c1f, h))
                    tok_bf[("c1b", h)] = p.inc(GE, i)
                    p.eng[GE].drain()
                    i = p.eng[GE].tensor_copy(h128(cb["c1tb"], h), h128(c1tf, h))
                    tok_bf[("c1tb", h)] = p.inc(GE, i)
                    p.eng[GE].drain()
                    i = p.eng[GE].tensor_tensor(h128(m_t, h), eye[:], h128(c1tf, h),
                                                AL.subtract)
                    t_m[h] = p.inc(GE, i)
                t_gm = {}
                for h in range(H):
                    bk.pre_write(BNK[h], ())
                    nc.tensor.matmul(sB(h, 256, 384), kfh(h, 0), qfh(h, 0),
                                     start=True, stop=False)
                    ins = nc.tensor.matmul(sB(h, 256, 384), kfh(h, 1), qfh(h, 1),
                                           start=False, stop=True)
                    bk.set_write(BNK[h], p.inc(TE, ins))
                for h in range(H):
                    bk.read(BNK[h], VE)
                    p.eng[VE].drain()
                    i = p.eng[VE].tensor_tensor(h128(gmc, h), sB(h, 256, 384), maskui[:],
                                                AL.mult)
                    t_gm[h] = p.inc(VE, i)
                    bk.add_reader(BNK[h], t_gm[h])
                t_pb = {}
                for h in range(H):
                    p.waits(GE, (tok_bf[("c1b", h)], t_consts))
                    p.eng[GE].drain()
                    i = p.eng[GE].tensor_tensor(h128(pbA, h), h128(cb["c1b"], h), eyeb[:],
                                                AL.add)
                    t_pb[h] = p.inc(GE, i)
                yield
                curb, nxtb = pbA, pbB
                cur_pair, nxt_pair = ("c1b", "c1tb"), ("cA", "cAt")
                for lvl in range(5):
                    pl, pt = cur_pair
                    nl, nt = nxt_pair
                    for h in range(H):
                        bk.pre_write(BNK[h], (tok_bf[(pl, h)], tok_bf[(pt, h)]))
                        ins = nc.tensor.matmul(sB(h, 384, 512), h128(cb[pt], h),
                                               h128(cb[pl], h), start=True, stop=True)
                        bk.set_write(BNK[h], p.inc(TE, ins))
                    for h in range(H):
                        eng = SE if h % 2 == 0 else VE
                        bk.read(BNK[h], eng)
                        p.eng[eng].drain()
                        if eng == SE:
                            i = p.eng[eng].copy(h128(cb[nl], h), sB(h, 384, 512))
                        else:
                            i = p.eng[eng].tensor_copy(h128(cb[nl], h), sB(h, 384, 512))
                        tok_bf[(nl, h)] = p.inc(eng, i)
                        bk.add_reader(BNK[h], tok_bf[(nl, h)])
                    yield
                    for h in range(H):
                        bk.pre_write(BNK[h], (tok_bf[(pl, h)], tok_bf[(pt, h)]))
                        ins = nc.tensor.matmul(sB(h, 384, 512), h128(cb[pl], h),
                                               h128(cb[pt], h), start=True, stop=True)
                        bk.set_write(BNK[h], p.inc(TE, ins))
                    for h in range(H):
                        eng = SE if h % 2 == 1 else VE
                        bk.read(BNK[h], eng)
                        p.eng[eng].drain()
                        if eng == SE:
                            i = p.eng[eng].copy(h128(cb[nt], h), sB(h, 384, 512))
                        else:
                            i = p.eng[eng].tensor_copy(h128(cb[nt], h), sB(h, 384, 512))
                        tok_bf[(nt, h)] = p.inc(eng, i)
                        bk.add_reader(BNK[h], tok_bf[(nt, h)])
                    yield
                    for h in range(H):
                        bk.pre_write(BNK[h], (t_pb[h], tok_bf[(nt, h)]))
                        ins = nc.tensor.matmul(sB(h, 384, 512), h128(cb[nt], h),
                                               h128(curb, h), start=True, stop=True)
                        bk.set_write(BNK[h], p.inc(TE, ins))
                    for h in range(H):
                        bk.read(BNK[h], VE)
                        p.eng[VE].drain()
                        i = p.eng[VE].tensor_tensor(h128(nxtb, h), sB(h, 384, 512),
                                                    h128(curb, h), AL.add)
                        t_pb[h] = p.inc(VE, i)
                        bk.add_reader(BNK[h], t_pb[h])
                    yield
                    curb, nxtb = nxtb, curb
                    cur_pair, nxt_pair = nxt_pair, (("cB", "cBt") if nxt_pair[0] == "cA"
                                                    else ("cA", "cAt"))
                for h in range(H):
                    pl, pt = cur_pair
                    bk.pre_write(BNK[h], (tok_bf[(pl, h)], tok_bf[(pt, h)]))
                    ins = nc.tensor.matmul(sB(h, 384, 512), h128(cb[pl], h),
                                           h128(cb[pt], h), start=True, stop=True)
                    bk.set_write(BNK[h], p.inc(TE, ins))
                for h in range(H):
                    bk.read(BNK[h], SE)
                    p.eng[SE].drain()
                    i = p.eng[SE].copy(h128(cb["c64t"], h), sB(h, 384, 512))
                    tok_bf[("c64t", h)] = p.inc(SE, i)
                    bk.add_reader(BNK[h], tok_bf[("c64t", h)])
                yield
                t_pf = {}
                t_ptf = {}
                for h in range(H):
                    bk.pre_write(BNK[h], (t_pb[h], tok_bf[("c64t", h)]))
                    ins = nc.tensor.matmul(sB(h, 384, 512), h128(cb["c64t"], h),
                                           h128(curb, h), start=True, stop=True)
                    bk.set_write(BNK[h], p.inc(TE, ins))
                for h in range(H):
                    bk.read(BNK[h], VE)
                    p.eng[VE].drain()
                    i = p.eng[VE].tensor_tensor(h128(pfA, h), sB(h, 384, 512),
                                                h128(curb, h), AL.add)
                    t_pf[h] = p.inc(VE, i)
                    bk.add_reader(BNK[h], t_pf[h])
                yield
                for h in range(H):
                    bk.pre_write(BNK[h], ())
                    nc.tensor.matmul(sB(h, 384, 512), h128(curb, h), h128(cb["c64t"], h),
                                     start=True, stop=False)
                    ins = nc.tensor.matmul(sB(h, 384, 512), h128(curb, h), eyeb[:],
                                           start=False, stop=True)
                    bk.set_write(BNK[h], p.inc(TE, ins))
                for h in range(H):
                    bk.read(BNK[h], SE)
                    p.eng[SE].drain()
                    i = p.eng[SE].copy(h128(ptfA, h), sB(h, 384, 512))
                    t_ptf[h] = p.inc(SE, i)
                    bk.add_reader(BNK[h], t_ptf[h])
                yield
                pf, pf2 = pfA, pfB
                ptf, ptf2 = ptfA, ptfB
                for it in range(2):
                    t_w = {}
                    for h in range(H):
                        bk.pre_write(BNK[h], (t_pf[h], t_m[h]))
                        ins = nc.tensor.matmul(sB(h, 384, 512), h128(m_t, h), h128(pf, h),
                                               start=True, stop=True)
                        bk.set_write(BNK[h], p.inc(TE, ins))
                    for h in range(H):
                        bk.read(BNK[h], VE)
                        p.eng[VE].drain()
                        i = p.eng[VE].scalar_tensor_tensor(h128(w_t, h), sB(h, 384, 512),
                                                           -1.0, eye2[:], AL.mult, AL.add)
                        t_w[h] = p.inc(VE, i)
                        bk.add_reader(BNK[h], t_w[h])
                    yield
                    for h in range(H):
                        bk.pre_write(BNK[h], (t_w[h], t_ptf[h]))
                        ins = nc.tensor.matmul(sB(h, 384, 512), h128(ptf, h), h128(w_t, h),
                                               start=True, stop=True)
                        bk.set_write(BNK[h], p.inc(TE, ins))
                    for h in range(H):
                        bk.read(BNK[h], SE)
                        p.eng[SE].drain()
                        dstp = h128(pfc, h) if it == 1 else h128(pf2, h)
                        i = p.eng[SE].copy(dstp, sB(h, 384, 512))
                        t_pf[h] = p.inc(SE, i)
                        bk.add_reader(BNK[h], t_pf[h])
                    yield
                    if it == 0:
                        for h in range(H):
                            bk.pre_write(BNK[h], ())
                            ins = nc.tensor.matmul(sB(h, 384, 512), h128(w_t, h),
                                                   h128(ptf, h), start=True, stop=True)
                            bk.set_write(BNK[h], p.inc(TE, ins))
                        for h in range(H):
                            bk.read(BNK[h], SE)
                            p.eng[SE].drain()
                            i = p.eng[SE].copy(h128(ptf2, h), sB(h, 384, 512))
                            t_ptf[h] = p.inc(SE, i)
                            bk.add_reader(BNK[h], t_ptf[h])
                        yield
                        ptf, ptf2 = ptf2, ptf
                    pf, pf2 = pf2, pf
                if b == 0 and ci == 0:
                    tap("c1_h0", c1f[:, 0:128], [p.tok(VE)])
                    tap("P_h0", pfc[:, 0:128], [p.tok(SE)])
                pend[(b, ci)] = dict(t_pf=dict(t_pf), t_gm=dict(t_gm), t_vb=t_vb,
                                     tx=tx, gmc=gmc, pfc=pfc, negc=negc,
                                     kfh=kfh, qfh=qfh, oD=oD, jx=jx)

            def state_part(b, ci):
                nonlocal rms_prev
                st = pend.pop((b, ci))
                t0 = ci * CH
                jx, oD = st["jx"], st["oD"]
                kfh, qfh = st["kfh"], st["qfh"]
                gmc, pfc, negc = st["gmc"], st["pfc"], st["negc"]
                t_pf, t_gm, t_vb, tx = st["t_pf"], st["t_gm"], st["t_vb"], st["tx"]
                t_sq_heads = []
                t_r = {}
                t_u = {}
                for h in range(H):
                    bk.pre_write(A(h), (t_vb,))
                    nc.tensor.matmul(s1(h, 0, 256), kfh(h, 0), s0_(h)[:, 0:256],
                                     start=True, stop=False)
                    ins = nc.tensor.matmul(s1(h, 0, 256), kfh(h, 1), s0_(h)[:, 256:512],
                                           start=False, stop=True)
                    bk.set_write(A(h), p.inc(TE, ins))
                    bk.read(A(h), VE)
                    p.eng[VE].drain()
                    i = p.eng[VE].scalar_tensor_tensor(h256(r_t, h), s1(h, 0, 256),
                                                       negc[:, h:h + 1],
                                                       vt[:, oD + h * 256:oD + (h + 1) * 256],
                                                       AL.mult, AL.add)
                    t_r[h] = p.inc(VE, i)
                    bk.add_reader(A(h), t_r[h])
                    yield
                for h in range(H):
                    bk.pre_write(A(h), (t_pf[h], t_r[h]))
                    ins = nc.tensor.matmul(s1(h, 256, 512), h128(pfc, h), h256(r_t, h),
                                           start=True, stop=True)
                    bk.set_write(A(h), p.inc(TE, ins))
                    bk.read(A(h), SE)
                    p.eng[SE].drain()
                    i = p.eng[SE].copy(h256(u_t, h), s1(h, 256, 512))
                    t_u[h] = p.inc(SE, i)
                    bk.add_reader(A(h), t_u[h])
                    yield
                for h in range(H):
                    bk.pre_write(A(h), (t_u[h], t_gm[h]))
                    nc.tensor.matmul(s1(h, 0, 256), qfh(h, 0), s0_(h)[:, 0:256],
                                     start=True, stop=False)
                    nc.tensor.matmul(s1(h, 0, 256), qfh(h, 1), s0_(h)[:, 256:512],
                                     start=False, stop=False)
                    ins = nc.tensor.matmul(s1(h, 0, 256), h128(gmc, h), h256(u_t, h),
                                           start=False, stop=True)
                    bk.set_write(A(h), p.inc(TE, ins))
                    bk.read(A(h), SE)
                    p.wait(SE, rms_prev)
                    p.eng[SE].drain()
                    i = p.eng[SE].copy(h256(on_t, h), s1(h, 0, 256))
                    bk.add_reader(A(h), p.inc(SE, i))
                    p.eng[SE].drain()
                    i = p.eng[SE].activation(junkD[:, 0:256], s1(h, 0, 256), AF.Square,
                                             accum_out=stats[:, 8 + h:9 + h])
                    t_ss = p.inc(SE, i)
                    bk.add_reader(A(h), t_ss)
                    t_sq_heads.append(t_ss)
                    yield
                for h in range(H):
                    bk.pre_write(A(h), (t_u[h],))
                    for half in range(2):
                        ins = nc.tensor.matmul(
                            bank_ps[A(h)][:, half * 256:(half + 1) * 256],
                            kt[:, oD + h * 256 + half * 128: oD + h * 256 + half * 128 + 128],
                            h256(u_t, h), start=True, stop=True)
                    bk.set_write(A(h), p.inc(TE, ins))
                    bk.read(A(h), VE)
                    p.eng[VE].drain()
                    i = p.eng[VE].tensor_tensor(s0_(h)[:], s0_(h)[:],
                                                bank_ps[A(h)][:, 0:512], AL.add)
                    bk.add_reader(A(h), p.inc(VE, i))
                    yield
                if b == 0 and ci == 0:
                    tap("u_h0", u_t[:, 0:256], [p.tok(SE)])
                p.waits(VE, tuple(t_sq_heads))
                p.eng[VE].drain()
                i = p.eng[VE].tensor_scalar(stats[:, 8:12], stats[:, 8:12],
                                            1.0 / DK, 1e-5, AL.mult, AL.add)
                p.inc(VE, i)
                rsqrt_newton(stats[:, 8:12], stats[:, 12:16],
                             stats[:, 16:20], stats[:, 20:24])
                for h in range(H):
                    p.eng[VE].drain()
                    i = p.eng[VE].scalar_tensor_tensor(
                        on_t[:, h * 256:(h + 1) * 256], on_t[:, h * 256:(h + 1) * 256],
                        stats[:, 12 + h:13 + h], gn_bc[:], AL.mult, AL.mult)
                rms_prev = p.inc(VE, i)
                if b == 0 and ci == 0:
                    tap("on_h0", on_t[:, 0:256], [rms_prev])
                yield
                for half in range(2):
                    for h in range(H):
                        bki = A(h)
                        off = half * 128
                        bk.pre_write(bki, (rms_prev,))
                        ins = nc.tensor.transpose(
                            bank_ps[bki][:, off:off + 128],
                            on_t[:, h * 256 + half * 128: h * 256 + half * 128 + 128],
                            eye[:])
                        bk.set_write(bki, p.inc(TE, ins))
                        dsl = slice(h * 256 + half * 128, h * 256 + half * 128 + 128)
                        bk.read(bki, SE)
                        p.eng[SE].drain()
                        i = p.eng[SE].copy(onf_hi[:, dsl], bank_ps[bki][:, off:off + 128])
                        t_of = p.inc(SE, i)
                        bk.add_reader(bki, t_of)
                        bk.read(bki, VE)
                        p.wait(VE, t_of)
                        p.eng[VE].drain()
                        i = p.eng[VE].scalar_tensor_tensor(
                            onf_lob[:, dsl], bank_ps[bki][:, off:off + 128], 1.0,
                            onf_hi[:, dsl].bitcast(F32), AL.mult, AL.subtract)
                        bk.add_reader(bki, p.inc(VE, i))
                        p.waits(GE, (t_of,))
                        p.eng[GE].drain()
                        i = p.eng[GE].tensor_copy(onf_hib[:, dsl],
                                                  onf_hi[:, dsl].bitcast(F32))
                        p.inc(GE, i)
                    yield
                t_split_all = (p.tok(SE), p.tok(VE), p.tok(GE))
                for half in range(2):
                    bki = 4 + half
                    bk.pre_write(bki, t_split_all)
                    for vc in range(NDC):
                        csl = slice(vc * 128, (vc + 1) * 128)
                        wsl = slice(vc * D + half * 512, vc * D + (half + 1) * 512)
                        p.waits(TE, t_wsplit[vc])
                        nc.tensor.matmul(bank_ps[bki][:], onf_hi[:, csl], wo_hi[:, wsl],
                                         start=(vc == 0), stop=False)
                        nc.tensor.matmul(bank_ps[bki][:], onf_hib[:, csl], wo_lob[:, wsl],
                                         start=False, stop=False)
                        ins = nc.tensor.matmul(bank_ps[bki][:], onf_lob[:, csl],
                                               wo_hib[:, wsl],
                                               start=False, stop=(vc == NDC - 1))
                        if vc in (2, 5):
                            yield
                    bk.set_write(bki, p.inc(TE, ins))
                    bk.read(bki, VE)
                    p.eng[VE].drain()
                    i = p.eng[VE].tensor_tensor(qt[:, oD + half * 512:oD + (half + 1) * 512],
                                                bank_ps[bki][:],
                                                ga_bc[:, half * 512:(half + 1) * 512], AL.mult)
                    t_x1a = p.inc(VE, i)
                    bk.add_reader(bki, t_x1a)
                    p.waits(GE, (t_x1a, tx))
                    p.eng[GE].drain()
                    i = p.eng[GE].tensor_tensor(qt[:, oD + half * 512:oD + (half + 1) * 512],
                                                qt[:, oD + half * 512:oD + (half + 1) * 512],
                                                xt2[:, oD + half * 512:oD + (half + 1) * 512],
                                                AL.add)
                    p.inc(GE, i)
                    yield
                t_x1st = p.dma(GE, X1S[b, t0:t0 + CH, :], qt[:, oD:oD + D],
                               waits=(p.tok(GE),), slot=4 + jx)
                if b == 0 and ci == 0:
                    tap("x1_c0", qt[:, 0:D], [p.tok(GE)])
                cons[jx] = (p.tok(TE), p.tok(VE), p.tok(SE), p.tok(GE), t_x1st)

            def drive(gc, gs):
                if gc is not None:
                    try:
                        next(gc)
                    except StopIteration:
                        gc = None
                done_c = gc is None
                done_s = gs is None
                while not (done_c and done_s):
                    if not done_s:
                        try:
                            next(gs)
                        except StopIteration:
                            done_s = True
                    if not done_c:
                        try:
                            next(gc)
                        except StopIteration:
                            done_c = True

            for b in range(n_seq):
                i = p.eng[VE].memset(s_state[:], 0.0)
                p.inc(VE, i)
                build_gate_bc(2, ga_bc, b, bank_ps[0], bk, 0)
                p.barrier()
                cons = [(), ()]
                drive(chain_part(b, 0), None)
                for ci in range(NCH):
                    gc2 = chain_part(b, ci + 1) if ci + 1 < NCH else None
                    drive(gc2, state_part(b, ci))
            p.barrier()

        with ExitStack() as s3:
            D2, D4 = D // 2, D // 4
            wst = junkD[:]
            cn1_hi = sb("cn1_hi", [128, NDC * D], F32R, stx=s3)
            cn1_hib = sb("cn1_hib", [128, NDC * D], BF16, stx=s3)
            cn1_lob = sb("cn1_lob", [128, NDC * D], BF16, stx=s3)
            cn2_hi = sb("cn2_hi", [128, NDC * D2], F32R, stx=s3)
            cn2_hib = sb("cn2_hib", [128, NDC * D2], BF16, stx=s3)
            cn2_lob = sb("cn2_lob", [128, NDC * D2], BF16, stx=s3)
            cn3_t = sb("cn3_t", [128, 4 * D4], stx=s3)
            cn4_t = sb("cn4_t", [128, 2], stx=s3)
            cnb = sb("cnb", [128, 16], stx=s3)
            gb_bc = sb("gb_bc", [128, D], stx=s3)
            TH = max(T // 4, CH)
            NCHH = TH // CH
            h2res = sb("h2res", [128, TH * NDC], stx=s3)
            h2sum = sb("h2sum", [128, 16], stx=s3)
            if TH * NDC >= 4 * D:
                mod_s = h2res[:, 2 * D:3 * D]
                mod_b = h2res[:, 3 * D:4 * D]

            TILE = min(512, TH)
            NPT = TH // TILE
            if TH * NDC < 2 * D:
                x1dev = sb("x1dev", [128, 2 * D], stx=s3)
            if TH * NDC < 4 * D:
                bcdev = sb("bcdev", [128, 2 * D], stx=s3)
                mod_s = bcdev[:, 0:D]
                mod_b = bcdev[:, D:2 * D]
            onesc = stats[:, 16:17]
            spl_hi = sb("spl_hi", [128, NDC * TILE], F32R, stx=s3)
            spl_hib = sb("spl_hib", [128, NDC * TILE], BF16, stx=s3)
            spl_lob = sb("spl_lob", [128, NDC * TILE], BF16, stx=s3)
            g1_hi = sb("g1_hi", [128, NDC * TILE], F32R, stx=s3)
            g1_hib = sb("g1_hib", [128, NDC * TILE], BF16, stx=s3)
            g1_lob = sb("g1_lob", [128, NDC * TILE], BF16, stx=s3)
            g2f = sb("g2f", [128, 4 * TILE], stx=s3)
            tf_row = junkD[:1, 0:TILE]
            assert 2 * TILE <= 1024
            gcw_t = sb("gcw", [128, 1024], stx=s3)
            gcw = gcw_t[:]
            g3f = gcw_t[:, 0:2 * TILE]
            if 4 * TILE >= 2 * D:
                x1b = g2f[:, 0:D]
                outb = g2f[:, D:2 * D]
            else:
                x1b = sb("x1b_dev", [128, D], stx=s3)[:]
                outb = sb("outb_dev", [128, D], stx=s3)[:]
                outb2 = sb("outb2_dev", [128, 2 * D], stx=s3)
            tw1 = g2f[:, 0:TILE]
            tw2 = g2f[:, TILE:2 * TILE]
            bank_ps = [psum(f"f{i}", [128, 512], stx=s3) for i in range(6)]
            bk = Banks(p, bank_ps)

            def load_split(Wd, hi, hib, lob, ncols, nchunks):
                toks = []
                prev = None
                for dc in range(nchunks):
                    dsl = slice(dc * ncols, (dc + 1) * ncols)
                    tl = p.dma(SY, wst[:, 0:ncols], Wd[dc * 128:(dc + 1) * 128, :],
                               waits=(prev,) if prev else ())
                    p.wait(SE, tl)
                    p.eng[SE].drain()
                    i = p.eng[SE].copy(hi[:, dsl], wst[:, 0:ncols])
                    th = p.inc(SE, i)
                    p.waits(GE, (tl, th))
                    p.eng[GE].drain()
                    i = p.eng[GE].tensor_tensor(lob[:, dsl], wst[:, 0:ncols],
                                                hi[:, dsl].bitcast(F32), AL.subtract)
                    tlo = p.inc(GE, i)
                    p.wait(VE, th)
                    p.eng[VE].drain()
                    i = p.eng[VE].tensor_copy(hib[:, dsl], hi[:, dsl].bitcast(F32))
                    thb = p.inc(VE, i)
                    prev = tlo
                    toks.append((tlo, thb))
                return toks

            t_cn1 = load_split(CW[0], cn1_hi, cn1_hib, cn1_lob, D, NDC)
            p.barrier()
            t_cn2 = load_split(CW[1], cn2_hi, cn2_hib, cn2_lob, D2, NDC)
            t_cn3 = [p.dma(SY, cn3_t[:, dc * D4:(dc + 1) * D4],
                           CW[2][dc * 128:(dc + 1) * 128, :]) for dc in range(4)]
            t_cn4 = [p.dma(SY, cn4_t[:, dc:dc + 1], CW[3][dc * 128:(dc + 1) * 128, :])
                     for dc in range(2)]
            for dc in range(8):
                p.dma(SY, cnb[:, dc:dc + 1], CB[0][dc * 128:(dc + 1) * 128])
            for dc in range(4):
                p.dma(SY, cnb[:, 8 + dc:9 + dc], CB[1][dc * 128:(dc + 1) * 128])
            for dc in range(2):
                p.dma(SY, cnb[:, 12 + dc:13 + dc], CB[2][dc * 128:(dc + 1) * 128])
            t_cb4 = p.dma(SY, cnb[:1, 14:15], CB[3][:])
            t_cnb = p.tok("dma3")
            p.barrier()

            spl_war = None
            i = p.eng[VE].memset(onesc, 1.0)
            p.inc(VE, i)

            def pb_seq(b):
                nonlocal spl_war
                i = p.eng[VE].memset(h2sum[:], 0.0)
                p.inc(VE, i)
                build_gate_bc(5, gb_bc, b, bank_ps[0], bk, 0)
                build_gate_bc(0, mod_s, b, bank_ps[0], bk, 0,
                              colf=lambda dc: cb1[:, dc * n_seq + b:dc * n_seq + b + 1])
                build_gate_bc(3, mod_b, b, bank_ps[0], bk, 0)
                t_bar = p.barrier()
                b1gate = (t_bar,)
                t_h2st = {}
                if TH * NDC >= 2 * D:
                    x1v = lambda j: h2res[:, (j % 2) * D:(j % 2) * D + D]
                else:
                    x1v = lambda j: x1dev[:, (j % 2) * D:(j % 2) * D + D]
                t_lnp = {}
                tx1p = {0: p.dma(SY, x1v(0), X1S[b, 0:CH, :], waits=(t_bar,))}
                t_lnp[0] = ln_stats(x1v(0), (tx1p[0],), co=0)
                for ci in range(NCH):
                    t0 = ci * CH
                    j = ci % 2
                    xb = x1v(j)
                    co = j * 8
                    ob = (g2f[:, j * D:(j + 1) * D] if 4 * TILE >= 2 * D
                          else outb2[:, j * D:(j + 1) * D])
                    p.wait(VE, t_lnp[ci])
                    p.eng[VE].drain()
                    p.eng[VE].tensor_scalar(xb, xb, stats[:, co + 2:co + 3],
                                            stats[:, co + 7:co + 8], AL.add, AL.mult)
                    p.eng[VE].drain()
                    p.eng[VE].tensor_tensor(xb, xb, mod_s, AL.mult)
                    p.eng[VE].drain()
                    i = p.eng[VE].tensor_tensor(xb, xb, mod_b, AL.add)
                    t_xn = p.inc(VE, i)
                    if ci + 1 < NCH:
                        tx1p[ci + 1] = p.dma(SY, x1v(ci + 1),
                                             X1S[b, t0 + CH:t0 + 2 * CH, :], waits=b1gate)
                        t_lnp[ci + 1] = ln_stats(x1v(ci + 1), (tx1p[ci + 1],),
                                                 co=(1 - j) * 8)
                    for half in range(2):
                        rb = 4 + half
                        if ci == 0:
                            bk.pre_write(rb, (t_xn,))
                        p.waits(TE, (t_xn,))
                        ins = nc.tensor.matmul(bank_ps[rb][:1, 0:512], onesc,
                                               xb[:, half * 512:(half + 1) * 512],
                                               start=(ci == 0), stop=(ci == NCH - 1))
                        if ci == NCH - 1:
                            bk.set_write(rb, p.inc(TE, ins))
                    for dc in range(NDC):
                        bki = dc % 2
                        off = ((dc // 2) % 2) * 128
                        bk.pre_write(bki, (t_xn,))
                        ins = nc.tensor.transpose(bank_ps[bki][:, off:off + 128],
                                                  xb[:, dc * 128:(dc + 1) * 128], eye[:])
                        bk.set_write(bki, p.inc(TE, ins))
                        bk.read(bki, SE)
                        if dc == 0:
                            p.wait(SE, t_h2st.get(ci - 2))
                        p.eng[SE].drain()
                        i = p.eng[SE].copy(ob[:, dc * 128:(dc + 1) * 128],
                                           bank_ps[bki][:, off:off + 128])
                        t_h2 = p.inc(SE, i)
                        bk.add_reader(bki, t_h2)
                        t_h2st[ci] = p.dma(GE, H2S[b, ci, dc * 128:(dc + 1) * 128, :],
                                           ob[:, dc * 128:(dc + 1) * 128], waits=(t_h2,),
                                           slot=4 + ci % 2)
                    b1gate = (p.tok(TE), p.tok(VE), p.tok(SE))
                rowb = junkD[:1, 0:D]
                for half in range(2):
                    rb = 4 + half
                    bk.read(rb, SE)
                    p.eng[SE].drain()
                    i = p.eng[SE].copy(rowb[:, half * 512:(half + 1) * 512],
                                       bank_ps[rb][:1, 0:512])
                    bk.add_reader(rb, p.inc(SE, i))
                t_row = p.tok(SE)
                for dc in range(NDC):
                    bki = dc % 2
                    bk.pre_write(bki, (t_row,))
                    ins = nc.tensor.transpose(bank_ps[bki][:, 0:1],
                                              rowb[:, dc * 128:(dc + 1) * 128],
                                              eye[:1, 0:1])
                    bk.set_write(bki, p.inc(TE, ins))
                    bk.read(bki, SE)
                    p.eng[SE].drain()
                    i = p.eng[SE].copy(h2sum[:, 8 + dc:9 + dc], bank_ps[bki][:, 0:1])
                    bk.add_reader(bki, p.inc(SE, i))
                t_bar = p.barrier()
                tap("h2sum" if b == 0 else "_skip", h2sum[:, 8:16],
                    [p.tok(VE), p.tok(SE)])

                p.eng[VE].drain()
                i = p.eng[VE].tensor_scalar(h2sum[:, 0:8], h2sum[:, 8:16], 1.0 / T, None,
                                            AL.mult)
                t_mean = p.inc(VE, i)
                g1c = h2sum[:, 8:12]
                g2c = h2sum[:, 12:14]
                t_mv = None
                t_act_prev = None
                for m in range(4):
                    ldt = []
                    for dc in range(NDC):
                        ldt.append(p.dma(SY, gcw[:, dc * 128:(dc + 1) * 128],
                                         GW[0][dc * 128:(dc + 1) * 128, m * 128:(m + 1) * 128],
                                         waits=(t_mv,) if (t_mv and dc == 0) else ()))
                    bk.pre_write(4, (t_mean,))
                    p.waits(TE, tuple(ldt))
                    for dc in range(NDC):
                        ins = nc.tensor.matmul(bank_ps[4][:, m:m + 1],
                                               gcw[:, dc * 128:(dc + 1) * 128],
                                               h2sum[:, dc:dc + 1],
                                               start=(dc == 0), stop=(dc == NDC - 1))
                    t_mv = p.inc(TE, ins)
                    bk.set_write(4, t_mv)
                    tb1 = p.dma(SY, stats[:, 20:21], GB[0][m * 128:(m + 1) * 128],
                                waits=(t_act_prev,) if t_act_prev else ())
                    bk.read(4, SE)
                    p.waits(SE, (tb1,))
                    p.eng[SE].drain()
                    i = p.eng[SE].activation(g1c[:, m:m + 1], bank_ps[4][:, m:m + 1], AF.Gelu,
                                             bias=stats[:, 20:21], scale=1.0)
                    t_act_prev = p.inc(SE, i)
                    bk.add_reader(4, t_act_prev)
                for m in range(2):
                    ldt = []
                    for dc in range(4):
                        ldt.append(p.dma(SY, gcw[:, dc * 128:(dc + 1) * 128],
                                         GW[1][dc * 128:(dc + 1) * 128, m * 128:(m + 1) * 128],
                                         waits=(t_mv,) if dc == 0 else ()))
                    bk.pre_write(4, ())
                    p.waits(TE, tuple(ldt))
                    for dc in range(4):
                        ins = nc.tensor.matmul(bank_ps[4][:, 8 + m:9 + m],
                                               gcw[:, dc * 128:(dc + 1) * 128],
                                               g1c[:, dc:dc + 1],
                                               start=(dc == 0), stop=(dc == 3))
                    t_mv = p.inc(TE, ins)
                    bk.set_write(4, t_mv)
                    tb2 = p.dma(SY, stats[:, 20:21], GB[1][m * 128:(m + 1) * 128],
                                waits=(t_act_prev,))
                    bk.read(4, SE)
                    p.waits(SE, (tb2,))
                    p.eng[SE].drain()
                    i = p.eng[SE].activation(g2c[:, m:m + 1], bank_ps[4][:, 8 + m:9 + m],
                                             AF.Gelu, bias=stats[:, 20:21], scale=1.0)
                    t_act_prev = p.inc(SE, i)
                    bk.add_reader(4, t_act_prev)
                ldt = [p.dma(SY, gcw[:, dc * 128:dc * 128 + 1],
                             GW[2][dc * 128:(dc + 1) * 128, :], waits=(t_mv,) if dc == 0 else ())
                       for dc in range(2)]
                bk.pre_write(4, ())
                p.waits(TE, tuple(ldt))
                for dc in range(2):
                    ins = nc.tensor.matmul(bank_ps[4][:1, 12:13], gcw[:, dc * 128:dc * 128 + 1],
                                           g2c[:, dc:dc + 1], start=(dc == 0), stop=(dc == 1))
                t_mv = p.inc(TE, ins)
                bk.set_write(4, t_mv)
                tb3 = p.dma(SY, stats[:1, 20:21], GB[2][:], waits=(t_act_prev,))
                bk.read(4, SE)
                p.waits(SE, (tb3,))
                p.eng[SE].drain()
                i = p.eng[SE].activation(stats[:1, 21:22], bank_ps[4][:1, 12:13], AF.Sigmoid,
                                         bias=stats[:1, 20:21], scale=1.0)
                t_gf = p.inc(SE, i)
                bk.add_reader(4, t_gf)
                p.waits(VE, (t_gf, t_ar))
                p.eng[VE].drain()
                p.eng[VE].tensor_tensor(gfk[:, 2 * b:2 * b + 1], stats[:1, 21:22], ar_t[:],
                                        AL.mult)
                p.eng[VE].drain()
                p.eng[VE].tensor_scalar(gfk[:, 2 * b:2 * b + 1], gfk[:, 2 * b:2 * b + 1],
                                        1.4, 0.1, AL.mult, AL.add)
                p.eng[VE].drain()
                p.eng[VE].tensor_tensor(gfk[:, 2 * b:2 * b + 1], gfk[:, 2 * b:2 * b + 1],
                                        ar_t[:], AL.subtract)
                p.eng[VE].drain()
                i = p.eng[VE].tensor_scalar(gfk[:, 2 * b + 1:2 * b + 2], ar_t[:], 0.6, None,
                                            AL.mult)
                p.inc(VE, i)
                if b == 0:
                    tap("gf", stats[:1, 21:22], [p.tok(SE)])
                t_bar = p.barrier()

                t_outst = None
                for half in range(T // TH):
                  ci_base = half * NCHH
                  for cl in range(NCHH):
                      for dc in range(NDC):
                          col = (dc * NCHH + cl) * CH
                          gate_h = t_bar if (t_bar and isinstance(t_bar[0], tuple)) else (t_bar,)
                          p.dma(SY, h2res[:, col:col + CH],
                                H2S[b, ci_base + cl, dc * 128:(dc + 1) * 128, :],
                                waits=gate_h if (cl == 0 and dc == 0) else ())
                  t_ld = p.tok("dma3")
                  tw_tok = {}
                  assert NPT == 1
                  for cyc in range(3):
                    for ti in range(NPT):
                        tc0 = ti * TILE

                        def outsl(dc):
                            base = dc * NCHH * CH + tc0
                            return h2res[:, base:base + TILE]

                        prev_g1_split = None
                        if cyc == 0:
                            spl_tok = {}
                            gate0 = (t_ld, spl_war)
                            for dc in range(NDC):
                                ssl = slice(dc * TILE, (dc + 1) * TILE)
                                deps0 = gate0 if dc == 0 else ()
                                p.waits(SE, deps0)
                                p.eng[SE].drain()
                                i = p.eng[SE].copy(spl_hi[:, ssl], outsl(dc))
                                th = p.inc(SE, i)
                                p.eng[SE].drain()
                                i = p.eng[SE].copy(spl_hib[:, ssl],
                                                   spl_hi[:, ssl].bitcast(F32))
                                thb = p.inc(SE, i)
                                p.waits(VE, deps0 + (th,))
                                p.eng[VE].drain()
                                i = p.eng[VE].tensor_tensor(spl_lob[:, ssl], outsl(dc),
                                                            spl_hi[:, ssl].bitcast(F32),
                                                            AL.subtract)
                                tlo = p.inc(VE, i)
                                spl_tok[dc] = (thb, tlo)
                            dcord = list(range(NDC))
                        for m in range(NDC):
                            bki = 2 + m % 2
                            bk.pre_write(bki, tuple(t for pr in t_cn1 for t in pr))
                            for di, dc in enumerate(dcord):
                                wsl = slice(dc * D + m * 128, dc * D + m * 128 + 128)
                                ssl = slice(dc * TILE, (dc + 1) * TILE)
                                p.waits(TE, spl_tok[dc])
                                nc.tensor.matmul(bank_ps[bki][:, 0:TILE], cn1_hi[:, wsl],
                                                 spl_hi[:, ssl], start=(di == 0), stop=False)
                                nc.tensor.matmul(bank_ps[bki][:, 0:TILE], cn1_lob[:, wsl],
                                                 spl_hib[:, ssl], start=False, stop=False)
                                ins = nc.tensor.matmul(bank_ps[bki][:, 0:TILE], cn1_hib[:, wsl],
                                                       spl_lob[:, ssl], start=False,
                                                       stop=(di == NDC - 1))
                            bk.set_write(bki, p.inc(TE, ins))
                            bk.pre_write(5, ())
                            bk.read(bki, SE)
                            p.waits(SE, (t_cnb, prev_g1_split))
                            p.eng[SE].drain()
                            i = p.eng[SE].activation(bank_ps[5][:, 0:TILE],
                                                     bank_ps[bki][:, 0:TILE], AF.Gelu,
                                                     bias=cnb[:, m:m + 1], scale=1.0)
                            t_g1e = p.inc(SE, i)
                            bk.add_reader(bki, t_g1e)
                            bk.set_write(5, t_g1e)
                            bk.read(5, SE)
                            p.eng[SE].drain()
                            i = p.eng[SE].copy(g1_hi[:, m * TILE:(m + 1) * TILE],
                                               bank_ps[5][:, 0:TILE])
                            t_hi = p.inc(SE, i)
                            bk.add_reader(5, t_hi)
                            bk.read(5, VE)
                            p.wait(VE, t_hi)
                            p.eng[VE].drain()
                            i = p.eng[VE].scalar_tensor_tensor(
                                g1_lob[:, m * TILE:(m + 1) * TILE],
                                bank_ps[5][:, 0:TILE], 1.0,
                                g1_hi[:, m * TILE:(m + 1) * TILE].bitcast(F32),
                                AL.mult, AL.subtract)
                            prev_g1_split = p.inc(VE, i)
                            bk.add_reader(5, prev_g1_split)
                            p.waits(GE, (t_hi,))
                            p.eng[GE].drain()
                            i = p.eng[GE].tensor_copy(
                                g1_hib[:, m * TILE:(m + 1) * TILE],
                                g1_hi[:, m * TILE:(m + 1) * TILE].bitcast(F32))
                            p.inc(GE, i)
                        t_g1s = (p.tok(SE), p.tok(VE), p.tok(GE))
                        for m in range(4):
                            bki = 2 + m % 2
                            bk.pre_write(bki, t_g1s + tuple(t for pr in t_cn2 for t in pr))
                            for dc in range(NDC):
                                wsl = slice(dc * D2 + m * 128, dc * D2 + m * 128 + 128)
                                ssl = slice(dc * TILE, (dc + 1) * TILE)
                                nc.tensor.matmul(bank_ps[bki][:, 0:TILE], cn2_hi[:, wsl],
                                                 g1_hi[:, ssl], start=(dc == 0), stop=False)
                                nc.tensor.matmul(bank_ps[bki][:, 0:TILE], cn2_lob[:, wsl],
                                                 g1_hib[:, ssl], start=False, stop=False)
                                ins = nc.tensor.matmul(bank_ps[bki][:, 0:TILE], cn2_hib[:, wsl],
                                                       g1_lob[:, ssl], start=False,
                                                       stop=(dc == NDC - 1))
                            bk.set_write(bki, p.inc(TE, ins))
                            bk.read(bki, SE)
                            p.wait(SE, p.tok(GE))
                            p.eng[SE].drain()
                            i = p.eng[SE].activation(g2f[:, m * TILE:(m + 1) * TILE],
                                                     bank_ps[bki][:, 0:TILE], AF.Gelu,
                                                     bias=cnb[:, 8 + m:9 + m], scale=1.0)
                            bk.add_reader(bki, p.inc(SE, i))
                        t_g2 = p.tok(SE)
                        for m in range(2):
                            bki = 2 + m
                            bk.pre_write(bki, (t_g2,) + tuple(t_cn3))
                            for dc in range(4):
                                wsl = slice(dc * D4 + m * 128, dc * D4 + m * 128 + 128)
                                ins = nc.tensor.matmul(bank_ps[bki][:, 0:TILE], cn3_t[:, wsl],
                                                       g2f[:, dc * TILE:(dc + 1) * TILE],
                                                       start=(dc == 0), stop=(dc == 3))
                            bk.set_write(bki, p.inc(TE, ins))
                            bk.read(bki, SE)
                            p.eng[SE].drain()
                            i = p.eng[SE].activation(g3f[:, m * TILE:(m + 1) * TILE],
                                                     bank_ps[bki][:, 0:TILE], AF.Gelu,
                                                     bias=cnb[:, 12 + m:13 + m], scale=1.0)
                            bk.add_reader(bki, p.inc(SE, i))
                        t_g3 = p.tok(SE)
                        bk.pre_write(1, (t_g3,) + tuple(t_cn4))
                        for dc in range(2):
                            ins = nc.tensor.matmul(bank_ps[1][:1, 0:TILE], cn4_t[:, dc:dc + 1],
                                                   g3f[:, dc * TILE:(dc + 1) * TILE],
                                                   start=(dc == 0), stop=(dc == 1))
                        bk.set_write(1, p.inc(TE, ins))
                        bk.read(1, SE)
                        p.waits(SE, (t_cb4,))
                        p.eng[SE].drain()
                        i = p.eng[SE].activation(tf_row, bank_ps[1][:1, 0:TILE], AF.Sigmoid,
                                                 bias=cnb[:1, 14:15], scale=1.0)
                        t_tf = p.inc(SE, i)
                        bk.add_reader(1, t_tf)
                        p.waits(VE, (t_tf,))
                        p.eng[VE].drain()
                        i = p.eng[VE].tensor_scalar(tf_row, tf_row,
                                                    gfk[:, 2 * b + 1:2 * b + 2],
                                                    gfk[:, 2 * b:2 * b + 1], AL.mult, AL.add)
                        t_cp = p.inc(VE, i)
                        bk.pre_write(1, (t_cp,))
                        ins = nc.tensor.matmul(bank_ps[1][:, 0:TILE], ones1[:], tf_row,
                                               start=True, stop=True)
                        t_cbc = p.inc(TE, ins)
                        bk.set_write(1, t_cbc)
                        spl_war = t_cbc
                        prev_twA = {}
                        spl_new = {}
                        for pr in range(4):
                            q13 = outsl(pr)
                            q24 = outsl(pr + 4)
                            bA = g2f[:, (2 * (pr % 2)) * TILE:(2 * (pr % 2) + 1) * TILE]
                            bB = g2f[:, (2 * (pr % 2) + 1) * TILE:
                                     (2 * (pr % 2) + 2) * TILE]
                            bk.read(1, VE)
                            p.waits(VE, (prev_twA.get(pr - 2),))
                            p.eng[VE].drain()
                            p.eng[VE].tensor_tensor(bA, q24, bank_ps[1][:, 0:TILE], AL.mult)
                            p.eng[VE].drain()
                            i = p.eng[VE].tensor_tensor(bB, q13, bank_ps[1][:, 0:TILE],
                                                        AL.mult)
                            tsc = p.inc(VE, i)
                            bk.add_reader(1, tsc)
                            p.waits(GE, (tsc,))
                            p.eng[GE].drain()
                            i = p.eng[GE].tensor_tensor(q13, q13, bA,
                                                        AL.add if pr < 2 else AL.subtract)
                            tg13 = p.inc(GE, i)
                            p.eng[GE].drain()
                            i = p.eng[GE].tensor_tensor(q24, q24, bB,
                                                        AL.subtract if pr < 2 else AL.add)
                            tg24 = p.inc(GE, i)
                            prev_twA[pr] = tg24
                            if cyc < 2:
                                for qq, dcq, tg in ((q13, pr, tg13), (q24, pr + 4, tg24)):
                                    ssl = slice(dcq * TILE, (dcq + 1) * TILE)
                                    p.waits(SE, (tg, t_cbc))
                                    p.eng[SE].drain()
                                    i = p.eng[SE].copy(spl_hi[:, ssl], qq)
                                    th = p.inc(SE, i)
                                    p.eng[SE].drain()
                                    i = p.eng[SE].copy(spl_hib[:, ssl],
                                                       spl_hi[:, ssl].bitcast(F32))
                                    thb = p.inc(SE, i)
                                    p.waits(VE, (th,))
                                    p.eng[VE].drain()
                                    i = p.eng[VE].tensor_tensor(
                                        spl_lob[:, ssl], qq,
                                        spl_hi[:, ssl].bitcast(F32), AL.subtract)
                                    tlo = p.inc(VE, i)
                                    spl_new[dcq] = (thb, tlo)
                        tw_tok[ti] = prev_twA[3]
                        if cyc < 2:
                            spl_tok = spl_new
                            dcord = [0, 4, 1, 5, 2, 6, 3, 7]

                  b3gate = tuple(tw_tok.values())
                  t_b3ge = {}
                  for cl in range(NCHH):
                    ci = ci_base + cl
                    t0 = ci * CH
                    mbuf = junkD if cl % 2 == 0 else gcw_t
                    tx1 = p.dma(SY, x1b, X1S[b, t0:t0 + CH, :],
                                waits=b3gate if cl == 0 else (p.tok(GE),))
                    for dc in range(NDC):
                        bki = dc % 2
                        off = ((dc // 2) % 2) * 128
                        col = (dc * NCHH + cl) * CH
                        bk.pre_write(bki, b3gate if (cl == 0 and dc == 0) else ())
                        ins = nc.tensor.transpose(bank_ps[bki][:, off:off + 128],
                                                  h2res[:, col:col + CH], eye[:])
                        bk.set_write(bki, p.inc(TE, ins))
                        bk.read(bki, VE)
                        if dc == 0:
                            p.wait(VE, t_b3ge.get(cl - 2))
                        p.eng[VE].drain()
                        i = p.eng[VE].tensor_tensor(mbuf[:, dc * 128:(dc + 1) * 128],
                                                    bank_ps[bki][:, off:off + 128],
                                                    gb_bc[:, dc * 128:(dc + 1) * 128], AL.mult)
                        t_m = p.inc(VE, i)
                        bk.add_reader(bki, t_m)
                        p.waits(GE, (t_m, tx1) + ((t_outst,) if (t_outst and dc == 0) else ()))
                        p.eng[GE].drain()
                        i = p.eng[GE].tensor_tensor(outb[:, dc * 128:(dc + 1) * 128],
                                                    mbuf[:, dc * 128:(dc + 1) * 128],
                                                    x1b[:, dc * 128:(dc + 1) * 128], AL.add)
                        t_b3ge[cl] = p.inc(GE, i)
                    t_outst = p.dma(GE, OUT[b, t0:t0 + CH, :], outb, waits=(p.tok(GE),),
                                    slot=4 + cl % 2)
                  t_bar = (p.tok(TE), p.tok(GE), p.tok(VE), t_outst)

            for b in range(n_seq):
                pb_seq(b)
            p.barrier()

        for d in ("dma4", "dma5"):
            if p.cnt[d]:
                p.eng[GE].wait_ge(p.sem[d], p.cnt[d])
        if p.cnt[GE]:
            p.eng[GE].wait_ge(p.sem[GE], p.cnt[GE])

    return nc, tap_handles


def _consts():
    return {
        "eye": np.eye(128, dtype=np.float32),
        "masku": np.triu(np.ones((128, 128), np.float32), 1),
        "maskl": np.tril(np.ones((128, 128), np.float32), -1),
        "maskui": np.triu(np.ones((128, 128), np.float32), 0),
        "ones1": np.ones((1, 128), np.float32),
        "sel4": np.concatenate([np.tile((np.arange(4) == h).astype(np.float32)[:, None],
                                        (1, 128)) for h in range(4)], axis=1),
    }


def make_in_map(inputs, c, n_seq):
    m = {
        "x": np.ascontiguousarray(inputs["L0_attn_x"][c * n_seq:(c + 1) * n_seq]),
        "c": np.ascontiguousarray(inputs["L0_c"][c * n_seq:(c + 1) * n_seq]),
        "wmod": inputs["L0_wmod"],
        "wq": inputs["L0_attn_Wq"], "wk": inputs["L0_attn_Wk"],
        "wv": inputs["L0_attn_Wv"], "wb": inputs["L0_attn_Wbeta"],
        "wo": inputs["L0_attn_Wo"], "g_norm": inputs["L0_attn_g_norm"],
        "cn_w1": inputs["L0_ffn_cn_w1"], "cn_b1": inputs["L0_ffn_cn_b1"],
        "cn_w2": inputs["L0_ffn_cn_w2"], "cn_b2": inputs["L0_ffn_cn_b2"],
        "cn_w3": inputs["L0_ffn_cn_w3"], "cn_b3": inputs["L0_ffn_cn_b3"],
        "cn_w4": inputs["L0_ffn_cn_w4"], "cn_b4": inputs["L0_ffn_cn_b4"],
        "gc_w1": inputs["L0_ffn_gc_w1"], "gc_b1": inputs["L0_ffn_gc_b1"],
        "gc_w2": inputs["L0_ffn_gc_w2"], "gc_b2": inputs["L0_ffn_gc_b2"],
        "gc_w3": inputs["L0_ffn_gc_w3"], "gc_b3": inputs["L0_ffn_gc_b3"],
        "ar": inputs["L0_ffn_adaptive_range"],
    }
    m.update(_consts())
    return m


def kernel(**inputs):
    n_cores = 8
    n_seq = 32 // n_cores
    nc, _ = build_program(n_seq, 2048)
    in_maps = [make_in_map(inputs, c, n_seq) for c in range(n_cores)]
    res = run_bass_kernel_spmd(nc, in_maps, core_ids=list(range(n_cores)))
    return np.concatenate([r["out"] for r in res.results], axis=0)
